# version 18
# baseline (speedup 1.0000x reference)
import contextlib
import numpy as np

import concourse.bass as bass
import concourse.mybir as mybir
from concourse.bass_utils import run_bass_kernel_spmd

F32 = mybir.dt.float32
BF16 = mybir.dt.bfloat16
U8 = mybir.dt.uint8
AF = mybir.ActivationFunctionType
ALU = mybir.AluOpType
AX = mybir.AxisListType

D = 1024
ND = 8
HID = 2048
NJC = 16
NH = 16
NPAIR = 8
NE = 8
NB = 2
EPS_LN = 1e-5
EPS_ATTN = 1e-6
BIG = 1e9
LNKV_LA = 2

ENGINES = ("t", "v", "s", "g", "y")
SYNC_SELF_OK = ("t", "v", "s")
DMA_POOL = {"g": 8, "y": 4}


class Prog:
    def __init__(self):
        self.items = {e: [] for e in ENGINES}
        self.count = {}
        self.pending = {e: {} for e in ENGINES}
        self.max_waited = {e: {} for e in ENGINES}
        self.dma_slot = {q: 0 for q in DMA_POOL}
        self.sem_names = set(ENGINES) | {"gc"}
        for q, n in DMA_POOL.items():
            for i in range(n):
                self.sem_names.add(f"{q}{i}")
        for s in self.sem_names:
            self.count[s] = 0

    def wait(self, eng, tok):
        if tok is None:
            return
        if isinstance(tok, list):
            for t in tok:
                self.wait(eng, t)
            return
        sem, val = tok
        if sem == eng and eng in SYNC_SELF_OK:
            return
        if val <= self.max_waited[eng].get(sem, 0):
            return
        if val <= self.pending[eng].get(sem, 0):
            return
        self.pending[eng][sem] = val

    def wait_force(self, eng, tok):
        if tok is None:
            return
        if isinstance(tok, list):
            for t in tok:
                self.wait_force(eng, t)
            return
        sem, val = tok
        if val <= self.max_waited[eng].get(sem, 0):
            return
        if val <= self.pending[eng].get(sem, 0):
            return
        self.pending[eng][sem] = val

    AUTO_CHAIN = ("v", "s")

    def emit(self, eng, fn):
        if eng in self.AUTO_CHAIN and self.items[eng]:
            if not self.items[eng][-1][2]:
                self.sig(eng)
            self.wait_force(eng, (eng, self.count[eng]))
        w = self.pending[eng]
        self.pending[eng] = {}
        for s, v in w.items():
            self.max_waited[eng][s] = max(self.max_waited[eng].get(s, 0), v)
        self.items[eng].append([w, fn, []])

    def sig(self, eng, amt=1, sem=None):
        sem = sem or eng
        assert self.items[eng], f"sig on empty engine {eng}"
        self.items[eng][-1][2].append((sem, amt))
        self.count[sem] += amt
        return (sem, self.count[sem])

    def dma(self, q, fn):
        slot = self.dma_slot[q]
        self.dma_slot[q] = (slot + 1) % DMA_POOL[q]
        sem = f"{q}{slot}"
        self.wait_force(q, (sem, self.count[sem]))
        self.emit(q, fn)
        return self.sig(q, 16, sem=sem)

    def queue_done(self, q):
        return [(f"{q}{i}", self.count[f"{q}{i}"]) for i in range(DMA_POOL[q])]

    def tok(self, eng):
        return (eng, self.count[eng])

    def sig_or_reuse(self, eng):
        if not self.items[eng]:
            return None
        incs = self.items[eng][-1][2]
        if incs:
            s, _ = incs[-1]
            return (s, self.count[s])
        return self.sig(eng)

    def barrier(self, engines=("t", "v", "s")):
        toks = []
        for e in engines:
            if e in DMA_POOL:
                toks.extend(self.queue_done(e))
            else:
                t = self.sig_or_reuse(e)
                if t is not None:
                    toks.append(t)
        for e in engines:
            for t in toks:
                self.wait(e, t)

    def check_deadlock(self):
        ptr = {e: 0 for e in ENGINES}
        val = {s: 0 for s in self.sem_names}
        progress = True
        while progress:
            progress = False
            for e in ENGINES:
                while ptr[e] < len(self.items[e]):
                    waits, _, incs = self.items[e][ptr[e]]
                    if all(val[s] >= v for s, v in waits.items()):
                        for s, amt in incs:
                            val[s] += amt
                        ptr[e] += 1
                        progress = True
                    else:
                        break
        stuck = {}
        for e in ENGINES:
            if ptr[e] < len(self.items[e]):
                waits = self.items[e][ptr[e]][0]
                stuck[e] = {"at": ptr[e], "of": len(self.items[e]), "waits": waits,
                            "have": {s: val[s] for s in waits}}
        if stuck:
            raise RuntimeError(f"deadlock in program build: {stuck}")


def build(n_tc=16, taps=()):
    T = n_tc * 128
    n_tiles = (n_tc + 3) // 4

    nc = bass.Bass("TRN2", target_bir_lowering=False)

    def din(name, shape):
        return nc.dram_tensor(name, shape, F32, kind="ExternalInput").ap()

    x_h = din("x", [NB, T, D])
    out_h = nc.dram_tensor("out", [NB, T, D], F32, kind="ExternalOutput").ap()
    wq_h = din("wq", [128, ND * 1024])
    wk_h = din("wk", [128, ND * 1024])
    wv_h = din("wv", [128, ND * 1024])
    wo_h = din("wo", [128, ND * 1024])
    winR_h = din("winR", [NJC, 128, 2048])
    w1R_h = din("w1R", [NJC, 128, 2048])
    woutR_h = din("woutR", [8, 128, 2048])
    w2R_h = din("w2R", [NE, 2, 128, NJC * 512])
    gatesT_h = din("gatesT", [D, 10])
    bk_h = din("bk", [1, D])
    bv_h = din("bv", [1, D])
    bo_h = din("bo", [1, D])
    bq_h = din("bq", [1, D])
    geb_h = din("geb", [1, 10])

    ident_h = din("ident", [128, 128])
    e16_h = din("e16", [NPAIR, 128, 128])
    maskb_h = din("maskb", [128, 2])

    tap_out = {}
    for name, shape in taps:
        tap_out[name] = nc.dram_tensor("tap_" + name, list(shape), F32,
                                       kind="ExternalOutput").ap()

    es = contextlib.ExitStack()
    with es:
        acc = es.enter_context(nc.sbuf_tensor("acc", [128, n_tc * 1024], F32))
        harena = es.enter_context(nc.sbuf_tensor("harena", [128, max(65536, NJC * T * 2)], U8))
        xarena = es.enter_context(nc.sbuf_tensor("xarena", [128, max(n_tc * 2048, 32768)], U8))
        wbuf = es.enter_context(nc.sbuf_tensor("wbuf", [128, 8192], U8))
        woutbuf = es.enter_context(nc.sbuf_tensor("woutbuf", [128, 8192], U8))
        sc_a = es.enter_context(nc.sbuf_tensor("sc_a", [128, 512], F32))
        sc_b = es.enter_context(nc.sbuf_tensor("sc_b", [128, 512], F32))
        sc_c = es.enter_context(nc.sbuf_tensor("sc_c", [128, 512], F32))

        kv2 = es.enter_context(nc.sbuf_tensor("kv2", [128, NPAIR * 128], BF16))
        k16 = es.enter_context(nc.sbuf_tensor("k16", [128, ND * 128], BF16))
        rn_sb = es.enter_context(nc.sbuf_tensor("rn_sb", [128, 512], BF16))
        rb_sb = es.enter_context(nc.sbuf_tensor("rb_sb", [128, 1024], BF16))
        wse = es.enter_context(nc.sbuf_tensor("wse", [128, n_tc * NE], F32))
        gsc = es.enter_context(nc.sbuf_tensor("gsc", [128, 128], F32))
        glog = es.enter_context(nc.sbuf_tensor("glog", [128, n_tc * 10], F32))
        stat = es.enter_context(nc.sbuf_tensor("stat", [128, 32], F32))
        ident32 = es.enter_context(nc.sbuf_tensor("ident32_s", [128, 128], F32))
        e16b = es.enter_context(nc.sbuf_tensor("e16b", [128, NPAIR * 128], BF16))
        bbk = es.enter_context(nc.sbuf_tensor("bbk_s", [128, D], BF16))
        bbv = es.enter_context(nc.sbuf_tensor("bbv_s", [128, D], BF16))
        bbo = es.enter_context(nc.sbuf_tensor("bbo_s", [128, D], BF16))
        bqf = es.enter_context(nc.sbuf_tensor("bqf_s", [128, D], BF16))
        bgeb = es.enter_context(nc.sbuf_tensor("bgeb_s", [128, 10], BF16))
        oscl = es.enter_context(nc.sbuf_tensor("oscl_s", [128, 128], BF16))
        onesr = es.enter_context(nc.sbuf_tensor("onesr_s", [128, 512], BF16))
        maskb = es.enter_context(nc.sbuf_tensor("maskb_s", [128, 2], F32))
        onesb = es.enter_context(nc.sbuf_tensor("onesb", [128, 1], BF16))
        czero = es.enter_context(nc.sbuf_tensor("czero_s", [128, 4], F32))
        gatesb = es.enter_context(nc.sbuf_tensor("gatesb", [128, ND * 10], BF16))

        ha_b = harena.bitcast(BF16)
        ha_f = harena.bitcast(F32)
        xa_b = xarena.bitcast(BF16)
        wb_b = wbuf.bitcast(BF16)
        wo8_b = woutbuf.bitcast(BF16)

        def bv(base, byte_off, cols):
            return base[:, byte_off // 2: byte_off // 2 + cols]

        def fv(base, byte_off, cols):
            return base[:, byte_off // 4: byte_off // 4 + cols]

        R0, R1, R2, R3 = 0, 16384, 32768, 49152
        wk_sb = bv(ha_b, R0, ND * 1024)
        wv_sb = bv(ha_b, R1, ND * 1024)
        wq_sb = bv(ha_b, R3, ND * 1024)
        wo_sb = bv(ha_b, R0, ND * 1024)
        q_fm2 = [bv(ha_b, R1, ND * 512), bv(ha_b, R1 + 8192, ND * 512)]
        apre2 = [bv(ha_b, R2, ND * 512), bv(ha_b, R2 + 8192, ND * 512)]
        q_fm = q_fm2[0]
        apre = apre2[0]
        ktm = [bv(ha_b, R2, 1024), bv(ha_b, R2 + 2048, 1024)]
        vtm = [bv(ha_b, R2 + 4096, 1024), bv(ha_b, R2 + 6144, 1024)]
        xnst = [fv(ha_f, R2 + 8192, 1024), fv(ha_f, R2 + 12288, 1024)]
        swg = ha_b[:, 0:NJC * T]
        h_sb = ha_b[:, 0:NJC * T]
        xn = xa_b[:, 0:n_tc * 1024]
        w2b = [bv(xa_b, 0, 16 * 512), bv(xa_b, 16384, 16 * 512)]
        wob = [bv(wo8_b, 0, 16 * 128), bv(wo8_b, 4096, 16 * 128)]

        PS = [es.enter_context(nc.psum_tensor(f"pb{i}", [128, 512], F32)) for i in range(8)]

        p = Prog()
        sems = {s: es.enter_context(nc.semaphore(f"sem_{s}")) for s in sorted(p.sem_names)}
        blk = es.enter_context(nc.Block())
        bank_free = [None] * 8
        rel = {"R0": None, "R1": None, "R2": None, "R3": None, "acc": None,
               "wbuf": [None, None], "woutbuf": [None, None], "xarena": None}
        late = {}
        glog_tok = [None] * n_tc
        rel["sc_tt"] = {0: None, 1: None}

        def ld(fn):
            return p.dma("g", fn)

        tok_ident = ld(lambda g: g.dma_start(ident32[:, :], ident_h[:, :]))
        p.emit("g", lambda g: g.memset(czero[:, 0:1], 0.0))
        p.emit("g", lambda g: g.memset(czero[:, 1:2], EPS_LN))
        p.emit("g", lambda g: g.memset(czero[:, 2:3], 1.0))
        p.emit("g", lambda g: g.memset(oscl[:, :], 0.0))
        p.emit("g", lambda g: g.memset(oscl[0:1, :], 1.0))
        p.emit("g", lambda g: g.memset(onesb[:, :], 1.0))
        p.emit("g", lambda g: g.memset(bbk[:, :], 0.0))
        p.emit("g", lambda g: g.memset(bbv[:, :], 0.0))
        p.emit("g", lambda g: g.memset(rn_sb[:, :], 0.0))
        init_tok = p.sig("g", sem="gc")
        tok_czero = tok_oscl = tok_onesb = init_tok
        tok_bbk = ld(lambda g: g.dma_start(bbk[0:1, :], bk_h[:, :]))
        tok_bbv = ld(lambda g: g.dma_start(bbv[0:1, :], bv_h[:, :]))
        nc.const_aps.register(F32, 0.0, czero[:, 0:1])
        nc.const_aps.register(F32, EPS_LN, czero[:, 1:2])
        nc.const_aps.register(F32, 1.0, czero[:, 2:3])
        tok_maskb = p.dma("y", lambda y: y.dma_start(maskb[:, :], maskb_h[:, :]))

        def emit_late_consts():
            lc = {}
            p.emit("g", lambda g: g.memset(onesr[:, :], 0.0))
            p.emit("g", lambda g: g.memset(onesr[0:1, :], 1.0))
            p.emit("g", lambda g: g.memset(bqf[:, :], 0.0))
            p.emit("g", lambda g: g.memset(bbo[:, :], 0.0))
            p.emit("g", lambda g: g.memset(bgeb[:, :], 0.0))
            lc["z"] = p.sig("g", sem="gc")
            lc["e16"] = ld(lambda g: g.dma_start(
                e16b[:, :].rearrange("p (q m) -> p q m", q=NPAIR),
                e16_h.rearrange("q k m -> k q m")))
            lc["bqf"] = [ld(lambda g: g.dma_start(bqf[0:1, :], bq_h[:, :])), lc["z"]]
            lc["bbo"] = [ld(lambda g: g.dma_start(bbo[0:1, :], bo_h[:, :])), lc["z"]]
            lc["onesr"] = lc["z"]
            lc["bgeb"] = [ld(lambda g: g.dma_start(bgeb[0:1, :], geb_h[:, :])), lc["z"]]
            lc["gatesb"] = ld(lambda g: g.dma_start(
                gatesb[:, :].rearrange("p (c n) -> p c n", c=ND),
                gatesT_h.rearrange("(c k) n -> k c n", k=128)))
            return lc

        def load_w_chunked(dst_view, src_ap, rel_tok):
            p.wait("g", rel_tok)
            return p.dma("g", lambda g: g.dma_start(dst_view, src_ap))

        def stage_lnkv(b, acc_ready, wk_tok, wv_tok):
            ln_done = [None] * n_tc
            tp_tok = [None] * n_tc
            xn_ready = [None] * n_tc
            kbanks = (0, 7)
            vbanks = (1,)
            exp_war = {0: None, 1: None}
            kv_started = [False, False, False]
            kvmm_tok = [None] * n_tc
            ktoks = {}
            vtoks = {}

            def emit_kv(tc):
                sl = tc % 2
                for half in range(2):
                    kvbank = 2 + half
                    p.wait("t", ktoks[tc * 2 + half])
                    p.wait("t", vtoks[tc * 2 + half])
                    if tc == 0:
                        p.wait("t", bank_free[kvbank])
                        p.wait("t", bank_free[4])
                        p.wait("t", tok_onesb)
                    for qq in range(4):
                        st = not kv_started[half]
                        kv_started[half] = True
                        p.emit("t", lambda t, sl=sl, half=half, qq=qq, st=st, kvbank=kvbank: t.matmul(
                            PS[kvbank][:, qq * 128:(qq + 1) * 128],
                            ktm[sl][:, half * 512 + qq * 128: half * 512 + (qq + 1) * 128],
                            vtm[sl][:, half * 512 + qq * 128: half * 512 + (qq + 1) * 128],
                            start=st, stop=False, skip_group_check=True))
                    for blkc in range(4):
                        st = not kv_started[2]
                        kv_started[2] = True
                        col = half * 4 + blkc
                        p.emit("t", lambda t, sl=sl, half=half, blkc=blkc, st=st, col=col: t.matmul(
                            PS[4][:, col:col + 1],
                            ktm[sl][:, half * 512 + blkc * 128: half * 512 + (blkc + 1) * 128],
                            onesb[:, :],
                            start=st, stop=False, skip_group_check=True))
                kvmm_tok[tc] = p.sig("t")

            p1_rstd = [None] * n_tc

            def emit_p1_stats(tc):
                if b > 0 and tc < 4:
                    p.wait("y", rel["acc"])
                acc_ready[tc] = p.dma("y", lambda y, b=b, tc=tc: y.dma_start(
                    acc[:, tc * 1024:(tc + 1) * 1024], x_h[b, tc * 128:(tc + 1) * 128, :]))
                so = (tc % 2) * 16
                xch = acc[:, tc * 1024:(tc + 1) * 1024]
                p.wait("v", acc_ready[tc])
                if tc >= 2:
                    p.wait("v", ln_done[tc - 2])
                p.emit("v", lambda v, xch=xch, so=so: v.bn_stats(
                    stat[:, so:so + 6], xch[:, 0:512]))
                p.emit("v", lambda v, xch=xch, so=so: v.bn_stats(
                    stat[:, so + 6:so + 12], xch[:, 512:1024]))
                p.emit("v", lambda v, so=so: v.bn_aggr(
                    stat[:, so + 12:so + 14], stat[:, so:so + 12]))
                tv = p.sig("v")
                p.wait("s", tv)
                p.wait("s", tok_czero)
                if tc >= 2:
                    p.wait("s", ln_done[tc - 2])
                p.emit("s", lambda s, so=so: s.activation(
                    stat[:, so + 14:so + 15], stat[:, so + 13:so + 14], AF.Ln,
                    bias=czero[:, 1:2], scale=1.0))
                p.emit("s", lambda s, so=so: s.activation(
                    stat[:, so + 15:so + 16], stat[:, so + 14:so + 15], AF.Exp,
                    scale=-0.5))
                p1_rstd[tc] = p.sig("s")

            def emit_p1_norm(tc):
                so = (tc % 2) * 16
                xch = acc[:, tc * 1024:(tc + 1) * 1024]
                p.wait("v", p1_rstd[tc])
                if tc >= 2:
                    p.wait("v", tp_tok[tc - 2])
                sl = tc % 2
                p.emit("v", lambda v, sl=sl, so=so, xch=xch: v.tensor_scalar(
                    out=xnst[sl][:, :], in0=xch,
                    scalar1=stat[:, so + 12:so + 13], scalar2=stat[:, so + 15:so + 16],
                    op0=ALU.subtract, op1=ALU.mult))
                ln_done[tc] = p.sig("v")

            def emit_p1_tp(tc):
                sl = tc % 2
                p.wait("t", ln_done[tc])
                p.wait("t", bank_free[5])
                p.wait("t", bank_free[6])
                p.wait("t", tok_ident)
                for c in range(ND):
                    p.emit("t", lambda t, sl=sl, c=c: t.transpose(
                        PS[5 + c // 4][:, (c % 4) * 128:(c % 4 + 1) * 128],
                        xnst[sl][:, c * 128:(c + 1) * 128], ident32[:, :]))
                tp_tok[tc] = p.sig("t")
                p.wait("s", tp_tok[tc])
                p.emit("s", lambda s, tc=tc: s.copy(
                    out=xn[:, :].rearrange("p (c t) -> p c t", c=ND)[:, 0:4, tc * 128:(tc + 1) * 128],
                    in_=PS[5][:, 0:512].rearrange("p (c t) -> p c t", c=4)))
                bank_free[5] = p.sig("s")
                p.wait("v", tp_tok[tc])
                p.emit("v", lambda v, tc=tc: v.tensor_copy(
                    out=xn[:, :].rearrange("p (c t) -> p c t", c=ND)[:, 4:8, tc * 128:(tc + 1) * 128],
                    in_=PS[6][:, 0:512].rearrange("p (c t) -> p c t", c=4)))
                bank_free[6] = p.sig("v")
                xn_ready[tc] = [bank_free[5], bank_free[6]]

            def emit_p2(tc):
                sl = tc % 2
                for half in range(2):
                    idx = tc * 2 + half
                    kb = kbanks[idx % 2]
                    vb = vbanks[0]
                    p.wait("t", wk_tok)
                    p.wait("t", wv_tok)
                    p.wait("t", [tok_oscl, tok_bbk, tok_bbv])
                    p.wait("t", xn_ready[tc])
                    p.wait("t", bank_free[kb])
                    for c in range(ND):
                        p.emit("t", lambda t, c=c, tc=tc, half=half, kb=kb: t.matmul(
                            PS[kb][:, :],
                            xn[:, c * T + tc * 128: c * T + (tc + 1) * 128],
                            wk_sb[:, c * 1024 + half * 512: c * 1024 + half * 512 + 512],
                            start=(c == 0), stop=False))
                    p.emit("t", lambda t, half=half, kb=kb: t.matmul(
                        PS[kb][:, :], oscl[:, :],
                        bbk[:, half * 512: half * 512 + 512], start=False, stop=True))
                    kmm = p.sig("t")
                    p.wait("t", bank_free[vb])
                    for c in range(ND):
                        p.emit("t", lambda t, c=c, tc=tc, half=half, vb=vb: t.matmul(
                            PS[vb][:, :],
                            xn[:, c * T + tc * 128: c * T + (tc + 1) * 128],
                            wv_sb[:, c * 1024 + half * 512: c * 1024 + half * 512 + 512],
                            start=(c == 0), stop=False))
                    p.emit("t", lambda t, half=half, vb=vb: t.matmul(
                        PS[vb][:, :], oscl[:, :],
                        bbv[:, half * 512: half * 512 + 512], start=False, stop=True))
                    vmm = p.sig("t")

                    eb = sc_a if (half == 0) else sc_c
                    p.wait("s", kmm)
                    p.wait("s", exp_war[half])
                    p.emit("s", lambda s, eb=eb, kb=kb: s.activation(
                        eb[:, :], PS[kb][:, :], AF.Exp))
                    te = p.sig("s")
                    p.wait("s", vmm)
                    if tc >= 2:
                        p.wait("s", kvmm_tok[tc - 2])
                    p.emit("s", lambda s, sl=sl, half=half, vb=vb: s.copy(
                        out=vtm[sl][:, half * 512: half * 512 + 512],
                        in_=PS[vb][:, :]))
                    vtoks[idx] = p.sig("s")
                    bank_free[vb] = vtoks[idx]
                    p.wait("v", te)
                    if tc >= 2:
                        p.wait("v", kvmm_tok[tc - 2])
                    p.emit("v", lambda v, eb=eb: v.tensor_scalar(
                        out=eb[:, :], in0=eb[:, :], scalar1=1.0, scalar2=None,
                        op0=ALU.min))
                    p.wait_force("v", p.sig("v"))
                    p.emit("v", lambda v, sl=sl, half=half, eb=eb, kb=kb: v.scalar_tensor_tensor(
                        out=ktm[sl][:, half * 512: half * 512 + 512],
                        in0=PS[kb][:, :], scalar=0.0, in1=eb[:, :],
                        op0=ALU.max, op1=ALU.add))
                    ktoks[idx] = p.sig("v")
                    exp_war[half] = ktoks[idx]
                    bank_free[kb] = ktoks[idx]

            LA = LNKV_LA if LNKV_LA > 0 else n_tc
            emit_p1_stats(0)
            for i in range(n_tc + LA + 1):
                if i + 1 < n_tc:
                    emit_p1_stats(i + 1)
                if i < n_tc:
                    emit_p1_norm(i)
                if LA <= i < n_tc + LA:
                    emit_p2(i - LA)
                if i >= LA + 1:
                    emit_kv(i - LA - 1)
                if i < n_tc:
                    emit_p1_tp(i)

            last_t = p.tok("t")
            p.emit("g", lambda g: g.memset(kv2[:, :], 0.0))
            p.emit("g", lambda g: g.memset(k16[:, :], 0.0))
            gz = p.sig("g", sem="gc")
            p.wait("v", last_t)
            p.wait("v", gz)
            p.wait("v", tok_maskb)
            for pair in range(NPAIR):
                bank = 2 + pair // 4
                qq = pair % 4
                p.emit("v", lambda v, pair=pair, bank=bank, qq=qq: v.tensor_copy(
                    out=kv2[0:64, pair * 128: pair * 128 + 64],
                    in_=PS[bank][0:64, qq * 128: qq * 128 + 64]))
                p.emit("v", lambda v, pair=pair, bank=bank, qq=qq: v.tensor_copy(
                    out=kv2[64:128, pair * 128 + 64: pair * 128 + 128],
                    in_=PS[bank][64:128, qq * 128 + 64: qq * 128 + 128]))
            for c in range(ND):
                p.emit("v", lambda v, c=c: v.tensor_scalar(
                    out=k16[:, c * 128 + 2 * c: c * 128 + 2 * c + 1],
                    in0=PS[4][:, c:c + 1], scalar1=maskb[:, 0:1], scalar2=None,
                    op0=ALU.mult))
                p.emit("v", lambda v, c=c: v.tensor_scalar(
                    out=k16[:, c * 128 + 2 * c + 1: c * 128 + 2 * c + 2],
                    in0=PS[4][:, c:c + 1], scalar1=maskb[:, 1:2], scalar2=None,
                    op0=ALU.mult))
            ev = p.sig("v")
            for i in (2, 3, 4):
                bank_free[i] = ev
            return ev

        def stage_qattn(b, wq_tok, wo_tok, kv_tok, acc_ready):
            exp_war = {0: None, 1: None}
            qelu = {}
            num_done = [None] * n_tiles
            wo_done = [None] * n_tiles
            rb_done = [None] * n_tiles
            rntoks = [None] * n_tiles
            apre_sl_war = {0: None, 1: None}

            def emit_qproj(ti, dcs):
                qf = q_fm2[ti % 2]
                for dc in dcs:
                    qb = dc % 2
                    p.wait("t", wq_tok)
                    p.wait("t", [late["bqf"], late["onesr"]])
                    p.wait("t", bank_free[qb])
                    for c in range(ND):
                        p.emit("t", lambda t, c=c, dc=dc, ti=ti, qb=qb: t.matmul(
                            PS[qb][:, :],
                            wq_sb[:, c * 1024 + dc * 128: c * 1024 + (dc + 1) * 128],
                            xn[:, c * T + ti * 512: c * T + (ti + 1) * 512],
                            start=(c == 0), stop=False))
                    p.emit("t", lambda t, dc=dc, qb=qb: t.matmul(
                        PS[qb][:, :], bqf[:, dc * 128:(dc + 1) * 128],
                        onesr[:, :], start=False, stop=True))
                    qmm = p.sig("t")
                    eb = sc_a if (dc % 2 == 0) else sc_c
                    p.wait("s", qmm)
                    p.wait("s", exp_war[dc % 2])
                    p.emit("s", lambda s, qb=qb, eb=eb: s.activation(
                        eb[:, :], PS[qb][:, :], AF.Exp))
                    te = p.sig("s")
                    p.wait("v", te)
                    if ti >= 2:
                        p.wait("v", num_done[ti - 2])
                    p.emit("v", lambda v, eb=eb: v.tensor_scalar(
                        out=eb[:, :], in0=eb[:, :], scalar1=1.0, scalar2=None,
                        op0=ALU.min))
                    p.wait_force("v", p.sig("v"))
                    p.emit("v", lambda v, dc=dc, qb=qb, eb=eb, qf=qf: v.scalar_tensor_tensor(
                        out=qf[:, dc * 512:(dc + 1) * 512],
                        in0=PS[qb][:, :], scalar=0.0, in1=eb[:, :],
                        op0=ALU.max, op1=ALU.add))
                    qelu[(ti, dc)] = p.sig("v")
                    exp_war[dc % 2] = qelu[(ti, dc)]
                    bank_free[qb] = qelu[(ti, dc)]

            def emit_nmm(ti):
                qf = q_fm2[ti % 2]
                p.wait("t", kv_tok)
                p.wait("t", bank_free[4])
                for pair in range(NPAIR):
                    p.wait("t", qelu[(ti, pair)])
                    p.emit("t", lambda t, pair=pair, qf=qf: t.matmul(
                        PS[4][:, :], k16[:, pair * 128:(pair + 1) * 128],
                        qf[:, pair * 512:(pair + 1) * 512],
                        start=(pair == 0), stop=(pair == NPAIR - 1)))
                nmm = p.sig("t")
                p.wait("v", nmm)
                p.wait("v", init_tok)
                if ti >= 1:
                    p.wait("v", rb_done[ti - 1])
                p.emit("v", lambda v: v.tensor_scalar(
                    out=sc_b[0:16, :], in0=PS[4][0:16, :], scalar1=EPS_ATTN,
                    scalar2=None, op0=ALU.max))
                p.wait_force("v", p.sig("v"))
                p.emit("v", lambda v: v.reciprocal(out=sc_b[0:16, :], in_=sc_b[0:16, :]))
                p.wait_force("v", p.sig("v"))
                p.emit("v", lambda v: v.tensor_copy(out=rn_sb[0:16, :], in_=sc_b[0:16, :]))
                rntoks[ti] = p.sig("v")
                bank_free[4] = rntoks[ti]

            apre_toks = {}

            def emit_tail_a(ti):
                qf = q_fm2[ti % 2]
                ap = apre2[ti % 2]
                apre_tok = apre_toks[ti] = [None] * NPAIR
                for pair in range(NPAIR):
                    nb_ = (2, 3, 4)[pair % 3]
                    rbb = 5 + pair % 2
                    p.wait("t", bank_free[nb_])
                    p.emit("t", lambda t, pair=pair, nb_=nb_, qf=qf: t.matmul(
                        PS[nb_][:, :], kv2[:, pair * 128:(pair + 1) * 128],
                        qf[:, pair * 512:(pair + 1) * 512], start=True, stop=True))
                    num_t = p.sig("t")
                    if pair == NPAIR - 1:
                        num_done[ti] = num_t
                    p.wait("t", rntoks[ti])
                    p.wait("t", bank_free[rbb])
                    p.emit("t", lambda t, pair=pair, rbb=rbb: t.matmul(
                        PS[rbb][:, :], e16b[:, pair * 128:(pair + 1) * 128],
                        rn_sb[:, :], start=True, stop=True))
                    rb_t = p.sig("t")
                    rb_done[ti] = rb_t
                    sl = pair % 2
                    p.wait("s", rb_t)
                    p.wait("s", apre_sl_war[sl])
                    p.emit("s", lambda s, sl=sl, rbb=rbb: s.copy(
                        out=rb_sb[:, sl * 512:(sl + 1) * 512], in_=PS[rbb][:, :]))
                    bank_free[rbb] = p.sig("s")
                    p.wait("v", bank_free[rbb])
                    p.wait("v", num_t)
                    if ti >= 2:
                        p.wait("v", wo_done[ti - 2])
                    p.emit("v", lambda v, pair=pair, sl=sl, nb_=nb_, ap=ap: v.tensor_tensor(
                        out=ap[:, pair * 512:(pair + 1) * 512], in0=PS[nb_][:, :],
                        in1=rb_sb[:, sl * 512:(sl + 1) * 512], op=ALU.mult))
                    apre_tok[pair] = p.sig("v")
                    apre_sl_war[sl] = apre_tok[pair]
                    bank_free[nb_] = apre_tok[pair]

            def emit_tail_b(ti):
                ap = apre2[ti % 2]
                apre_tok = apre_toks[ti]
                for sl4 in range(4):
                    for dh in range(2):
                        wb_ = 7 if (sl4 * 2 + dh) % 2 == 0 else 4
                        p.wait("t", wo_tok)
                        p.wait("t", bank_free[wb_])
                        for c in range(ND):
                            p.wait("t", apre_tok[c])
                            p.emit("t", lambda t, c=c, sl4=sl4, dh=dh, wb_=wb_, ap=ap: t.matmul(
                                PS[wb_][:, :],
                                ap[:, c * 512 + sl4 * 128: c * 512 + (sl4 + 1) * 128],
                                wo_sb[:, c * 1024 + dh * 512: c * 1024 + dh * 512 + 512],
                                start=(c == 0), stop=False))
                        p.emit("t", lambda t, dh=dh, wb_=wb_: t.matmul(
                            PS[wb_][:, :], oscl[:, :],
                            bbo[:, dh * 512: dh * 512 + 512], start=False, stop=True))
                        womm = p.sig("t")
                        tcg = ti * 4 + sl4
                        p.wait("v", womm)
                        p.wait("v", acc_ready[tcg])
                        p.emit("v", lambda v, tcg=tcg, dh=dh, wb_=wb_: v.tensor_tensor(
                            out=acc[:, tcg * 1024 + dh * 512: tcg * 1024 + dh * 512 + 512],
                            in0=PS[wb_][:, :],
                            in1=acc[:, tcg * 1024 + dh * 512: tcg * 1024 + dh * 512 + 512],
                            op=ALU.add))
                        bank_free[wb_] = p.sig("v")
                wo_done[ti] = p.tok("t")
                for k4 in range(4):
                    gates_mm(ti * 4 + k4)

            emit_qproj(0, range(ND))
            for ti in range(n_tiles):
                emit_nmm(ti)
                if ti + 1 < n_tiles:
                    emit_qproj(ti + 1, range(0, 5))
                emit_tail_a(ti)
                if ti + 1 < n_tiles:
                    emit_qproj(ti + 1, range(5, 8))
                emit_tail_b(ti)
            return p.tok("v")

        def gates_mm(tc):
            gb_ = 5 + tc % 2
            p.wait("t", [late["gatesb"], late["bgeb"], tok_oscl])
            p.wait("t", bank_free[gb_])
            for c in range(ND):
                p.emit("t", lambda t, c=c, tc=tc, gb_=gb_: t.matmul(
                    PS[gb_][:, 0:10],
                    xn[:, c * T + tc * 128: c * T + (tc + 1) * 128],
                    gatesb[:, c * 10:(c + 1) * 10],
                    start=(c == 0), stop=False))
            p.emit("t", lambda t, gb_=gb_: t.matmul(
                PS[gb_][:, 0:10], oscl[:, :], bgeb[:, :], start=False, stop=True))
            gmm = p.sig("t")
            p.wait("s", gmm)
            p.emit("s", lambda s, tc=tc, gb_=gb_: s.copy(
                out=glog[:, tc * 10:(tc + 1) * 10], in_=PS[gb_][:, 0:10]))
            glog_tok[tc] = bank_free[gb_] = p.sig("s")

        def gates_steps():
            for pr in range(n_tc // 2):
                yield from gates_pair_steps(2 * pr, 2 * pr + 1)

        def gates_pair_steps(tc_a, tc_b):
            Gs = [gsc[:, 0:64], gsc[:, 64:128]]

            def phase_a(G, tc):
                return [
                    lambda v: v.tensor_copy(out=G[:, 0:10], in_=glog[:, tc * 10:(tc + 1) * 10]),
                    lambda v: v.tensor_tensor(out=G[:, 10:11], in0=G[:, 0:1],
                                              in1=G[:, 1:2], op=ALU.subtract),
                    lambda v: v.tensor_scalar(out=G[:, 11:12], in0=G[:, 10:11],
                                              scalar1=0.0, scalar2=None, op0=ALU.is_lt),
                    lambda v: v.tensor_scalar(out=G[:, 12:13], in0=G[:, 11:12],
                                              scalar1=BIG, scalar2=None, op0=ALU.mult),
                    lambda v: v.tensor_scalar(out=G[:, 13:14], in0=G[:, 11:12],
                                              scalar1=-BIG, scalar2=BIG, op0=ALU.mult,
                                              op1=ALU.add),
                    lambda v: v.tensor_scalar(out=G[:, 24:28], in0=G[:, 2:6],
                                              scalar1=G[:, 12:13], scalar2=None,
                                              op0=ALU.subtract),
                    lambda v: v.tensor_scalar(out=G[:, 28:32], in0=G[:, 6:10],
                                              scalar1=G[:, 13:14], scalar2=None,
                                              op0=ALU.subtract),
                    lambda v: v.tensor_reduce(out=G[:, 17:18], in_=G[:, 24:32],
                                              axis=AX.X, op=ALU.max, negate=True),
                    lambda v: v.tensor_scalar(out=G[:, 14:15], in0=G[:, 10:11],
                                              scalar1=-1.0, scalar2=G[:, 10:11],
                                              op0=ALU.mult, op1=ALU.max),
                ]

            def phase_b(G, tc):
                return [
                    lambda v: v.tensor_reduce(out=G[:, 18:19], in_=G[:, 32:40],
                                              axis=AX.X, op=ALU.add),
                    lambda v: v.tensor_scalar(out=G[:, 16:17], in0=G[:, 15:16],
                                              scalar1=1.0, scalar2=None, op0=ALU.add),
                    lambda v: v.reciprocal(out=G[:, 19:20], in_=G[:, 18:19]),
                    lambda v: v.reciprocal(out=G[:, 16:17], in_=G[:, 16:17]),
                    lambda v: v.tensor_scalar(out=G[:, 40:48], in0=G[:, 32:40],
                                              scalar1=G[:, 19:20], scalar2=G[:, 16:17],
                                              op0=ALU.mult, op1=ALU.mult),
                    lambda v: v.tensor_reduce(out=G[:, 20:21], in_=G[:, 40:48],
                                              axis=AX.X, op=ALU.max),
                    lambda v: v.tensor_scalar(out=G[:, 48:56], in0=G[:, 40:48],
                                              scalar1=G[:, 20:21], scalar2=None,
                                              op0=ALU.is_ge),
                    lambda v: v.tensor_scalar(out=G[:, 48:56], in0=G[:, 48:56],
                                              scalar1=-1.0, scalar2=1.0, op0=ALU.mult,
                                              op1=ALU.add),
                    lambda v: v.tensor_tensor(out=G[:, 56:64], in0=G[:, 48:56],
                                              in1=G[:, 40:48], op=ALU.mult),
                    lambda v: v.tensor_reduce(out=G[:, 21:22], in_=G[:, 56:64],
                                              axis=AX.X, op=ALU.max),
                    lambda v: v.scalar_tensor_tensor(
                        out=wse[:, tc * NE:(tc + 1) * NE], in0=G[:, 40:48],
                        scalar=G[:, 21:22], in1=G[:, 40:48], op0=ALU.is_ge,
                        op1=ALU.mult),
                ]

            chains_a = [phase_a(Gs[0], tc_a), phase_a(Gs[1], tc_b)]
            first = [True]

            def astep(op):
                if first[0]:
                    p.wait("v", glog_tok[tc_a])
                    p.wait("v", glog_tok[tc_b])
                    first[0] = False
                p.emit("v", op)
                p.sig("v")

            for i in range(len(chains_a[0])):
                for idx in range(2):
                    yield lambda op=chains_a[idx][i]: astep(op)

            se_box = [None]

            def sstep():
                p.wait("s", p.tok("v"))
                for G in Gs:
                    p.emit("s", lambda s, G=G: s.activation(
                        G[:, 32:40], G[:, 24:32], AF.Exp, bias=G[:, 17:18], scale=1.0))
                    p.emit("s", lambda s, G=G: s.activation(
                        G[:, 15:16], G[:, 14:15], AF.Exp, scale=-1.0))
                se_box[0] = p.sig("s")

            yield sstep

            chains_b = [phase_b(Gs[0], tc_a), phase_b(Gs[1], tc_b)]
            first_b = [True]

            def bstep(op):
                if first_b[0]:
                    p.wait("v", se_box[0])
                    first_b[0] = False
                p.emit("v", op)
                p.sig("v")

            for i in range(len(chains_b[0])):
                for idx in range(2):
                    yield lambda op=chains_b[idx][i]: bstep(op)

        def swiglu_load(srcR_h, jc):
            sa = (jc % 2) * 2
            p.wait("g", rel["wbuf"][jc % 2])
            la = p.dma("g", lambda g, sa=sa, srcR_h=srcR_h, jc=jc: g.dma_start(
                wb_b[:, sa * 1024:(sa + 2) * 1024], srcR_h[jc]))
            return la, la

        def swiglu_prefetch(srcT_h):
            return {jc: swiglu_load(srcT_h, jc) for jc in (0, 1)}

        def swiglu_stage(b, srcT_h, dst_sb, pre=None, cb=None, ti_toks=None):
            sc_tt_tok = dict(rel["sc_tt"])
            for jc in range(NJC):
                sa = (jc % 2) * 2
                a_v = wb_b[:, sa * 1024:(sa + 1) * 1024]
                b_v = wb_b[:, (sa + 1) * 1024:(sa + 2) * 1024]
                if pre is not None and jc in pre:
                    la, lb = pre[jc]
                else:
                    la, lb = swiglu_load(srcT_h, jc)

                for ti in range(n_tiles):
                    gidx = ti // 2
                    par = ti % 2
                    ab = (gidx % 2) * 4 + par
                    bb = (gidx % 2) * 4 + 2 + par
                    p.wait("t", la)
                    p.wait("t", lb)
                    p.wait("t", bank_free[ab])
                    for c in range(ND):
                        p.emit("t", lambda t, c=c, ti=ti, a_v=a_v, ab=ab: t.matmul(
                            PS[ab][:, :], a_v[:, c * 128:(c + 1) * 128],
                            xn[:, c * T + ti * 512: c * T + (ti + 1) * 512],
                            start=(c == 0), stop=(c == ND - 1)))
                    amm = p.sig("t")
                    p.wait("t", bank_free[bb])
                    for c in range(ND):
                        p.emit("t", lambda t, c=c, ti=ti, b_v=b_v, bb=bb: t.matmul(
                            PS[bb][:, :], b_v[:, c * 128:(c + 1) * 128],
                            xn[:, c * T + ti * 512: c * T + (ti + 1) * 512],
                            start=(c == 0), stop=(c == ND - 1)))
                    bmm = p.sig("t")
                    sc2 = sc_a if ti % 2 == 0 else sc_b
                    p.wait("s", amm)
                    p.wait("s", sc_tt_tok[ti % 2])
                    p.emit("s", lambda s, ab=ab, sc2=sc2: s.activation(
                        sc2[:, :], PS[ab][:, :], AF.Silu))
                    st_ = p.sig("s")
                    bank_free[ab] = st_
                    p.wait("v", st_)
                    p.wait("v", bmm)
                    p.emit("v", lambda v, jc=jc, ti=ti, bb=bb, sc2=sc2, dst_sb=dst_sb: v.tensor_tensor(
                        out=dst_sb[:, jc * T + ti * 512: jc * T + (ti + 1) * 512],
                        in0=sc2[:, :], in1=PS[bb][:, :], op=ALU.mult))
                    sc_tt_tok[ti % 2] = p.sig("v")
                    bank_free[bb] = sc_tt_tok[ti % 2]
                    if ti_toks is not None and jc == NJC - 1:
                        ti_toks[ti] = sc_tt_tok[ti % 2]
                    if cb is not None:
                        cb(jc, ti)
                rel["wbuf"][jc % 2] = p.tok("t")
            rel["sc_tt"] = dict(sc_tt_tok)
            return p.tok("v")

        def stage_wout(b, swg_tok, swg_ti=None, after_first_loads=None):
            load_tok = [None] * 8
            comp_tok = [None] * 8

            def emit_load(q8):
                sl = q8 % 2
                p.wait("g", comp_tok[q8 - 2] if q8 >= 2 else rel["woutbuf"][sl])
                load_tok[q8] = p.dma("g", lambda g, q8=q8, sl=sl: g.dma_start(
                    wob[sl][:, :], woutR_h[q8]))

            emit_load(0)
            emit_load(1)
            if after_first_loads is not None:
                after_first_loads()
            for q8 in range(8):
                sl = q8 % 2
                for tc in range(n_tc):
                    wb_ = tc % 2
                    p.wait("t", load_tok[q8])
                    p.wait("t", swg_ti[tc // 4] if swg_ti is not None else swg_tok)
                    p.wait("t", bank_free[wb_])
                    for jc in range(NJC):
                        p.emit("t", lambda t, jc=jc, tc=tc, sl=sl, wb_=wb_: t.matmul(
                            PS[wb_][:, 0:128],
                            swg[:, jc * T + tc * 128: jc * T + (tc + 1) * 128],
                            wob[sl][:, jc * 128:(jc + 1) * 128],
                            start=(jc == 0), stop=(jc == NJC - 1)))
                    wmm = p.sig("t")
                    p.wait("v", wmm)
                    p.emit("v", lambda v, tc=tc, q8=q8, wb_=wb_: v.tensor_tensor(
                        out=acc[:, tc * 1024 + q8 * 128: tc * 1024 + (q8 + 1) * 128],
                        in0=PS[wb_][:, 0:128],
                        in1=acc[:, tc * 1024 + q8 * 128: tc * 1024 + (q8 + 1) * 128],
                        op=ALU.add))
                    bank_free[wb_] = p.sig("v")
                comp_tok[q8] = p.tok("t")
                rel["woutbuf"][sl] = comp_tok[q8]
                if q8 + 2 < 8:
                    emit_load(q8 + 2)
            return p.tok("v")

        def stage_experts(b, h_tok, wse_tok, xn_rel_tok):
            order = [(e, dh) for e in range(NE) for dh in range(2)]
            load_tok = {}
            comp_tok = {}
            out_toks = []

            def emit_load(i):
                e, dh = order[i]
                buf = w2b[i % 2]
                p.wait("g", comp_tok[i - 2] if i >= 2 else xn_rel_tok)
                load_tok[i] = p.dma("g", lambda g, e=e, dh=dh, buf=buf: g.dma_start(
                    buf[:, :], w2R_h[e, dh]))

            emit_load(0)
            emit_load(1)
            for i, (e, dh) in enumerate(order):
                buf = w2b[i % 2]
                for tc in range(n_tc):
                    wb_ = tc % 2
                    p.wait("t", load_tok[i])
                    p.wait("t", h_tok)
                    p.wait("t", bank_free[wb_])
                    for jc in range(NJC):
                        p.emit("t", lambda t, jc=jc, tc=tc, buf=buf, wb_=wb_: t.matmul(
                            PS[wb_][:, :],
                            h_sb[:, jc * T + tc * 128: jc * T + (tc + 1) * 128],
                            buf[:, jc * 512:(jc + 1) * 512],
                            start=(jc == 0), stop=(jc == NJC - 1)))
                    emm = p.sig("t")
                    p.wait("v", emm)
                    p.wait("v", wse_tok)
                    p.emit("v", lambda v, tc=tc, dh=dh, e=e, wb_=wb_: v.scalar_tensor_tensor(
                        out=acc[:, tc * 1024 + dh * 512: tc * 1024 + dh * 512 + 512],
                        in0=PS[wb_][:, :], scalar=wse[:, tc * NE + e: tc * NE + e + 1],
                        in1=acc[:, tc * 1024 + dh * 512: tc * 1024 + dh * 512 + 512],
                        op0=ALU.mult, op1=ALU.add))
                    stok = p.sig("v")
                    bank_free[wb_] = stok
                    if e == NE - 1 and dh == 1:
                        p.wait("y", stok)
                        out_toks.append(p.dma("y", lambda y, b=b, tc=tc: y.dma_start(
                            out_h[b, tc * 128:(tc + 1) * 128, :],
                            acc[:, tc * 1024:(tc + 1) * 1024])))
                comp_tok[i] = p.tok("t")
                if i + 2 < len(order):
                    emit_load(i + 2)
            return out_toks

        def emit_tap(name):
            if name in tap_out:
                ncols = tap_out[name].shape[-1]
                src = {"xn": xn, "kv2": kv2, "k16": k16, "acc": acc, "wse": wse,
                       "swg": swg, "h": h_sb, "qfm": q_fm, "apre": apre,
                       "rn": rn_sb, "gsc": gsc}[name]
                p.barrier(("t", "v", "s"))
                tv = p.tok("v")
                p.wait("g", tv)
                gt = p.dma("g", lambda g, src=src, name=name, ncols=ncols: g.dma_start(
                    tap_out[name][:, :], src[:, 0:ncols]))
                p.wait("v", gt)
                p.emit("v", lambda v: v.nop())

        for b in range(NB):
            acc_ready = [None] * n_tc
            wk_tok = load_w_chunked(wk_sb, wk_h, rel["R0"])
            wv_tok = load_w_chunked(wv_sb, wv_h, rel["R1"])
            if b == 0:
                late.update(emit_late_consts())
            wq_tok = load_w_chunked(wq_sb, wq_h, rel["R3"])

            kv_tok = stage_lnkv(b, acc_ready, wk_tok, wv_tok)
            p.barrier()
            if b == 0:
                emit_tap("xn")
            rel["R0"] = p.tok("t")
            if b == 0:
                emit_tap("kv2")
                emit_tap("k16")

            wo_tok = load_w_chunked(wo_sb, wo_h, rel["R0"])
            pre_in = swiglu_prefetch(winR_h)
            stage_qattn(b, wq_tok, wo_tok, kv_tok, acc_ready)
            p.barrier()
            rel["R2"] = p.tok("t")
            rel["R3"] = p.tok("t")
            if b == 0:
                emit_tap("acc")
                emit_tap("qfm")
                emit_tap("apre")

            gates_iter = gates_steps()

            def gdrain(jc, ti, n=6):
                if jc < 1:
                    return
                for _ in range(n):
                    st = next(gates_iter, None)
                    if st is None:
                        return
                    st()

            swg_ti = {}
            swg_tok = swiglu_stage(b, winR_h, swg, pre_in, cb=gdrain, ti_toks=swg_ti)
            for st in gates_iter:
                st()
            gate_tok = p.tok("v")
            if b == 0:
                emit_tap("swg")
                emit_tap("wse")
                emit_tap("gsc")
            pre_h = {}
            stage_wout(b, swg_tok, swg_ti,
                       after_first_loads=lambda: pre_h.update(swiglu_prefetch(w1R_h)))
            p.barrier()

            h_tok = swiglu_stage(b, w1R_h, h_sb, pre_h)
            p.barrier()
            xn_rel = p.tok("t")
            if b == 0:
                emit_tap("h")

            out_toks = stage_experts(b, h_tok, gate_tok, xn_rel)
            p.barrier(("t", "v", "s", "y"))
            rel["acc"] = p.queue_done("y")
            rel["R0"] = rel["R1"] = rel["R2"] = rel["R3"] = p.tok("t")
            rel["xarena"] = p.tok("t")
            rel["wbuf"] = [p.tok("t"), p.tok("t")]
            rel["woutbuf"] = [p.tok("t"), p.tok("t")]

        p.wait_force("y", p.queue_done("y"))
        p.emit("y", lambda y: y.nop())
        p.wait_force("g", p.queue_done("g"))
        p.emit("g", lambda g: g.nop())

        p.check_deadlock()
        n_ins = {e: len(p.items[e]) for e in ENGINES}

        def runner(name):
            def run(eng):
                for waits, fn, incs in p.items[name]:
                    for s_, v_ in waits.items():
                        eng.wait_ge(sems[s_], v_)
                    ins = fn(eng)
                    for sem_n, amt in incs:
                        ins.then_inc(sems[sem_n], amt)
            return run

        blk.tensor(runner("t"))
        blk.vector(runner("v"))
        blk.scalar(runner("s"))
        blk.gpsimd(runner("g"))
        blk.sync(runner("y"))

    return nc, n_ins


def prep_inputs(inputs, n_tc=16, n_cores=8):
    T = n_tc * 128
    f32c = lambda a: np.ascontiguousarray(np.asarray(a), dtype=np.float32)
    x = f32c(inputs["L0_attn_inputs_q"])[:, :T, :]

    def projR(w):
        return f32c(np.asarray(w, dtype=np.float32)
                    .reshape(ND, 128, D).transpose(1, 0, 2).reshape(128, ND * D))

    def swigluR(w):
        wT = np.asarray(w, dtype=np.float32).T
        B = wT.reshape(ND, 128, 2 * HID).transpose(1, 0, 2)
        out = np.empty((NJC, 128, 2048), dtype=np.float32)
        for jc in range(NJC):
            a = B[:, :, jc * 128:(jc + 1) * 128].reshape(128, 1024)
            bb = B[:, :, HID + jc * 128: HID + (jc + 1) * 128].reshape(128, 1024)
            out[jc] = np.concatenate([a, bb], axis=1)
        return f32c(out)

    winR = swigluR(inputs["L0_ffn_w_in"])
    w1R = swigluR(inputs["L0_ffn_w1"])
    woutT = np.asarray(inputs["L0_ffn_w_out"], dtype=np.float32).T
    C = woutT.reshape(NJC, 128, D).transpose(1, 0, 2)
    woutR = f32c(np.stack([C[:, :, q8 * 128:(q8 + 1) * 128].reshape(128, 2048)
                           for q8 in range(8)]))
    w2T = np.asarray(inputs["L0_ffn_w2"], dtype=np.float32).transpose(0, 2, 1)
    A = w2T.reshape(NE, NJC, 128, D).transpose(0, 2, 1, 3)
    w2R = f32c(np.stack([A[:, :, :, 0:512].reshape(NE, 128, NJC * 512),
                         A[:, :, :, 512:1024].reshape(NE, 128, NJC * 512)], axis=1))
    gatesT = f32c(np.concatenate([np.asarray(inputs["L0_ffn_wg"]).T,
                                  np.asarray(inputs["L0_ffn_we"]).T], axis=1))
    bk = f32c(np.asarray(inputs["L0_attn_bk"]).reshape(1, D))
    bv = f32c(np.asarray(inputs["L0_attn_bv"]).reshape(1, D))
    bo = f32c(np.asarray(inputs["L0_attn_bo"]).reshape(1, D))
    bq = f32c(np.asarray(inputs["L0_attn_bq"]).reshape(1, D))
    geb = f32c(np.concatenate([np.asarray(inputs["L0_ffn_gb"]),
                               np.asarray(inputs["L0_ffn_eb"])]).reshape(1, 10))
    ident = np.eye(128, dtype=np.float32)
    e16 = np.zeros((NPAIR, 128, 128), dtype=np.float32)
    for pr in range(NPAIR):
        e16[pr, 2 * pr, 0:64] = 1.0
        e16[pr, 2 * pr + 1, 64:128] = 1.0
    maskb = np.zeros((128, 2), dtype=np.float32)
    maskb[0:64, 0] = 1.0
    maskb[64:128, 1] = 1.0

    shared = dict(wq=projR(inputs["L0_attn_Wq"]), wk=projR(inputs["L0_attn_Wk"]),
                  wv=projR(inputs["L0_attn_Wv"]), wo=projR(inputs["L0_attn_Wo"]),
                  winR=winR, w1R=w1R, woutR=woutR, w2R=w2R, gatesT=gatesT,
                  bk=bk, bv=bv, bo=bo, bq=bq, geb=geb,
                  ident=ident, e16=e16, maskb=maskb)
    in_maps = []
    for c in range(n_cores):
        m = dict(shared)
        m["x"] = np.ascontiguousarray(x[c * NB:(c + 1) * NB])
        in_maps.append(m)
    return in_maps


def kernel(**inputs):
    n_tc = 16
    in_maps = prep_inputs(inputs, n_tc=n_tc, n_cores=8)
    nc, _ = build(n_tc=n_tc)
    res = run_bass_kernel_spmd(nc, in_maps, core_ids=list(range(8)))
    outs = [np.asarray(r["out"], dtype=np.float32) for r in res.results]
    return np.concatenate(outs, axis=0)


# version 19
# speedup vs baseline: 1.0402x; 1.0402x over previous
import contextlib
import numpy as np

import concourse.bass as bass
import concourse.mybir as mybir
from concourse.bass_utils import run_bass_kernel_spmd

F32 = mybir.dt.float32
BF16 = mybir.dt.bfloat16
U8 = mybir.dt.uint8
AF = mybir.ActivationFunctionType
ALU = mybir.AluOpType
AX = mybir.AxisListType

D = 1024
ND = 8
HID = 2048
NJC = 16
NH = 16
NPAIR = 8
NE = 8
NB = 2
EPS_LN = 1e-5
EPS_ATTN = 1e-6
BIG = 1e9
LNKV_LA = 2

ENGINES = ("t", "v", "s", "g", "y")
SYNC_SELF_OK = ("t", "v", "s")
DMA_POOL = {"g": 8, "y": 4}


class Prog:
    def __init__(self):
        self.items = {e: [] for e in ENGINES}
        self.count = {}
        self.pending = {e: {} for e in ENGINES}
        self.max_waited = {e: {} for e in ENGINES}
        self.dma_slot = {q: 0 for q in DMA_POOL}
        self.sem_names = set(ENGINES) | {"gc"}
        for q, n in DMA_POOL.items():
            for i in range(n):
                self.sem_names.add(f"{q}{i}")
        for s in self.sem_names:
            self.count[s] = 0

    def wait(self, eng, tok):
        if tok is None:
            return
        if isinstance(tok, list):
            for t in tok:
                self.wait(eng, t)
            return
        sem, val = tok
        if sem == eng and eng in SYNC_SELF_OK:
            return
        if val <= self.max_waited[eng].get(sem, 0):
            return
        if val <= self.pending[eng].get(sem, 0):
            return
        self.pending[eng][sem] = val

    def wait_force(self, eng, tok):
        if tok is None:
            return
        if isinstance(tok, list):
            for t in tok:
                self.wait_force(eng, t)
            return
        sem, val = tok
        if val <= self.max_waited[eng].get(sem, 0):
            return
        if val <= self.pending[eng].get(sem, 0):
            return
        self.pending[eng][sem] = val

    AUTO_CHAIN = ("v", "s")

    def emit(self, eng, fn):
        if eng in self.AUTO_CHAIN and self.items[eng]:
            if not self.items[eng][-1][2]:
                self.sig(eng)
            self.wait_force(eng, (eng, self.count[eng]))
        w = self.pending[eng]
        self.pending[eng] = {}
        for s, v in w.items():
            self.max_waited[eng][s] = max(self.max_waited[eng].get(s, 0), v)
        self.items[eng].append([w, fn, []])

    def sig(self, eng, amt=1, sem=None):
        sem = sem or eng
        assert self.items[eng], f"sig on empty engine {eng}"
        self.items[eng][-1][2].append((sem, amt))
        self.count[sem] += amt
        return (sem, self.count[sem])

    def dma(self, q, fn):
        slot = self.dma_slot[q]
        self.dma_slot[q] = (slot + 1) % DMA_POOL[q]
        sem = f"{q}{slot}"
        self.wait_force(q, (sem, self.count[sem]))
        self.emit(q, fn)
        return self.sig(q, 16, sem=sem)

    def queue_done(self, q):
        return [(f"{q}{i}", self.count[f"{q}{i}"]) for i in range(DMA_POOL[q])]

    def tok(self, eng):
        return (eng, self.count[eng])

    def sig_or_reuse(self, eng):
        if not self.items[eng]:
            return None
        incs = self.items[eng][-1][2]
        if incs:
            s, _ = incs[-1]
            return (s, self.count[s])
        return self.sig(eng)

    def barrier(self, engines=("t", "v", "s")):
        toks = []
        for e in engines:
            if e in DMA_POOL:
                toks.extend(self.queue_done(e))
            else:
                t = self.sig_or_reuse(e)
                if t is not None:
                    toks.append(t)
        for e in engines:
            for t in toks:
                self.wait(e, t)

    def check_deadlock(self):
        ptr = {e: 0 for e in ENGINES}
        val = {s: 0 for s in self.sem_names}
        progress = True
        while progress:
            progress = False
            for e in ENGINES:
                while ptr[e] < len(self.items[e]):
                    waits, _, incs = self.items[e][ptr[e]]
                    if all(val[s] >= v for s, v in waits.items()):
                        for s, amt in incs:
                            val[s] += amt
                        ptr[e] += 1
                        progress = True
                    else:
                        break
        stuck = {}
        for e in ENGINES:
            if ptr[e] < len(self.items[e]):
                waits = self.items[e][ptr[e]][0]
                stuck[e] = {"at": ptr[e], "of": len(self.items[e]), "waits": waits,
                            "have": {s: val[s] for s in waits}}
        if stuck:
            raise RuntimeError(f"deadlock in program build: {stuck}")


def build(n_tc=16, taps=()):
    T = n_tc * 128
    n_tiles = (n_tc + 3) // 4

    nc = bass.Bass("TRN2", target_bir_lowering=False)

    def din(name, shape):
        return nc.dram_tensor(name, shape, F32, kind="ExternalInput").ap()

    x_h = din("x", [NB, T, D])
    out_h = nc.dram_tensor("out", [NB, T, D], F32, kind="ExternalOutput").ap()
    wq_h = din("wq", [128, ND * 1024])
    wk_h = din("wk", [128, ND * 1024])
    wv_h = din("wv", [128, ND * 1024])
    wo_h = din("wo", [128, ND * 1024])
    winR_h = din("winR", [NJC, 128, 2048])
    w1R_h = din("w1R", [NJC, 128, 2048])
    woutR_h = din("woutR", [8, 128, 2048])
    w2R_h = din("w2R", [NE, 2, 128, NJC * 512])
    gatesT_h = din("gatesT", [D, 10])
    bk_h = din("bk", [1, D])
    bv_h = din("bv", [1, D])
    bo_h = din("bo", [1, D])
    bq_h = din("bq", [1, D])
    geb_h = din("geb", [1, 10])

    ident_h = din("ident", [128, 128])
    e16_h = din("e16", [NPAIR, 128, 128])
    maskb_h = din("maskb", [128, 2])

    tap_out = {}
    for name, shape in taps:
        tap_out[name] = nc.dram_tensor("tap_" + name, list(shape), F32,
                                       kind="ExternalOutput").ap()

    es = contextlib.ExitStack()
    with es:
        acc = es.enter_context(nc.sbuf_tensor("acc", [128, n_tc * 1024], F32))
        harena = es.enter_context(nc.sbuf_tensor("harena", [128, max(65536, NJC * T * 2)], U8))
        xarena = es.enter_context(nc.sbuf_tensor("xarena", [128, max(n_tc * 2048, 32768)], U8))
        wbuf = es.enter_context(nc.sbuf_tensor("wbuf", [128, 8192], U8))
        woutbuf = es.enter_context(nc.sbuf_tensor("woutbuf", [128, 8192], U8))
        sc_a = es.enter_context(nc.sbuf_tensor("sc_a", [128, 512], F32))
        sc_b = es.enter_context(nc.sbuf_tensor("sc_b", [128, 512], F32))
        sc_c = es.enter_context(nc.sbuf_tensor("sc_c", [128, 512], F32))

        kv2 = es.enter_context(nc.sbuf_tensor("kv2", [128, NPAIR * 128], BF16))
        k16 = es.enter_context(nc.sbuf_tensor("k16", [128, ND * 128], BF16))
        rn_sb = es.enter_context(nc.sbuf_tensor("rn_sb", [128, 512], BF16))
        rb_sb = es.enter_context(nc.sbuf_tensor("rb_sb", [128, 1024], BF16))
        wse = es.enter_context(nc.sbuf_tensor("wse", [128, n_tc * NE], F32))
        gsc = es.enter_context(nc.sbuf_tensor("gsc", [128, 128], F32))
        glog = es.enter_context(nc.sbuf_tensor("glog", [128, n_tc * 10], F32))
        stat = es.enter_context(nc.sbuf_tensor("stat", [128, 32], F32))
        ident32 = es.enter_context(nc.sbuf_tensor("ident32_s", [128, 128], F32))
        e16b = es.enter_context(nc.sbuf_tensor("e16b", [128, NPAIR * 128], BF16))
        bbk = es.enter_context(nc.sbuf_tensor("bbk_s", [128, D], BF16))
        bbv = es.enter_context(nc.sbuf_tensor("bbv_s", [128, D], BF16))
        bbo = es.enter_context(nc.sbuf_tensor("bbo_s", [128, D], BF16))
        bqf = es.enter_context(nc.sbuf_tensor("bqf_s", [128, D], BF16))
        bgeb = es.enter_context(nc.sbuf_tensor("bgeb_s", [128, 10], BF16))
        oscl = es.enter_context(nc.sbuf_tensor("oscl_s", [128, 128], BF16))
        onesr = es.enter_context(nc.sbuf_tensor("onesr_s", [128, 512], BF16))
        maskb = es.enter_context(nc.sbuf_tensor("maskb_s", [128, 2], F32))
        onesb = es.enter_context(nc.sbuf_tensor("onesb", [128, 1], BF16))
        czero = es.enter_context(nc.sbuf_tensor("czero_s", [128, 4], F32))
        gatesb = es.enter_context(nc.sbuf_tensor("gatesb", [128, ND * 10], BF16))

        ha_b = harena.bitcast(BF16)
        ha_f = harena.bitcast(F32)
        xa_b = xarena.bitcast(BF16)
        wb_b = wbuf.bitcast(BF16)
        wo8_b = woutbuf.bitcast(BF16)

        def bv(base, byte_off, cols):
            return base[:, byte_off // 2: byte_off // 2 + cols]

        def fv(base, byte_off, cols):
            return base[:, byte_off // 4: byte_off // 4 + cols]

        R0, R1, R2, R3 = 0, 16384, 32768, 49152
        wk_sb = bv(ha_b, R0, ND * 1024)
        wv_sb = bv(ha_b, R1, ND * 1024)
        wq_sb = bv(ha_b, R3, ND * 1024)
        wo_sb = bv(ha_b, R0, ND * 1024)
        q_fm2 = [bv(ha_b, R1, ND * 512), bv(ha_b, R1 + 8192, ND * 512)]
        apre2 = [bv(ha_b, R2, ND * 512), bv(ha_b, R2 + 8192, ND * 512)]
        q_fm = q_fm2[0]
        apre = apre2[0]
        ktm = [bv(ha_b, R2, 1024), bv(ha_b, R2 + 2048, 1024)]
        vtm = [bv(ha_b, R2 + 4096, 1024), bv(ha_b, R2 + 6144, 1024)]
        xnst = [fv(ha_f, R2 + 8192, 1024), fv(ha_f, R2 + 12288, 1024)]
        swg = ha_b[:, 0:NJC * T]
        h_sb = ha_b[:, 0:NJC * T]
        xn = xa_b[:, 0:n_tc * 1024]
        w2b = [bv(xa_b, 0, 16 * 512), bv(xa_b, 16384, 16 * 512)]
        wob = [bv(wo8_b, 0, 16 * 128), bv(wo8_b, 4096, 16 * 128)]

        PS = [es.enter_context(nc.psum_tensor(f"pb{i}", [128, 512], F32)) for i in range(8)]

        p = Prog()
        sems = {s: es.enter_context(nc.semaphore(f"sem_{s}")) for s in sorted(p.sem_names)}
        blk = es.enter_context(nc.Block())
        bank_free = [None] * 8
        rel = {"R0": None, "R1": None, "R2": None, "R3": None, "acc": None,
               "wbuf": [None, None], "woutbuf": [None, None], "xarena": None}
        late = {}
        glog_tok = [None] * n_tc
        rel["sc_tt"] = {0: None, 1: None}

        def ld(fn):
            return p.dma("g", fn)

        tok_ident = ld(lambda g: g.dma_start(ident32[:, :], ident_h[:, :]))
        p.emit("g", lambda g: g.memset(czero[:, 0:1], 0.0))
        p.emit("g", lambda g: g.memset(czero[:, 1:2], EPS_LN))
        p.emit("g", lambda g: g.memset(czero[:, 2:3], 1.0))
        p.emit("g", lambda g: g.memset(oscl[:, :], 0.0))
        p.emit("g", lambda g: g.memset(oscl[0:1, :], 1.0))
        p.emit("g", lambda g: g.memset(onesb[:, :], 1.0))
        p.emit("g", lambda g: g.memset(bbk[:, :], 0.0))
        p.emit("g", lambda g: g.memset(bbv[:, :], 0.0))
        p.emit("g", lambda g: g.memset(rn_sb[:, :], 0.0))
        init_tok = p.sig("g", sem="gc")
        tok_czero = tok_oscl = tok_onesb = init_tok
        tok_bbk = ld(lambda g: g.dma_start(bbk[0:1, :], bk_h[:, :]))
        tok_bbv = ld(lambda g: g.dma_start(bbv[0:1, :], bv_h[:, :]))
        nc.const_aps.register(F32, 0.0, czero[:, 0:1])
        nc.const_aps.register(F32, EPS_LN, czero[:, 1:2])
        nc.const_aps.register(F32, 1.0, czero[:, 2:3])
        tok_maskb = p.dma("y", lambda y: y.dma_start(maskb[:, :], maskb_h[:, :]))

        def emit_late_consts():
            lc = {}
            p.emit("g", lambda g: g.memset(onesr[:, :], 0.0))
            p.emit("g", lambda g: g.memset(onesr[0:1, :], 1.0))
            p.emit("g", lambda g: g.memset(bqf[:, :], 0.0))
            p.emit("g", lambda g: g.memset(bbo[:, :], 0.0))
            p.emit("g", lambda g: g.memset(bgeb[:, :], 0.0))
            lc["z"] = p.sig("g", sem="gc")
            lc["e16"] = ld(lambda g: g.dma_start(
                e16b[:, :].rearrange("p (q m) -> p q m", q=NPAIR),
                e16_h.rearrange("q k m -> k q m")))
            lc["bqf"] = [ld(lambda g: g.dma_start(bqf[0:1, :], bq_h[:, :])), lc["z"]]
            lc["bbo"] = [ld(lambda g: g.dma_start(bbo[0:1, :], bo_h[:, :])), lc["z"]]
            lc["onesr"] = lc["z"]
            lc["bgeb"] = [ld(lambda g: g.dma_start(bgeb[0:1, :], geb_h[:, :])), lc["z"]]
            lc["gatesb"] = ld(lambda g: g.dma_start(
                gatesb[:, :].rearrange("p (c n) -> p c n", c=ND),
                gatesT_h.rearrange("(c k) n -> k c n", k=128)))
            return lc

        def load_w_chunked(dst_view, src_ap, rel_tok):
            p.wait("g", rel_tok)
            return p.dma("g", lambda g: g.dma_start(dst_view, src_ap))

        def stage_lnkv(b, acc_ready, wk_tok, wv_tok):
            ln_done = [None] * n_tc
            tp_tok = [None] * n_tc
            xn_ready = [None] * n_tc
            kbanks = (0, 7)
            vbanks = (1,)
            exp_war = {0: None, 1: None}
            kv_started = [False, False, False]
            kvmm_tok = [None] * n_tc
            ktoks = {}
            vtoks = {}

            def emit_kv(tc):
                sl = tc % 2
                for half in range(2):
                    kvbank = 2 + half
                    p.wait("t", ktoks[tc * 2 + half])
                    p.wait("t", vtoks[tc * 2 + half])
                    if tc == 0:
                        p.wait("t", bank_free[kvbank])
                        p.wait("t", bank_free[4])
                        p.wait("t", tok_onesb)
                    for qq in range(4):
                        st = not kv_started[half]
                        kv_started[half] = True
                        p.emit("t", lambda t, sl=sl, half=half, qq=qq, st=st, kvbank=kvbank: t.matmul(
                            PS[kvbank][:, qq * 128:(qq + 1) * 128],
                            ktm[sl][:, half * 512 + qq * 128: half * 512 + (qq + 1) * 128],
                            vtm[sl][:, half * 512 + qq * 128: half * 512 + (qq + 1) * 128],
                            start=st, stop=False, skip_group_check=True))
                    for blkc in range(4):
                        st = not kv_started[2]
                        kv_started[2] = True
                        col = half * 4 + blkc
                        p.emit("t", lambda t, sl=sl, half=half, blkc=blkc, st=st, col=col: t.matmul(
                            PS[4][:, col:col + 1],
                            ktm[sl][:, half * 512 + blkc * 128: half * 512 + (blkc + 1) * 128],
                            onesb[:, :],
                            start=st, stop=False, skip_group_check=True))
                kvmm_tok[tc] = p.sig("t")

            p1_rstd = [None] * n_tc

            def emit_p1_stats(tc):
                if b > 0 and tc < 4:
                    p.wait("y", rel["acc"])
                acc_ready[tc] = p.dma("y", lambda y, b=b, tc=tc: y.dma_start(
                    acc[:, tc * 1024:(tc + 1) * 1024], x_h[b, tc * 128:(tc + 1) * 128, :]))
                so = (tc % 2) * 16
                xch = acc[:, tc * 1024:(tc + 1) * 1024]
                p.wait("v", acc_ready[tc])
                if tc >= 2:
                    p.wait("v", ln_done[tc - 2])
                p.emit("v", lambda v, xch=xch, so=so: v.bn_stats(
                    stat[:, so:so + 6], xch[:, 0:512]))
                p.emit("v", lambda v, xch=xch, so=so: v.bn_stats(
                    stat[:, so + 6:so + 12], xch[:, 512:1024]))
                p.emit("v", lambda v, so=so: v.bn_aggr(
                    stat[:, so + 12:so + 14], stat[:, so:so + 12]))
                tv = p.sig("v")
                p.wait("s", tv)
                p.wait("s", tok_czero)
                if tc >= 2:
                    p.wait("s", ln_done[tc - 2])
                p.emit("s", lambda s, so=so: s.activation(
                    stat[:, so + 14:so + 15], stat[:, so + 13:so + 14], AF.Ln,
                    bias=czero[:, 1:2], scale=1.0))
                p.emit("s", lambda s, so=so: s.activation(
                    stat[:, so + 15:so + 16], stat[:, so + 14:so + 15], AF.Exp,
                    scale=-0.5))
                p1_rstd[tc] = p.sig("s")

            def emit_p1_norm(tc):
                so = (tc % 2) * 16
                xch = acc[:, tc * 1024:(tc + 1) * 1024]
                p.wait("v", p1_rstd[tc])
                if tc >= 2:
                    p.wait("v", tp_tok[tc - 2])
                sl = tc % 2
                p.emit("v", lambda v, sl=sl, so=so, xch=xch: v.tensor_scalar(
                    out=xnst[sl][:, :], in0=xch,
                    scalar1=stat[:, so + 12:so + 13], scalar2=stat[:, so + 15:so + 16],
                    op0=ALU.subtract, op1=ALU.mult))
                ln_done[tc] = p.sig("v")

            def emit_p1_tp(tc):
                sl = tc % 2
                p.wait("t", ln_done[tc])
                p.wait("t", bank_free[5])
                p.wait("t", bank_free[6])
                p.wait("t", tok_ident)
                for c in range(ND):
                    p.emit("t", lambda t, sl=sl, c=c: t.transpose(
                        PS[5 + c // 4][:, (c % 4) * 128:(c % 4 + 1) * 128],
                        xnst[sl][:, c * 128:(c + 1) * 128], ident32[:, :]))
                tp_tok[tc] = p.sig("t")
                p.wait("s", tp_tok[tc])
                p.emit("s", lambda s, tc=tc: s.copy(
                    out=xn[:, :].rearrange("p (c t) -> p c t", c=ND)[:, 0:4, tc * 128:(tc + 1) * 128],
                    in_=PS[5][:, 0:512].rearrange("p (c t) -> p c t", c=4)))
                bank_free[5] = p.sig("s")
                p.wait("v", tp_tok[tc])
                p.emit("v", lambda v, tc=tc: v.tensor_copy(
                    out=xn[:, :].rearrange("p (c t) -> p c t", c=ND)[:, 4:8, tc * 128:(tc + 1) * 128],
                    in_=PS[6][:, 0:512].rearrange("p (c t) -> p c t", c=4)))
                bank_free[6] = p.sig("v")
                xn_ready[tc] = [bank_free[5], bank_free[6]]

            def emit_p2(tc):
                sl = tc % 2
                for half in range(2):
                    idx = tc * 2 + half
                    kb = kbanks[idx % 2]
                    vb = vbanks[0]
                    p.wait("t", wk_tok)
                    p.wait("t", wv_tok)
                    p.wait("t", [tok_oscl, tok_bbk, tok_bbv])
                    p.wait("t", xn_ready[tc])
                    p.wait("t", bank_free[kb])
                    for c in range(ND):
                        p.emit("t", lambda t, c=c, tc=tc, half=half, kb=kb: t.matmul(
                            PS[kb][:, :],
                            xn[:, c * T + tc * 128: c * T + (tc + 1) * 128],
                            wk_sb[:, c * 1024 + half * 512: c * 1024 + half * 512 + 512],
                            start=(c == 0), stop=False))
                    p.emit("t", lambda t, half=half, kb=kb: t.matmul(
                        PS[kb][:, :], oscl[:, :],
                        bbk[:, half * 512: half * 512 + 512], start=False, stop=True))
                    kmm = p.sig("t")
                    p.wait("t", bank_free[vb])
                    for c in range(ND):
                        p.emit("t", lambda t, c=c, tc=tc, half=half, vb=vb: t.matmul(
                            PS[vb][:, :],
                            xn[:, c * T + tc * 128: c * T + (tc + 1) * 128],
                            wv_sb[:, c * 1024 + half * 512: c * 1024 + half * 512 + 512],
                            start=(c == 0), stop=False))
                    p.emit("t", lambda t, half=half, vb=vb: t.matmul(
                        PS[vb][:, :], oscl[:, :],
                        bbv[:, half * 512: half * 512 + 512], start=False, stop=True))
                    vmm = p.sig("t")

                    eb = sc_a if (half == 0) else sc_c
                    p.wait("s", kmm)
                    p.wait("s", exp_war[half])
                    p.emit("s", lambda s, eb=eb, kb=kb: s.activation(
                        eb[:, :], PS[kb][:, :], AF.Exp))
                    te = p.sig("s")
                    p.wait("s", vmm)
                    if tc >= 2:
                        p.wait("s", kvmm_tok[tc - 2])
                    p.emit("s", lambda s, sl=sl, half=half, vb=vb: s.copy(
                        out=vtm[sl][:, half * 512: half * 512 + 512],
                        in_=PS[vb][:, :]))
                    vtoks[idx] = p.sig("s")
                    bank_free[vb] = vtoks[idx]
                    p.wait("v", te)
                    if tc >= 2:
                        p.wait("v", kvmm_tok[tc - 2])
                    p.emit("v", lambda v, eb=eb: v.tensor_scalar(
                        out=eb[:, :], in0=eb[:, :], scalar1=1.0, scalar2=None,
                        op0=ALU.min))
                    p.wait_force("v", p.sig("v"))
                    p.emit("v", lambda v, sl=sl, half=half, eb=eb, kb=kb: v.scalar_tensor_tensor(
                        out=ktm[sl][:, half * 512: half * 512 + 512],
                        in0=PS[kb][:, :], scalar=0.0, in1=eb[:, :],
                        op0=ALU.max, op1=ALU.add))
                    ktoks[idx] = p.sig("v")
                    exp_war[half] = ktoks[idx]
                    bank_free[kb] = ktoks[idx]

            LA = LNKV_LA if LNKV_LA > 0 else n_tc
            emit_p1_stats(0)
            for i in range(n_tc + LA + 1):
                if i + 1 < n_tc:
                    emit_p1_stats(i + 1)
                if i < n_tc:
                    emit_p1_norm(i)
                if LA <= i < n_tc + LA:
                    emit_p2(i - LA)
                if i >= LA + 1:
                    emit_kv(i - LA - 1)
                if i < n_tc:
                    emit_p1_tp(i)

            last_t = p.tok("t")
            p.emit("g", lambda g: g.memset(kv2[:, :], 0.0))
            p.emit("g", lambda g: g.memset(k16[:, :], 0.0))
            gz = p.sig("g", sem="gc")
            p.wait("v", last_t)
            p.wait("v", gz)
            p.wait("v", tok_maskb)
            for pair in range(NPAIR):
                bank = 2 + pair // 4
                qq = pair % 4
                p.emit("v", lambda v, pair=pair, bank=bank, qq=qq: v.tensor_copy(
                    out=kv2[0:64, pair * 128: pair * 128 + 64],
                    in_=PS[bank][0:64, qq * 128: qq * 128 + 64]))
                p.emit("v", lambda v, pair=pair, bank=bank, qq=qq: v.tensor_copy(
                    out=kv2[64:128, pair * 128 + 64: pair * 128 + 128],
                    in_=PS[bank][64:128, qq * 128 + 64: qq * 128 + 128]))
            for c in range(ND):
                p.emit("v", lambda v, c=c: v.tensor_scalar(
                    out=k16[:, c * 128 + 2 * c: c * 128 + 2 * c + 1],
                    in0=PS[4][:, c:c + 1], scalar1=maskb[:, 0:1], scalar2=None,
                    op0=ALU.mult))
                p.emit("v", lambda v, c=c: v.tensor_scalar(
                    out=k16[:, c * 128 + 2 * c + 1: c * 128 + 2 * c + 2],
                    in0=PS[4][:, c:c + 1], scalar1=maskb[:, 1:2], scalar2=None,
                    op0=ALU.mult))
            ev = p.sig("v")
            for i in (2, 3, 4):
                bank_free[i] = ev
            return ev

        def stage_qattn(b, wq_tok, wo_tok, kv_tok, acc_ready):
            exp_war = {0: None, 1: None}
            qelu = {}
            num_done = [None] * n_tiles
            wo_done = [None] * n_tiles
            rb_done = [None] * n_tiles
            rntoks = [None] * n_tiles
            apre_sl_war = {0: None, 1: None}

            def emit_qproj(ti, dcs):
                qf = q_fm2[ti % 2]
                for dc in dcs:
                    qb = dc % 2
                    p.wait("t", wq_tok)
                    p.wait("t", [late["bqf"], late["onesr"]])
                    p.wait("t", bank_free[qb])
                    for c in range(ND):
                        p.emit("t", lambda t, c=c, dc=dc, ti=ti, qb=qb: t.matmul(
                            PS[qb][:, :],
                            wq_sb[:, c * 1024 + dc * 128: c * 1024 + (dc + 1) * 128],
                            xn[:, c * T + ti * 512: c * T + (ti + 1) * 512],
                            start=(c == 0), stop=False))
                    p.emit("t", lambda t, dc=dc, qb=qb: t.matmul(
                        PS[qb][:, :], bqf[:, dc * 128:(dc + 1) * 128],
                        onesr[:, :], start=False, stop=True))
                    qmm = p.sig("t")
                    eb = sc_a if (dc % 2 == 0) else sc_c
                    p.wait("s", qmm)
                    p.wait("s", exp_war[dc % 2])
                    p.emit("s", lambda s, qb=qb, eb=eb: s.activation(
                        eb[:, :], PS[qb][:, :], AF.Exp))
                    te = p.sig("s")
                    p.wait("v", te)
                    if ti >= 2:
                        p.wait("v", num_done[ti - 2])
                    p.emit("v", lambda v, eb=eb: v.tensor_scalar(
                        out=eb[:, :], in0=eb[:, :], scalar1=1.0, scalar2=None,
                        op0=ALU.min))
                    p.wait_force("v", p.sig("v"))
                    p.emit("v", lambda v, dc=dc, qb=qb, eb=eb, qf=qf: v.scalar_tensor_tensor(
                        out=qf[:, dc * 512:(dc + 1) * 512],
                        in0=PS[qb][:, :], scalar=0.0, in1=eb[:, :],
                        op0=ALU.max, op1=ALU.add))
                    qelu[(ti, dc)] = p.sig("v")
                    exp_war[dc % 2] = qelu[(ti, dc)]
                    bank_free[qb] = qelu[(ti, dc)]

            def emit_nmm(ti):
                qf = q_fm2[ti % 2]
                p.wait("t", kv_tok)
                p.wait("t", bank_free[4])
                for pair in range(NPAIR):
                    p.wait("t", qelu[(ti, pair)])
                    p.emit("t", lambda t, pair=pair, qf=qf: t.matmul(
                        PS[4][:, :], k16[:, pair * 128:(pair + 1) * 128],
                        qf[:, pair * 512:(pair + 1) * 512],
                        start=(pair == 0), stop=(pair == NPAIR - 1)))
                nmm = p.sig("t")
                p.wait("v", nmm)
                p.wait("v", init_tok)
                if ti >= 1:
                    p.wait("v", rb_done[ti - 1])
                p.emit("v", lambda v: v.tensor_scalar(
                    out=sc_b[0:16, :], in0=PS[4][0:16, :], scalar1=EPS_ATTN,
                    scalar2=None, op0=ALU.max))
                p.wait_force("v", p.sig("v"))
                p.emit("v", lambda v: v.reciprocal(out=sc_b[0:16, :], in_=sc_b[0:16, :]))
                p.wait_force("v", p.sig("v"))
                p.emit("v", lambda v: v.tensor_copy(out=rn_sb[0:16, :], in_=sc_b[0:16, :]))
                rntoks[ti] = p.sig("v")
                bank_free[4] = rntoks[ti]

            apre_toks = {}

            def emit_tail_a(ti):
                qf = q_fm2[ti % 2]
                ap = apre2[ti % 2]
                apre_tok = apre_toks[ti] = [None] * NPAIR
                for pair in range(NPAIR):
                    nb_ = (2, 3, 4)[pair % 3]
                    rbb = 5 + pair % 2
                    p.wait("t", bank_free[nb_])
                    p.emit("t", lambda t, pair=pair, nb_=nb_, qf=qf: t.matmul(
                        PS[nb_][:, :], kv2[:, pair * 128:(pair + 1) * 128],
                        qf[:, pair * 512:(pair + 1) * 512], start=True, stop=True))
                    num_t = p.sig("t")
                    if pair == NPAIR - 1:
                        num_done[ti] = num_t
                    p.wait("t", rntoks[ti])
                    p.wait("t", bank_free[rbb])
                    p.emit("t", lambda t, pair=pair, rbb=rbb: t.matmul(
                        PS[rbb][:, :], e16b[:, pair * 128:(pair + 1) * 128],
                        rn_sb[:, :], start=True, stop=True))
                    rb_t = p.sig("t")
                    rb_done[ti] = rb_t
                    sl = pair % 2
                    p.wait("s", rb_t)
                    p.wait("s", apre_sl_war[sl])
                    p.emit("s", lambda s, sl=sl, rbb=rbb: s.copy(
                        out=rb_sb[:, sl * 512:(sl + 1) * 512], in_=PS[rbb][:, :]))
                    bank_free[rbb] = p.sig("s")
                    p.wait("v", bank_free[rbb])
                    p.wait("v", num_t)
                    if ti >= 2:
                        p.wait("v", wo_done[ti - 2])
                    p.emit("v", lambda v, pair=pair, sl=sl, nb_=nb_, ap=ap: v.tensor_tensor(
                        out=ap[:, pair * 512:(pair + 1) * 512], in0=PS[nb_][:, :],
                        in1=rb_sb[:, sl * 512:(sl + 1) * 512], op=ALU.mult))
                    apre_tok[pair] = p.sig("v")
                    apre_sl_war[sl] = apre_tok[pair]
                    bank_free[nb_] = apre_tok[pair]

            def emit_tail_b(ti):
                ap = apre2[ti % 2]
                apre_tok = apre_toks[ti]
                for sl4 in range(4):
                    for dh in range(2):
                        wb_ = 7 if (sl4 * 2 + dh) % 2 == 0 else 4
                        p.wait("t", wo_tok)
                        p.wait("t", bank_free[wb_])
                        for c in range(ND):
                            p.wait("t", apre_tok[c])
                            p.emit("t", lambda t, c=c, sl4=sl4, dh=dh, wb_=wb_, ap=ap: t.matmul(
                                PS[wb_][:, :],
                                ap[:, c * 512 + sl4 * 128: c * 512 + (sl4 + 1) * 128],
                                wo_sb[:, c * 1024 + dh * 512: c * 1024 + dh * 512 + 512],
                                start=(c == 0), stop=False))
                        p.emit("t", lambda t, dh=dh, wb_=wb_: t.matmul(
                            PS[wb_][:, :], oscl[:, :],
                            bbo[:, dh * 512: dh * 512 + 512], start=False, stop=True))
                        womm = p.sig("t")
                        tcg = ti * 4 + sl4
                        p.wait("v", womm)
                        p.wait("v", acc_ready[tcg])
                        p.emit("v", lambda v, tcg=tcg, dh=dh, wb_=wb_: v.tensor_tensor(
                            out=acc[:, tcg * 1024 + dh * 512: tcg * 1024 + dh * 512 + 512],
                            in0=PS[wb_][:, :],
                            in1=acc[:, tcg * 1024 + dh * 512: tcg * 1024 + dh * 512 + 512],
                            op=ALU.add))
                        bank_free[wb_] = p.sig("v")
                wo_done[ti] = p.tok("t")
                for k4 in range(4):
                    gates_mm(ti * 4 + k4)

            emit_qproj(0, range(ND))
            for ti in range(n_tiles):
                emit_nmm(ti)
                if ti + 1 < n_tiles:
                    emit_qproj(ti + 1, range(0, 5))
                emit_tail_a(ti)
                if ti + 1 < n_tiles:
                    emit_qproj(ti + 1, range(5, 8))
                emit_tail_b(ti)
            return p.tok("v")

        def gates_mm(tc):
            gb_ = 5 + tc % 2
            p.wait("t", [late["gatesb"], late["bgeb"], tok_oscl])
            p.wait("t", bank_free[gb_])
            for c in range(ND):
                p.emit("t", lambda t, c=c, tc=tc, gb_=gb_: t.matmul(
                    PS[gb_][:, 0:10],
                    xn[:, c * T + tc * 128: c * T + (tc + 1) * 128],
                    gatesb[:, c * 10:(c + 1) * 10],
                    start=(c == 0), stop=False))
            p.emit("t", lambda t, gb_=gb_: t.matmul(
                PS[gb_][:, 0:10], oscl[:, :], bgeb[:, :], start=False, stop=True))
            gmm = p.sig("t")
            p.wait("s", gmm)
            p.emit("s", lambda s, tc=tc, gb_=gb_: s.copy(
                out=glog[:, tc * 10:(tc + 1) * 10], in_=PS[gb_][:, 0:10]))
            glog_tok[tc] = bank_free[gb_] = p.sig("s")

        def gates_steps():
            for pr in range(n_tc // 2):
                yield from gates_pair_steps(2 * pr, 2 * pr + 1)

        def gates_pair_steps(tc_a, tc_b):
            Gs = [gsc[:, 0:64], gsc[:, 64:128]]

            def phase_a(G, tc):
                return [
                    lambda v: v.tensor_copy(out=G[:, 0:10], in_=glog[:, tc * 10:(tc + 1) * 10]),
                    lambda v: v.tensor_tensor(out=G[:, 10:11], in0=G[:, 0:1],
                                              in1=G[:, 1:2], op=ALU.subtract),
                    lambda v: v.tensor_scalar(out=G[:, 11:12], in0=G[:, 10:11],
                                              scalar1=0.0, scalar2=None, op0=ALU.is_lt),
                    lambda v: v.tensor_scalar(out=G[:, 12:13], in0=G[:, 11:12],
                                              scalar1=BIG, scalar2=None, op0=ALU.mult),
                    lambda v: v.tensor_scalar(out=G[:, 13:14], in0=G[:, 11:12],
                                              scalar1=-BIG, scalar2=BIG, op0=ALU.mult,
                                              op1=ALU.add),
                    lambda v: v.tensor_scalar(out=G[:, 24:28], in0=G[:, 2:6],
                                              scalar1=G[:, 12:13], scalar2=None,
                                              op0=ALU.subtract),
                    lambda v: v.tensor_scalar(out=G[:, 28:32], in0=G[:, 6:10],
                                              scalar1=G[:, 13:14], scalar2=None,
                                              op0=ALU.subtract),
                    lambda v: v.tensor_reduce(out=G[:, 17:18], in_=G[:, 24:32],
                                              axis=AX.X, op=ALU.max, negate=True),
                    lambda v: v.tensor_scalar(out=G[:, 14:15], in0=G[:, 10:11],
                                              scalar1=-1.0, scalar2=G[:, 10:11],
                                              op0=ALU.mult, op1=ALU.max),
                ]

            def phase_b(G, tc):
                return [
                    lambda v: v.tensor_reduce(out=G[:, 18:19], in_=G[:, 32:40],
                                              axis=AX.X, op=ALU.add),
                    lambda v: v.tensor_scalar(out=G[:, 16:17], in0=G[:, 15:16],
                                              scalar1=1.0, scalar2=None, op0=ALU.add),
                    lambda v: v.reciprocal(out=G[:, 19:20], in_=G[:, 18:19]),
                    lambda v: v.reciprocal(out=G[:, 16:17], in_=G[:, 16:17]),
                    lambda v: v.tensor_scalar(out=G[:, 40:48], in0=G[:, 32:40],
                                              scalar1=G[:, 19:20], scalar2=G[:, 16:17],
                                              op0=ALU.mult, op1=ALU.mult),
                    lambda v: v.tensor_reduce(out=G[:, 20:21], in_=G[:, 40:48],
                                              axis=AX.X, op=ALU.max),
                    lambda v: v.tensor_scalar(out=G[:, 48:56], in0=G[:, 40:48],
                                              scalar1=G[:, 20:21], scalar2=None,
                                              op0=ALU.is_ge),
                    lambda v: v.tensor_scalar(out=G[:, 48:56], in0=G[:, 48:56],
                                              scalar1=-1.0, scalar2=1.0, op0=ALU.mult,
                                              op1=ALU.add),
                    lambda v: v.tensor_tensor(out=G[:, 56:64], in0=G[:, 48:56],
                                              in1=G[:, 40:48], op=ALU.mult),
                    lambda v: v.tensor_reduce(out=G[:, 21:22], in_=G[:, 56:64],
                                              axis=AX.X, op=ALU.max),
                    lambda v: v.scalar_tensor_tensor(
                        out=wse[:, tc * NE:(tc + 1) * NE], in0=G[:, 40:48],
                        scalar=G[:, 21:22], in1=G[:, 40:48], op0=ALU.is_ge,
                        op1=ALU.mult),
                ]

            chains_a = [phase_a(Gs[0], tc_a), phase_a(Gs[1], tc_b)]
            first = [True]

            def astep(op):
                if first[0]:
                    p.wait("v", glog_tok[tc_a])
                    p.wait("v", glog_tok[tc_b])
                    first[0] = False
                p.emit("v", op)
                p.sig("v")

            for i in range(len(chains_a[0])):
                for idx in range(2):
                    yield lambda op=chains_a[idx][i]: astep(op)

            se_box = [None]

            def sstep():
                p.wait("s", p.tok("v"))
                for G in Gs:
                    p.emit("s", lambda s, G=G: s.activation(
                        G[:, 32:40], G[:, 24:32], AF.Exp, bias=G[:, 17:18], scale=1.0))
                    p.emit("s", lambda s, G=G: s.activation(
                        G[:, 15:16], G[:, 14:15], AF.Exp, scale=-1.0))
                se_box[0] = p.sig("s")

            yield sstep

            chains_b = [phase_b(Gs[0], tc_a), phase_b(Gs[1], tc_b)]
            first_b = [True]

            def bstep(op):
                if first_b[0]:
                    p.wait("v", se_box[0])
                    first_b[0] = False
                p.emit("v", op)
                p.sig("v")

            for i in range(len(chains_b[0])):
                for idx in range(2):
                    yield lambda op=chains_b[idx][i]: bstep(op)

        def swiglu_load(srcR_h, jc):
            sa = (jc % 2) * 2
            p.wait("g", rel["wbuf"][jc % 2])
            la = p.dma("g", lambda g, sa=sa, srcR_h=srcR_h, jc=jc: g.dma_start(
                wb_b[:, sa * 1024:(sa + 2) * 1024], srcR_h[jc]))
            return la, la

        def swiglu_prefetch(srcT_h):
            return {jc: swiglu_load(srcT_h, jc) for jc in (0, 1)}

        def swiglu_stage(b, srcT_h, dst_sb, pre=None, cb=None, ti_toks=None):
            sc_tt_tok = dict(rel["sc_tt"])
            for jc in range(NJC):
                sa = (jc % 2) * 2
                a_v = wb_b[:, sa * 1024:(sa + 1) * 1024]
                b_v = wb_b[:, (sa + 1) * 1024:(sa + 2) * 1024]
                if pre is not None and jc in pre:
                    la, lb = pre[jc]
                else:
                    la, lb = swiglu_load(srcT_h, jc)

                for ti in range(n_tiles):
                    gidx = ti // 2
                    par = ti % 2
                    ab = (gidx % 2) * 4 + par
                    bb = (gidx % 2) * 4 + 2 + par
                    p.wait("t", la)
                    p.wait("t", lb)
                    p.wait("t", bank_free[ab])
                    for c in range(ND):
                        p.emit("t", lambda t, c=c, ti=ti, a_v=a_v, ab=ab: t.matmul(
                            PS[ab][:, :], a_v[:, c * 128:(c + 1) * 128],
                            xn[:, c * T + ti * 512: c * T + (ti + 1) * 512],
                            start=(c == 0), stop=(c == ND - 1)))
                    amm = p.sig("t")
                    p.wait("t", bank_free[bb])
                    for c in range(ND):
                        p.emit("t", lambda t, c=c, ti=ti, b_v=b_v, bb=bb: t.matmul(
                            PS[bb][:, :], b_v[:, c * 128:(c + 1) * 128],
                            xn[:, c * T + ti * 512: c * T + (ti + 1) * 512],
                            start=(c == 0), stop=(c == ND - 1)))
                    bmm = p.sig("t")
                    sc2 = sc_a if ti % 2 == 0 else sc_b
                    p.wait("s", amm)
                    p.wait("s", sc_tt_tok[ti % 2])
                    p.emit("s", lambda s, ab=ab, sc2=sc2: s.activation(
                        sc2[:, :], PS[ab][:, :], AF.Silu))
                    st_ = p.sig("s")
                    bank_free[ab] = st_
                    p.wait("v", st_)
                    p.wait("v", bmm)
                    p.emit("v", lambda v, jc=jc, ti=ti, bb=bb, sc2=sc2, dst_sb=dst_sb: v.tensor_tensor(
                        out=dst_sb[:, jc * T + ti * 512: jc * T + (ti + 1) * 512],
                        in0=sc2[:, :], in1=PS[bb][:, :], op=ALU.mult))
                    sc_tt_tok[ti % 2] = p.sig("v")
                    bank_free[bb] = sc_tt_tok[ti % 2]
                    if ti_toks is not None and jc == NJC - 1:
                        ti_toks[ti] = sc_tt_tok[ti % 2]
                    if cb is not None:
                        cb(jc, ti)
                rel["wbuf"][jc % 2] = p.tok("t")
            rel["sc_tt"] = dict(sc_tt_tok)
            return p.tok("v")

        def stage_wout(b, swg_tok, swg_ti=None, after_first_loads=None):
            load_tok = [None] * 8
            comp_tok = [None] * 8

            def emit_load(q8):
                sl = q8 % 2
                p.wait("g", comp_tok[q8 - 2] if q8 >= 2 else rel["woutbuf"][sl])
                load_tok[q8] = p.dma("g", lambda g, q8=q8, sl=sl: g.dma_start(
                    wob[sl][:, :], woutR_h[q8]))

            emit_load(0)
            emit_load(1)
            if after_first_loads is not None:
                after_first_loads()
            for q8 in range(8):
                sl = q8 % 2
                for tc in range(n_tc):
                    wb_ = tc % 2
                    p.wait("t", load_tok[q8])
                    p.wait("t", swg_ti[tc // 4] if swg_ti is not None else swg_tok)
                    p.wait("t", bank_free[wb_])
                    for jc in range(NJC):
                        p.emit("t", lambda t, jc=jc, tc=tc, sl=sl, wb_=wb_: t.matmul(
                            PS[wb_][:, 0:128],
                            swg[:, jc * T + tc * 128: jc * T + (tc + 1) * 128],
                            wob[sl][:, jc * 128:(jc + 1) * 128],
                            start=(jc == 0), stop=(jc == NJC - 1)))
                    wmm = p.sig("t")
                    p.wait("v", wmm)
                    p.emit("v", lambda v, tc=tc, q8=q8, wb_=wb_: v.tensor_tensor(
                        out=acc[:, tc * 1024 + q8 * 128: tc * 1024 + (q8 + 1) * 128],
                        in0=PS[wb_][:, 0:128],
                        in1=acc[:, tc * 1024 + q8 * 128: tc * 1024 + (q8 + 1) * 128],
                        op=ALU.add))
                    bank_free[wb_] = p.sig("v")
                comp_tok[q8] = p.tok("t")
                rel["woutbuf"][sl] = comp_tok[q8]
                if q8 + 2 < 8:
                    emit_load(q8 + 2)
            return p.tok("v")

        def stage_experts(b, h_tok, wse_tok, xn_rel_tok):
            order = [(e, dh) for e in range(NE) for dh in range(2)]
            load_tok = {}
            comp_tok = {}
            out_toks = []

            def emit_load(i):
                e, dh = order[i]
                buf = w2b[i % 2]
                if i == 0:
                    p.wait("g", rel["woutbuf"])
                    tA = p.dma("g", lambda g: g.dma_start(
                        wo8_b[:, 0:4096], w2R_h[0, 0][:, 0:4096]))
                    p.wait("g", xn_rel_tok)
                    tB = p.dma("g", lambda g, buf=buf: g.dma_start(
                        buf[:, 4096:8192], w2R_h[0, 0][:, 4096:8192]))
                    load_tok[0] = [tA, tB]
                    return
                p.wait("g", comp_tok[i - 2] if i >= 2 else xn_rel_tok)
                load_tok[i] = p.dma("g", lambda g, e=e, dh=dh, buf=buf: g.dma_start(
                    buf[:, :], w2R_h[e, dh]))

            emit_load(0)
            emit_load(1)
            for i, (e, dh) in enumerate(order):
                buf = w2b[i % 2]
                for tc in range(n_tc):
                    wb_ = tc % 2
                    p.wait("t", load_tok[i])
                    p.wait("t", h_tok)
                    p.wait("t", bank_free[wb_])
                    for jc in range(NJC):
                        rv = wo8_b if (i == 0 and jc < 8) else buf
                        p.emit("t", lambda t, jc=jc, tc=tc, rv=rv, wb_=wb_: t.matmul(
                            PS[wb_][:, :],
                            h_sb[:, jc * T + tc * 128: jc * T + (tc + 1) * 128],
                            rv[:, jc * 512:(jc + 1) * 512],
                            start=(jc == 0), stop=(jc == NJC - 1)))
                    emm = p.sig("t")
                    p.wait("v", emm)
                    p.wait("v", wse_tok)
                    p.emit("v", lambda v, tc=tc, dh=dh, e=e, wb_=wb_: v.scalar_tensor_tensor(
                        out=acc[:, tc * 1024 + dh * 512: tc * 1024 + dh * 512 + 512],
                        in0=PS[wb_][:, :], scalar=wse[:, tc * NE + e: tc * NE + e + 1],
                        in1=acc[:, tc * 1024 + dh * 512: tc * 1024 + dh * 512 + 512],
                        op0=ALU.mult, op1=ALU.add))
                    stok = p.sig("v")
                    bank_free[wb_] = stok
                    if e == NE - 1 and dh == 1:
                        p.wait("y", stok)
                        out_toks.append(p.dma("y", lambda y, b=b, tc=tc: y.dma_start(
                            out_h[b, tc * 128:(tc + 1) * 128, :],
                            acc[:, tc * 1024:(tc + 1) * 1024])))
                comp_tok[i] = p.tok("t")
                if i + 2 < len(order):
                    emit_load(i + 2)
            return out_toks

        def emit_tap(name):
            if name in tap_out:
                ncols = tap_out[name].shape[-1]
                src = {"xn": xn, "kv2": kv2, "k16": k16, "acc": acc, "wse": wse,
                       "swg": swg, "h": h_sb, "qfm": q_fm, "apre": apre,
                       "rn": rn_sb, "gsc": gsc}[name]
                p.barrier(("t", "v", "s"))
                tv = p.tok("v")
                p.wait("g", tv)
                gt = p.dma("g", lambda g, src=src, name=name, ncols=ncols: g.dma_start(
                    tap_out[name][:, :], src[:, 0:ncols]))
                p.wait("v", gt)
                p.emit("v", lambda v: v.nop())

        for b in range(NB):
            acc_ready = [None] * n_tc
            wk_tok = load_w_chunked(wk_sb, wk_h, rel["R0"])
            wv_tok = load_w_chunked(wv_sb, wv_h, rel["R1"])
            if b == 0:
                late.update(emit_late_consts())
            wq_tok = load_w_chunked(wq_sb, wq_h, rel["R3"])

            kv_tok = stage_lnkv(b, acc_ready, wk_tok, wv_tok)
            p.barrier()
            if b == 0:
                emit_tap("xn")
            rel["R0"] = p.tok("t")
            if b == 0:
                emit_tap("kv2")
                emit_tap("k16")

            wo_tok = load_w_chunked(wo_sb, wo_h, rel["R0"])
            pre_in = swiglu_prefetch(winR_h)
            stage_qattn(b, wq_tok, wo_tok, kv_tok, acc_ready)
            p.barrier()
            rel["R2"] = p.tok("t")
            rel["R3"] = p.tok("t")
            if b == 0:
                emit_tap("acc")
                emit_tap("qfm")
                emit_tap("apre")

            gates_iter = gates_steps()

            def gdrain(jc, ti, n=6):
                if jc < 1:
                    return
                for _ in range(n):
                    st = next(gates_iter, None)
                    if st is None:
                        return
                    st()

            swg_ti = {}
            swg_tok = swiglu_stage(b, winR_h, swg, pre_in, cb=gdrain, ti_toks=swg_ti)
            for st in gates_iter:
                st()
            gate_tok = p.tok("v")
            if b == 0:
                emit_tap("swg")
                emit_tap("wse")
                emit_tap("gsc")
            pre_h = {}
            stage_wout(b, swg_tok, swg_ti,
                       after_first_loads=lambda: pre_h.update(swiglu_prefetch(w1R_h)))
            p.barrier()

            h_tok = swiglu_stage(b, w1R_h, h_sb, pre_h)
            p.barrier()
            xn_rel = p.tok("t")
            if b == 0:
                emit_tap("h")

            out_toks = stage_experts(b, h_tok, gate_tok, xn_rel)
            p.barrier(("t", "v", "s", "y"))
            rel["acc"] = p.queue_done("y")
            rel["R0"] = rel["R1"] = rel["R2"] = rel["R3"] = p.tok("t")
            rel["xarena"] = p.tok("t")
            rel["wbuf"] = [p.tok("t"), p.tok("t")]
            rel["woutbuf"] = [p.tok("t"), p.tok("t")]

        p.wait_force("y", p.queue_done("y"))
        p.emit("y", lambda y: y.nop())
        p.wait_force("g", p.queue_done("g"))
        p.emit("g", lambda g: g.nop())

        p.check_deadlock()
        n_ins = {e: len(p.items[e]) for e in ENGINES}

        def runner(name):
            def run(eng):
                for waits, fn, incs in p.items[name]:
                    for s_, v_ in waits.items():
                        eng.wait_ge(sems[s_], v_)
                    ins = fn(eng)
                    for sem_n, amt in incs:
                        ins.then_inc(sems[sem_n], amt)
            return run

        blk.tensor(runner("t"))
        blk.vector(runner("v"))
        blk.scalar(runner("s"))
        blk.gpsimd(runner("g"))
        blk.sync(runner("y"))

    return nc, n_ins


def prep_inputs(inputs, n_tc=16, n_cores=8):
    T = n_tc * 128
    f32c = lambda a: np.ascontiguousarray(np.asarray(a), dtype=np.float32)
    x = f32c(inputs["L0_attn_inputs_q"])[:, :T, :]

    def projR(w):
        return f32c(np.asarray(w, dtype=np.float32)
                    .reshape(ND, 128, D).transpose(1, 0, 2).reshape(128, ND * D))

    def swigluR(w):
        wT = np.asarray(w, dtype=np.float32).T
        B = wT.reshape(ND, 128, 2 * HID).transpose(1, 0, 2)
        out = np.empty((NJC, 128, 2048), dtype=np.float32)
        for jc in range(NJC):
            a = B[:, :, jc * 128:(jc + 1) * 128].reshape(128, 1024)
            bb = B[:, :, HID + jc * 128: HID + (jc + 1) * 128].reshape(128, 1024)
            out[jc] = np.concatenate([a, bb], axis=1)
        return f32c(out)

    winR = swigluR(inputs["L0_ffn_w_in"])
    w1R = swigluR(inputs["L0_ffn_w1"])
    woutT = np.asarray(inputs["L0_ffn_w_out"], dtype=np.float32).T
    C = woutT.reshape(NJC, 128, D).transpose(1, 0, 2)
    woutR = f32c(np.stack([C[:, :, q8 * 128:(q8 + 1) * 128].reshape(128, 2048)
                           for q8 in range(8)]))
    w2T = np.asarray(inputs["L0_ffn_w2"], dtype=np.float32).transpose(0, 2, 1)
    A = w2T.reshape(NE, NJC, 128, D).transpose(0, 2, 1, 3)
    w2R = f32c(np.stack([A[:, :, :, 0:512].reshape(NE, 128, NJC * 512),
                         A[:, :, :, 512:1024].reshape(NE, 128, NJC * 512)], axis=1))
    gatesT = f32c(np.concatenate([np.asarray(inputs["L0_ffn_wg"]).T,
                                  np.asarray(inputs["L0_ffn_we"]).T], axis=1))
    bk = f32c(np.asarray(inputs["L0_attn_bk"]).reshape(1, D))
    bv = f32c(np.asarray(inputs["L0_attn_bv"]).reshape(1, D))
    bo = f32c(np.asarray(inputs["L0_attn_bo"]).reshape(1, D))
    bq = f32c(np.asarray(inputs["L0_attn_bq"]).reshape(1, D))
    geb = f32c(np.concatenate([np.asarray(inputs["L0_ffn_gb"]),
                               np.asarray(inputs["L0_ffn_eb"])]).reshape(1, 10))
    ident = np.eye(128, dtype=np.float32)
    e16 = np.zeros((NPAIR, 128, 128), dtype=np.float32)
    for pr in range(NPAIR):
        e16[pr, 2 * pr, 0:64] = 1.0
        e16[pr, 2 * pr + 1, 64:128] = 1.0
    maskb = np.zeros((128, 2), dtype=np.float32)
    maskb[0:64, 0] = 1.0
    maskb[64:128, 1] = 1.0

    shared = dict(wq=projR(inputs["L0_attn_Wq"]), wk=projR(inputs["L0_attn_Wk"]),
                  wv=projR(inputs["L0_attn_Wv"]), wo=projR(inputs["L0_attn_Wo"]),
                  winR=winR, w1R=w1R, woutR=woutR, w2R=w2R, gatesT=gatesT,
                  bk=bk, bv=bv, bo=bo, bq=bq, geb=geb,
                  ident=ident, e16=e16, maskb=maskb)
    in_maps = []
    for c in range(n_cores):
        m = dict(shared)
        m["x"] = np.ascontiguousarray(x[c * NB:(c + 1) * NB])
        in_maps.append(m)
    return in_maps


def kernel(**inputs):
    n_tc = 16
    in_maps = prep_inputs(inputs, n_tc=n_tc, n_cores=8)
    nc, _ = build(n_tc=n_tc)
    res = run_bass_kernel_spmd(nc, in_maps, core_ids=list(range(8)))
    outs = [np.asarray(r["out"], dtype=np.float32) for r in res.results]
    return np.concatenate(outs, axis=0)


# version 20
# speedup vs baseline: 1.0949x; 1.0526x over previous
import contextlib
import numpy as np

import concourse.bass as bass
import concourse.mybir as mybir
from concourse.bass_utils import run_bass_kernel_spmd

F32 = mybir.dt.float32
BF16 = mybir.dt.bfloat16
U8 = mybir.dt.uint8
AF = mybir.ActivationFunctionType
ALU = mybir.AluOpType
AX = mybir.AxisListType

D = 1024
ND = 8
HID = 2048
NJC = 16
NH = 16
NPAIR = 8
NE = 8
NB = 2
EPS_LN = 1e-5
EPS_ATTN = 1e-6
BIG = 1e9
LNKV_LA = 2

ENGINES = ("t", "v", "s", "g", "y")
SYNC_SELF_OK = ("t", "v", "s")
DMA_POOL = {"g": 8, "y": 4}


class Prog:
    def __init__(self):
        self.items = {e: [] for e in ENGINES}
        self.count = {}
        self.pending = {e: {} for e in ENGINES}
        self.max_waited = {e: {} for e in ENGINES}
        self.dma_slot = {q: 0 for q in DMA_POOL}
        self.sem_names = set(ENGINES) | {"gc"}
        for q, n in DMA_POOL.items():
            for i in range(n):
                self.sem_names.add(f"{q}{i}")
        for s in self.sem_names:
            self.count[s] = 0

    def wait(self, eng, tok):
        if tok is None:
            return
        if isinstance(tok, list):
            for t in tok:
                self.wait(eng, t)
            return
        sem, val = tok
        if sem == eng and eng in SYNC_SELF_OK:
            return
        if val <= self.max_waited[eng].get(sem, 0):
            return
        if val <= self.pending[eng].get(sem, 0):
            return
        self.pending[eng][sem] = val

    def wait_force(self, eng, tok):
        if tok is None:
            return
        if isinstance(tok, list):
            for t in tok:
                self.wait_force(eng, t)
            return
        sem, val = tok
        if val <= self.max_waited[eng].get(sem, 0):
            return
        if val <= self.pending[eng].get(sem, 0):
            return
        self.pending[eng][sem] = val

    AUTO_CHAIN = ("v", "s")

    def emit(self, eng, fn):
        if eng in self.AUTO_CHAIN and self.items[eng]:
            if not self.items[eng][-1][2]:
                self.sig(eng)
            self.wait_force(eng, (eng, self.count[eng]))
        w = self.pending[eng]
        self.pending[eng] = {}
        for s, v in w.items():
            self.max_waited[eng][s] = max(self.max_waited[eng].get(s, 0), v)
        self.items[eng].append([w, fn, []])

    def sig(self, eng, amt=1, sem=None):
        sem = sem or eng
        assert self.items[eng], f"sig on empty engine {eng}"
        self.items[eng][-1][2].append((sem, amt))
        self.count[sem] += amt
        return (sem, self.count[sem])

    def dma(self, q, fn):
        slot = self.dma_slot[q]
        self.dma_slot[q] = (slot + 1) % DMA_POOL[q]
        sem = f"{q}{slot}"
        self.wait_force(q, (sem, self.count[sem]))
        self.emit(q, fn)
        return self.sig(q, 16, sem=sem)

    def queue_done(self, q):
        return [(f"{q}{i}", self.count[f"{q}{i}"]) for i in range(DMA_POOL[q])]

    def tok(self, eng):
        return (eng, self.count[eng])

    def sig_or_reuse(self, eng):
        if not self.items[eng]:
            return None
        incs = self.items[eng][-1][2]
        if incs:
            s, _ = incs[-1]
            return (s, self.count[s])
        return self.sig(eng)

    def barrier(self, engines=("t", "v", "s")):
        toks = []
        for e in engines:
            if e in DMA_POOL:
                toks.extend(self.queue_done(e))
            else:
                t = self.sig_or_reuse(e)
                if t is not None:
                    toks.append(t)
        for e in engines:
            for t in toks:
                self.wait(e, t)

    def check_deadlock(self):
        ptr = {e: 0 for e in ENGINES}
        val = {s: 0 for s in self.sem_names}
        progress = True
        while progress:
            progress = False
            for e in ENGINES:
                while ptr[e] < len(self.items[e]):
                    waits, _, incs = self.items[e][ptr[e]]
                    if all(val[s] >= v for s, v in waits.items()):
                        for s, amt in incs:
                            val[s] += amt
                        ptr[e] += 1
                        progress = True
                    else:
                        break
        stuck = {}
        for e in ENGINES:
            if ptr[e] < len(self.items[e]):
                waits = self.items[e][ptr[e]][0]
                stuck[e] = {"at": ptr[e], "of": len(self.items[e]), "waits": waits,
                            "have": {s: val[s] for s in waits}}
        if stuck:
            raise RuntimeError(f"deadlock in program build: {stuck}")


def build(n_tc=16, taps=()):
    T = n_tc * 128
    n_tiles = (n_tc + 3) // 4

    nc = bass.Bass("TRN2", target_bir_lowering=False)

    def din(name, shape):
        return nc.dram_tensor(name, shape, F32, kind="ExternalInput").ap()

    x_h = din("x", [NB, T, D])
    out_h = nc.dram_tensor("out", [NB, T, D], F32, kind="ExternalOutput").ap()
    wq_h = din("wq", [128, ND * 1024])
    wk_h = din("wk", [128, ND * 1024])
    wv_h = din("wv", [128, ND * 1024])
    wo_h = din("wo", [128, ND * 1024])
    winR_h = din("winR", [NJC, 128, 2048])
    w1R_h = din("w1R", [NJC, 128, 2048])
    woutR_h = din("woutR", [8, 128, 2048])
    w2R_h = din("w2R", [NE, 2, 128, NJC * 512])
    gatesT_h = din("gatesT", [D, 10])
    bk_h = din("bk", [1, D])
    bv_h = din("bv", [1, D])
    bo_h = din("bo", [1, D])
    bq_h = din("bq", [1, D])
    geb_h = din("geb", [1, 10])

    ident_h = din("ident", [128, 128])
    e16_h = din("e16", [NPAIR, 128, 128])
    maskb_h = din("maskb", [128, 2])

    tap_out = {}
    for name, shape in taps:
        tap_out[name] = nc.dram_tensor("tap_" + name, list(shape), F32,
                                       kind="ExternalOutput").ap()

    es = contextlib.ExitStack()
    with es:
        acc = es.enter_context(nc.sbuf_tensor("acc", [128, n_tc * 1024], F32))
        harena = es.enter_context(nc.sbuf_tensor("harena", [128, max(65536, NJC * T * 2)], U8))
        xarena = es.enter_context(nc.sbuf_tensor("xarena", [128, max(n_tc * 2048, 32768)], U8))
        wbuf = es.enter_context(nc.sbuf_tensor("wbuf", [128, 8192], U8))
        woutbuf = es.enter_context(nc.sbuf_tensor("woutbuf", [128, 8192], U8))
        sc_a = es.enter_context(nc.sbuf_tensor("sc_a", [128, 512], F32))
        sc_b = es.enter_context(nc.sbuf_tensor("sc_b", [128, 512], F32))
        sc_c = es.enter_context(nc.sbuf_tensor("sc_c", [128, 512], F32))

        kv2 = es.enter_context(nc.sbuf_tensor("kv2", [128, NPAIR * 128], BF16))
        k16 = es.enter_context(nc.sbuf_tensor("k16", [128, ND * 128], BF16))
        rn_sb = es.enter_context(nc.sbuf_tensor("rn_sb", [128, 512], BF16))
        rb_sb = es.enter_context(nc.sbuf_tensor("rb_sb", [128, 1024], BF16))
        wse = es.enter_context(nc.sbuf_tensor("wse", [128, n_tc * NE], F32))
        gsc = es.enter_context(nc.sbuf_tensor("gsc", [128, 128], F32))
        glog = es.enter_context(nc.sbuf_tensor("glog", [128, n_tc * 10], F32))
        stat = es.enter_context(nc.sbuf_tensor("stat", [128, 32], F32))
        ident32 = es.enter_context(nc.sbuf_tensor("ident32_s", [128, 128], F32))
        e16b = es.enter_context(nc.sbuf_tensor("e16b", [128, NPAIR * 128], BF16))
        bbk = es.enter_context(nc.sbuf_tensor("bbk_s", [128, D], BF16))
        bbv = es.enter_context(nc.sbuf_tensor("bbv_s", [128, D], BF16))
        bbo = es.enter_context(nc.sbuf_tensor("bbo_s", [128, D], BF16))
        bqf = es.enter_context(nc.sbuf_tensor("bqf_s", [128, D], BF16))
        bgeb = es.enter_context(nc.sbuf_tensor("bgeb_s", [128, 10], BF16))
        oscl = es.enter_context(nc.sbuf_tensor("oscl_s", [128, 128], BF16))
        onesr = es.enter_context(nc.sbuf_tensor("onesr_s", [128, 512], BF16))
        maskb = es.enter_context(nc.sbuf_tensor("maskb_s", [128, 2], F32))
        onesb = es.enter_context(nc.sbuf_tensor("onesb", [128, 1], BF16))
        czero = es.enter_context(nc.sbuf_tensor("czero_s", [128, 4], F32))
        gatesb = es.enter_context(nc.sbuf_tensor("gatesb", [128, ND * 10], BF16))

        ha_b = harena.bitcast(BF16)
        ha_f = harena.bitcast(F32)
        xa_b = xarena.bitcast(BF16)
        wb_b = wbuf.bitcast(BF16)
        wo8_b = woutbuf.bitcast(BF16)

        def bv(base, byte_off, cols):
            return base[:, byte_off // 2: byte_off // 2 + cols]

        def fv(base, byte_off, cols):
            return base[:, byte_off // 4: byte_off // 4 + cols]

        R0, R1, R2, R3 = 0, 16384, 32768, 49152
        wk_sb = bv(ha_b, R0, ND * 1024)
        wv_sb = bv(ha_b, R1, ND * 1024)
        wq_sb = bv(ha_b, R3, ND * 1024)
        wo_sb = bv(ha_b, R0, ND * 1024)
        q_fm2 = [bv(ha_b, R1, ND * 512), bv(ha_b, R1 + 8192, ND * 512)]
        apre2 = [bv(ha_b, R2, ND * 512), bv(ha_b, R2 + 8192, ND * 512)]
        q_fm = q_fm2[0]
        apre = apre2[0]
        ktm = [bv(ha_b, R2, 1024), bv(ha_b, R2 + 2048, 1024)]
        vtm = [bv(ha_b, R2 + 4096, 1024), bv(ha_b, R2 + 6144, 1024)]
        xnst = [fv(ha_f, R2 + 8192, 1024), fv(ha_f, R2 + 12288, 1024)]
        swg = ha_b[:, 0:NJC * T]
        h_sb = ha_b[:, 0:NJC * T]
        xn = xa_b[:, 0:n_tc * 1024]
        w2b = [bv(xa_b, 0, 16 * 512), bv(xa_b, 16384, 16 * 512)]
        wob = [bv(wo8_b, 0, 16 * 128), bv(wo8_b, 4096, 16 * 128)]

        PS = [es.enter_context(nc.psum_tensor(f"pb{i}", [128, 512], F32)) for i in range(8)]

        p = Prog()
        sems = {s: es.enter_context(nc.semaphore(f"sem_{s}")) for s in sorted(p.sem_names)}
        blk = es.enter_context(nc.Block())
        bank_free = [None] * 8
        rel = {"R0": None, "R1": None, "R2": None, "R3": None, "acc": None,
               "wbuf": [None, None], "woutbuf": [None, None], "xarena": None}
        late = {}
        glog_tok = [None] * n_tc
        rel["sc_tt"] = {0: None, 1: None}

        def ld(fn):
            return p.dma("g", fn)

        tok_ident = ld(lambda g: g.dma_start(ident32[:, :], ident_h[:, :]))
        p.emit("g", lambda g: g.memset(czero[:, 0:1], 0.0))
        p.emit("g", lambda g: g.memset(czero[:, 1:2], EPS_LN))
        p.emit("g", lambda g: g.memset(czero[:, 2:3], 1.0))
        p.emit("g", lambda g: g.memset(oscl[:, :], 0.0))
        p.emit("g", lambda g: g.memset(onesb[:, :], 1.0))
        p.emit("g", lambda g: g.memset(bbk[:, :], 0.0))
        p.emit("g", lambda g: g.memset(bbv[:, :], 0.0))
        p.emit("g", lambda g: g.memset(rn_sb[:, :], 0.0))
        p.wait_force("g", p.sig("g", sem="gc"))
        p.emit("g", lambda g: g.memset(oscl[0:1, :], 1.0))
        init_tok = p.sig("g", sem="gc")
        p.wait_force("g", init_tok)
        tok_czero = tok_oscl = tok_onesb = init_tok
        tok_bbk = ld(lambda g: g.dma_start(bbk[0:1, :], bk_h[:, :]))
        tok_bbv = ld(lambda g: g.dma_start(bbv[0:1, :], bv_h[:, :]))
        nc.const_aps.register(F32, 0.0, czero[:, 0:1])
        nc.const_aps.register(F32, EPS_LN, czero[:, 1:2])
        nc.const_aps.register(F32, 1.0, czero[:, 2:3])
        tok_maskb = p.dma("y", lambda y: y.dma_start(maskb[:, :], maskb_h[:, :]))

        def emit_late_consts():
            lc = {}
            p.emit("g", lambda g: g.memset(onesr[:, :], 0.0))
            p.emit("g", lambda g: g.memset(bqf[:, :], 0.0))
            p.emit("g", lambda g: g.memset(bbo[:, :], 0.0))
            p.emit("g", lambda g: g.memset(bgeb[:, :], 0.0))
            p.wait_force("g", p.sig("g", sem="gc"))
            p.emit("g", lambda g: g.memset(onesr[0:1, :], 1.0))
            lc["z"] = p.sig("g", sem="gc")
            p.wait_force("g", lc["z"])
            lc["e16"] = ld(lambda g: g.dma_start(
                e16b[:, :].rearrange("p (q m) -> p q m", q=NPAIR),
                e16_h.rearrange("q k m -> k q m")))
            lc["bqf"] = [ld(lambda g: g.dma_start(bqf[0:1, :], bq_h[:, :])), lc["z"]]
            lc["bbo"] = [ld(lambda g: g.dma_start(bbo[0:1, :], bo_h[:, :])), lc["z"]]
            lc["onesr"] = lc["z"]
            lc["bgeb"] = [ld(lambda g: g.dma_start(bgeb[0:1, :], geb_h[:, :])), lc["z"]]
            lc["gatesb"] = ld(lambda g: g.dma_start(
                gatesb[:, :].rearrange("p (c n) -> p c n", c=ND),
                gatesT_h.rearrange("(c k) n -> k c n", k=128)))
            return lc

        def load_w_chunked(dst_view, src_ap, rel_tok):
            p.wait("g", rel_tok)
            return p.dma("g", lambda g: g.dma_start(dst_view, src_ap))

        def stage_lnkv(b, acc_ready, wk_tok, wv_tok):
            ln_done = [None] * n_tc
            tp_tok = [None] * n_tc
            xn_ready = [None] * n_tc
            kbanks = (0, 7)
            vbanks = (1,)
            exp_war = {0: None, 1: None}
            kv_started = [False, False, False]
            kvmm_tok = [None] * n_tc
            ktoks = {}
            vtoks = {}

            def emit_kv(tc):
                sl = tc % 2
                for half in range(2):
                    kvbank = 2 + half
                    p.wait("t", ktoks[tc * 2 + half])
                    p.wait("t", vtoks[tc * 2 + half])
                    if tc == 0:
                        p.wait("t", bank_free[kvbank])
                        p.wait("t", bank_free[4])
                        p.wait("t", tok_onesb)
                    for qq in range(4):
                        st = not kv_started[half]
                        kv_started[half] = True
                        p.emit("t", lambda t, sl=sl, half=half, qq=qq, st=st, kvbank=kvbank: t.matmul(
                            PS[kvbank][:, qq * 128:(qq + 1) * 128],
                            ktm[sl][:, half * 512 + qq * 128: half * 512 + (qq + 1) * 128],
                            vtm[sl][:, half * 512 + qq * 128: half * 512 + (qq + 1) * 128],
                            start=st, stop=False, skip_group_check=True))
                    for blkc in range(4):
                        st = not kv_started[2]
                        kv_started[2] = True
                        col = half * 4 + blkc
                        p.emit("t", lambda t, sl=sl, half=half, blkc=blkc, st=st, col=col: t.matmul(
                            PS[4][:, col:col + 1],
                            ktm[sl][:, half * 512 + blkc * 128: half * 512 + (blkc + 1) * 128],
                            onesb[:, :],
                            start=st, stop=False, skip_group_check=True))
                kvmm_tok[tc] = p.sig("t")

            p1_rstd = [None] * n_tc

            def emit_p1_stats(tc):
                if b > 0 and tc < 4:
                    p.wait("y", rel["acc"])
                acc_ready[tc] = p.dma("y", lambda y, b=b, tc=tc: y.dma_start(
                    acc[:, tc * 1024:(tc + 1) * 1024], x_h[b, tc * 128:(tc + 1) * 128, :]))
                so = (tc % 2) * 16
                xch = acc[:, tc * 1024:(tc + 1) * 1024]
                p.wait("v", acc_ready[tc])
                if tc >= 2:
                    p.wait("v", ln_done[tc - 2])
                p.emit("v", lambda v, xch=xch, so=so: v.bn_stats(
                    stat[:, so:so + 6], xch[:, 0:512]))
                p.emit("v", lambda v, xch=xch, so=so: v.bn_stats(
                    stat[:, so + 6:so + 12], xch[:, 512:1024]))
                p.emit("v", lambda v, so=so: v.bn_aggr(
                    stat[:, so + 12:so + 14], stat[:, so:so + 12]))
                tv = p.sig("v")
                p.wait("s", tv)
                p.wait("s", tok_czero)
                if tc >= 2:
                    p.wait("s", ln_done[tc - 2])
                p.emit("s", lambda s, so=so: s.activation(
                    stat[:, so + 14:so + 15], stat[:, so + 13:so + 14], AF.Ln,
                    bias=czero[:, 1:2], scale=1.0))
                p.emit("s", lambda s, so=so: s.activation(
                    stat[:, so + 15:so + 16], stat[:, so + 14:so + 15], AF.Exp,
                    scale=-0.5))
                p1_rstd[tc] = p.sig("s")

            def emit_p1_norm(tc):
                so = (tc % 2) * 16
                xch = acc[:, tc * 1024:(tc + 1) * 1024]
                p.wait("v", p1_rstd[tc])
                if tc >= 2:
                    p.wait("v", tp_tok[tc - 2])
                sl = tc % 2
                p.emit("v", lambda v, sl=sl, so=so, xch=xch: v.tensor_scalar(
                    out=xnst[sl][:, :], in0=xch,
                    scalar1=stat[:, so + 12:so + 13], scalar2=stat[:, so + 15:so + 16],
                    op0=ALU.subtract, op1=ALU.mult))
                ln_done[tc] = p.sig("v")

            def emit_p1_tp(tc):
                sl = tc % 2
                p.wait("t", ln_done[tc])
                p.wait("t", bank_free[5])
                p.wait("t", bank_free[6])
                p.wait("t", tok_ident)
                for c in range(ND):
                    p.emit("t", lambda t, sl=sl, c=c: t.transpose(
                        PS[5 + c // 4][:, (c % 4) * 128:(c % 4 + 1) * 128],
                        xnst[sl][:, c * 128:(c + 1) * 128], ident32[:, :]))
                tp_tok[tc] = p.sig("t")
                p.wait("s", tp_tok[tc])
                p.emit("s", lambda s, tc=tc: s.copy(
                    out=xn[:, :].rearrange("p (c t) -> p c t", c=ND)[:, 0:4, tc * 128:(tc + 1) * 128],
                    in_=PS[5][:, 0:512].rearrange("p (c t) -> p c t", c=4)))
                bank_free[5] = p.sig("s")
                p.wait("v", tp_tok[tc])
                p.emit("v", lambda v, tc=tc: v.tensor_copy(
                    out=xn[:, :].rearrange("p (c t) -> p c t", c=ND)[:, 4:8, tc * 128:(tc + 1) * 128],
                    in_=PS[6][:, 0:512].rearrange("p (c t) -> p c t", c=4)))
                bank_free[6] = p.sig("v")
                xn_ready[tc] = [bank_free[5], bank_free[6]]

            def emit_p2(tc):
                sl = tc % 2
                for half in range(2):
                    idx = tc * 2 + half
                    kb = kbanks[idx % 2]
                    vb = vbanks[0]
                    p.wait("t", wk_tok)
                    p.wait("t", wv_tok)
                    p.wait("t", [tok_oscl, tok_bbk, tok_bbv])
                    p.wait("t", xn_ready[tc])
                    p.wait("t", bank_free[kb])
                    for c in range(ND):
                        p.emit("t", lambda t, c=c, tc=tc, half=half, kb=kb: t.matmul(
                            PS[kb][:, :],
                            xn[:, c * T + tc * 128: c * T + (tc + 1) * 128],
                            wk_sb[:, c * 1024 + half * 512: c * 1024 + half * 512 + 512],
                            start=(c == 0), stop=False))
                    p.emit("t", lambda t, half=half, kb=kb: t.matmul(
                        PS[kb][:, :], oscl[:, :],
                        bbk[:, half * 512: half * 512 + 512], start=False, stop=True))
                    kmm = p.sig("t")
                    p.wait("t", bank_free[vb])
                    for c in range(ND):
                        p.emit("t", lambda t, c=c, tc=tc, half=half, vb=vb: t.matmul(
                            PS[vb][:, :],
                            xn[:, c * T + tc * 128: c * T + (tc + 1) * 128],
                            wv_sb[:, c * 1024 + half * 512: c * 1024 + half * 512 + 512],
                            start=(c == 0), stop=False))
                    p.emit("t", lambda t, half=half, vb=vb: t.matmul(
                        PS[vb][:, :], oscl[:, :],
                        bbv[:, half * 512: half * 512 + 512], start=False, stop=True))
                    vmm = p.sig("t")

                    eb = sc_a if (half == 0) else sc_c
                    p.wait("s", kmm)
                    p.wait("s", exp_war[half])
                    p.emit("s", lambda s, eb=eb, kb=kb: s.activation(
                        eb[:, :], PS[kb][:, :], AF.Exp))
                    te = p.sig("s")
                    p.wait("s", vmm)
                    if tc >= 2:
                        p.wait("s", kvmm_tok[tc - 2])
                    p.emit("s", lambda s, sl=sl, half=half, vb=vb: s.copy(
                        out=vtm[sl][:, half * 512: half * 512 + 512],
                        in_=PS[vb][:, :]))
                    vtoks[idx] = p.sig("s")
                    bank_free[vb] = vtoks[idx]
                    p.wait("v", te)
                    if tc >= 2:
                        p.wait("v", kvmm_tok[tc - 2])
                    p.emit("v", lambda v, eb=eb: v.tensor_scalar(
                        out=eb[:, :], in0=eb[:, :], scalar1=1.0, scalar2=None,
                        op0=ALU.min))
                    p.wait_force("v", p.sig("v"))
                    p.emit("v", lambda v, sl=sl, half=half, eb=eb, kb=kb: v.scalar_tensor_tensor(
                        out=ktm[sl][:, half * 512: half * 512 + 512],
                        in0=PS[kb][:, :], scalar=0.0, in1=eb[:, :],
                        op0=ALU.max, op1=ALU.add))
                    ktoks[idx] = p.sig("v")
                    exp_war[half] = ktoks[idx]
                    bank_free[kb] = ktoks[idx]

            LA = LNKV_LA if LNKV_LA > 0 else n_tc
            emit_p1_stats(0)
            for i in range(n_tc + LA + 1):
                if i + 1 < n_tc:
                    emit_p1_stats(i + 1)
                if i < n_tc:
                    emit_p1_norm(i)
                if LA <= i < n_tc + LA:
                    emit_p2(i - LA)
                if i >= LA + 1:
                    emit_kv(i - LA - 1)
                if i < n_tc:
                    emit_p1_tp(i)

            last_t = p.tok("t")
            p.emit("g", lambda g: g.memset(kv2[:, :], 0.0))
            p.emit("g", lambda g: g.memset(k16[:, :], 0.0))
            gz = p.sig("g", sem="gc")
            p.wait("v", last_t)
            p.wait("v", gz)
            p.wait("v", tok_maskb)
            for pair in range(NPAIR):
                bank = 2 + pair // 4
                qq = pair % 4
                p.emit("v", lambda v, pair=pair, bank=bank, qq=qq: v.tensor_copy(
                    out=kv2[0:64, pair * 128: pair * 128 + 64],
                    in_=PS[bank][0:64, qq * 128: qq * 128 + 64]))
                p.emit("v", lambda v, pair=pair, bank=bank, qq=qq: v.tensor_copy(
                    out=kv2[64:128, pair * 128 + 64: pair * 128 + 128],
                    in_=PS[bank][64:128, qq * 128 + 64: qq * 128 + 128]))
            for c in range(ND):
                p.emit("v", lambda v, c=c: v.tensor_scalar(
                    out=k16[:, c * 128 + 2 * c: c * 128 + 2 * c + 1],
                    in0=PS[4][:, c:c + 1], scalar1=maskb[:, 0:1], scalar2=None,
                    op0=ALU.mult))
                p.emit("v", lambda v, c=c: v.tensor_scalar(
                    out=k16[:, c * 128 + 2 * c + 1: c * 128 + 2 * c + 2],
                    in0=PS[4][:, c:c + 1], scalar1=maskb[:, 1:2], scalar2=None,
                    op0=ALU.mult))
            ev = p.sig("v")
            for i in (2, 3, 4):
                bank_free[i] = ev
            return ev

        def stage_qattn(b, wq_tok, wo_tok, kv_tok, acc_ready):
            exp_war = {0: None, 1: None}
            qelu = {}
            num_done = [None] * n_tiles
            wo_done = [None] * n_tiles
            rb_done = [None] * n_tiles
            rntoks = [None] * n_tiles
            apre_sl_war = {0: None, 1: None}

            def emit_qproj(ti, dcs):
                qf = q_fm2[ti % 2]
                for dc in dcs:
                    qb = dc % 2
                    p.wait("t", wq_tok)
                    p.wait("t", [late["bqf"], late["onesr"]])
                    p.wait("t", bank_free[qb])
                    for c in range(ND):
                        p.emit("t", lambda t, c=c, dc=dc, ti=ti, qb=qb: t.matmul(
                            PS[qb][:, :],
                            wq_sb[:, c * 1024 + dc * 128: c * 1024 + (dc + 1) * 128],
                            xn[:, c * T + ti * 512: c * T + (ti + 1) * 512],
                            start=(c == 0), stop=False))
                    p.emit("t", lambda t, dc=dc, qb=qb: t.matmul(
                        PS[qb][:, :], bqf[:, dc * 128:(dc + 1) * 128],
                        onesr[:, :], start=False, stop=True))
                    qmm = p.sig("t")
                    eb = sc_a if (dc % 2 == 0) else sc_c
                    p.wait("s", qmm)
                    p.wait("s", exp_war[dc % 2])
                    p.emit("s", lambda s, qb=qb, eb=eb: s.activation(
                        eb[:, :], PS[qb][:, :], AF.Exp))
                    te = p.sig("s")
                    p.wait("v", te)
                    if ti >= 2:
                        p.wait("v", num_done[ti - 2])
                    p.emit("v", lambda v, eb=eb: v.tensor_scalar(
                        out=eb[:, :], in0=eb[:, :], scalar1=1.0, scalar2=None,
                        op0=ALU.min))
                    p.wait_force("v", p.sig("v"))
                    p.emit("v", lambda v, dc=dc, qb=qb, eb=eb, qf=qf: v.scalar_tensor_tensor(
                        out=qf[:, dc * 512:(dc + 1) * 512],
                        in0=PS[qb][:, :], scalar=0.0, in1=eb[:, :],
                        op0=ALU.max, op1=ALU.add))
                    qelu[(ti, dc)] = p.sig("v")
                    exp_war[dc % 2] = qelu[(ti, dc)]
                    bank_free[qb] = qelu[(ti, dc)]

            def emit_nmm(ti):
                qf = q_fm2[ti % 2]
                p.wait("t", kv_tok)
                p.wait("t", bank_free[4])
                for pair in range(NPAIR):
                    p.wait("t", qelu[(ti, pair)])
                    p.emit("t", lambda t, pair=pair, qf=qf: t.matmul(
                        PS[4][:, :], k16[:, pair * 128:(pair + 1) * 128],
                        qf[:, pair * 512:(pair + 1) * 512],
                        start=(pair == 0), stop=(pair == NPAIR - 1)))
                nmm = p.sig("t")
                p.wait("v", nmm)
                p.wait("v", init_tok)
                if ti >= 1:
                    p.wait("v", rb_done[ti - 1])
                p.emit("v", lambda v: v.tensor_scalar(
                    out=sc_b[0:16, :], in0=PS[4][0:16, :], scalar1=EPS_ATTN,
                    scalar2=None, op0=ALU.max))
                p.wait_force("v", p.sig("v"))
                p.emit("v", lambda v: v.reciprocal(out=sc_b[0:16, :], in_=sc_b[0:16, :]))
                p.wait_force("v", p.sig("v"))
                p.emit("v", lambda v: v.tensor_copy(out=rn_sb[0:16, :], in_=sc_b[0:16, :]))
                rntoks[ti] = p.sig("v")
                bank_free[4] = rntoks[ti]

            apre_toks = {}

            def emit_tail_a(ti):
                qf = q_fm2[ti % 2]
                ap = apre2[ti % 2]
                apre_tok = apre_toks[ti] = [None] * NPAIR
                for pair in range(NPAIR):
                    nb_ = (2, 3, 4)[pair % 3]
                    rbb = 5 + pair % 2
                    p.wait("t", bank_free[nb_])
                    p.emit("t", lambda t, pair=pair, nb_=nb_, qf=qf: t.matmul(
                        PS[nb_][:, :], kv2[:, pair * 128:(pair + 1) * 128],
                        qf[:, pair * 512:(pair + 1) * 512], start=True, stop=True))
                    num_t = p.sig("t")
                    if pair == NPAIR - 1:
                        num_done[ti] = num_t
                    p.wait("t", rntoks[ti])
                    p.wait("t", bank_free[rbb])
                    p.emit("t", lambda t, pair=pair, rbb=rbb: t.matmul(
                        PS[rbb][:, :], e16b[:, pair * 128:(pair + 1) * 128],
                        rn_sb[:, :], start=True, stop=True))
                    rb_t = p.sig("t")
                    rb_done[ti] = rb_t
                    sl = pair % 2
                    p.wait("s", rb_t)
                    p.wait("s", apre_sl_war[sl])
                    p.emit("s", lambda s, sl=sl, rbb=rbb: s.copy(
                        out=rb_sb[:, sl * 512:(sl + 1) * 512], in_=PS[rbb][:, :]))
                    bank_free[rbb] = p.sig("s")
                    p.wait("v", bank_free[rbb])
                    p.wait("v", num_t)
                    if ti >= 2:
                        p.wait("v", wo_done[ti - 2])
                    p.emit("v", lambda v, pair=pair, sl=sl, nb_=nb_, ap=ap: v.tensor_tensor(
                        out=ap[:, pair * 512:(pair + 1) * 512], in0=PS[nb_][:, :],
                        in1=rb_sb[:, sl * 512:(sl + 1) * 512], op=ALU.mult))
                    apre_tok[pair] = p.sig("v")
                    apre_sl_war[sl] = apre_tok[pair]
                    bank_free[nb_] = apre_tok[pair]

            def emit_tail_b(ti):
                ap = apre2[ti % 2]
                apre_tok = apre_toks[ti]
                for sl4 in range(4):
                    for dh in range(2):
                        wb_ = 7 if (sl4 * 2 + dh) % 2 == 0 else 4
                        p.wait("t", wo_tok)
                        p.wait("t", bank_free[wb_])
                        for c in range(ND):
                            p.wait("t", apre_tok[c])
                            p.emit("t", lambda t, c=c, sl4=sl4, dh=dh, wb_=wb_, ap=ap: t.matmul(
                                PS[wb_][:, :],
                                ap[:, c * 512 + sl4 * 128: c * 512 + (sl4 + 1) * 128],
                                wo_sb[:, c * 1024 + dh * 512: c * 1024 + dh * 512 + 512],
                                start=(c == 0), stop=False))
                        p.emit("t", lambda t, dh=dh, wb_=wb_: t.matmul(
                            PS[wb_][:, :], oscl[:, :],
                            bbo[:, dh * 512: dh * 512 + 512], start=False, stop=True))
                        womm = p.sig("t")
                        tcg = ti * 4 + sl4
                        p.wait("v", womm)
                        p.wait("v", acc_ready[tcg])
                        p.emit("v", lambda v, tcg=tcg, dh=dh, wb_=wb_: v.tensor_tensor(
                            out=acc[:, tcg * 1024 + dh * 512: tcg * 1024 + dh * 512 + 512],
                            in0=PS[wb_][:, :],
                            in1=acc[:, tcg * 1024 + dh * 512: tcg * 1024 + dh * 512 + 512],
                            op=ALU.add))
                        bank_free[wb_] = p.sig("v")
                wo_done[ti] = p.tok("t")
                for k4 in range(4):
                    gates_mm(ti * 4 + k4)

            emit_qproj(0, range(ND))
            for ti in range(n_tiles):
                emit_nmm(ti)
                if ti + 1 < n_tiles:
                    emit_qproj(ti + 1, range(0, 5))
                emit_tail_a(ti)
                if ti + 1 < n_tiles:
                    emit_qproj(ti + 1, range(5, 8))
                emit_tail_b(ti)
            return p.tok("v")

        def gates_mm(tc):
            gb_ = 5 + tc % 2
            p.wait("t", [late["gatesb"], late["bgeb"], tok_oscl])
            p.wait("t", bank_free[gb_])
            for c in range(ND):
                p.emit("t", lambda t, c=c, tc=tc, gb_=gb_: t.matmul(
                    PS[gb_][:, 0:10],
                    xn[:, c * T + tc * 128: c * T + (tc + 1) * 128],
                    gatesb[:, c * 10:(c + 1) * 10],
                    start=(c == 0), stop=False))
            p.emit("t", lambda t, gb_=gb_: t.matmul(
                PS[gb_][:, 0:10], oscl[:, :], bgeb[:, :], start=False, stop=True))
            gmm = p.sig("t")
            p.wait("s", gmm)
            p.emit("s", lambda s, tc=tc, gb_=gb_: s.copy(
                out=glog[:, tc * 10:(tc + 1) * 10], in_=PS[gb_][:, 0:10]))
            glog_tok[tc] = bank_free[gb_] = p.sig("s")

        def gates_steps():
            for pr in range(n_tc // 2):
                yield from gates_pair_steps(2 * pr, 2 * pr + 1)

        def gates_pair_steps(tc_a, tc_b):
            Gs = [gsc[:, 0:64], gsc[:, 64:128]]

            def phase_a(G, tc):
                return [
                    lambda v: v.tensor_copy(out=G[:, 0:10], in_=glog[:, tc * 10:(tc + 1) * 10]),
                    lambda v: v.tensor_tensor(out=G[:, 10:11], in0=G[:, 0:1],
                                              in1=G[:, 1:2], op=ALU.subtract),
                    lambda v: v.tensor_scalar(out=G[:, 11:12], in0=G[:, 10:11],
                                              scalar1=0.0, scalar2=None, op0=ALU.is_lt),
                    lambda v: v.tensor_scalar(out=G[:, 12:13], in0=G[:, 11:12],
                                              scalar1=BIG, scalar2=None, op0=ALU.mult),
                    lambda v: v.tensor_scalar(out=G[:, 13:14], in0=G[:, 11:12],
                                              scalar1=-BIG, scalar2=BIG, op0=ALU.mult,
                                              op1=ALU.add),
                    lambda v: v.tensor_scalar(out=G[:, 24:28], in0=G[:, 2:6],
                                              scalar1=G[:, 12:13], scalar2=None,
                                              op0=ALU.subtract),
                    lambda v: v.tensor_scalar(out=G[:, 28:32], in0=G[:, 6:10],
                                              scalar1=G[:, 13:14], scalar2=None,
                                              op0=ALU.subtract),
                    lambda v: v.tensor_reduce(out=G[:, 17:18], in_=G[:, 24:32],
                                              axis=AX.X, op=ALU.max, negate=True),
                    lambda v: v.tensor_scalar(out=G[:, 14:15], in0=G[:, 10:11],
                                              scalar1=-1.0, scalar2=G[:, 10:11],
                                              op0=ALU.mult, op1=ALU.max),
                ]

            def phase_b(G, tc):
                return [
                    lambda v: v.tensor_reduce(out=G[:, 18:19], in_=G[:, 32:40],
                                              axis=AX.X, op=ALU.add),
                    lambda v: v.tensor_scalar(out=G[:, 16:17], in0=G[:, 15:16],
                                              scalar1=1.0, scalar2=None, op0=ALU.add),
                    lambda v: v.reciprocal(out=G[:, 19:20], in_=G[:, 18:19]),
                    lambda v: v.reciprocal(out=G[:, 16:17], in_=G[:, 16:17]),
                    lambda v: v.tensor_scalar(out=G[:, 40:48], in0=G[:, 32:40],
                                              scalar1=G[:, 19:20], scalar2=G[:, 16:17],
                                              op0=ALU.mult, op1=ALU.mult),
                    lambda v: v.tensor_reduce(out=G[:, 20:21], in_=G[:, 40:48],
                                              axis=AX.X, op=ALU.max),
                    lambda v: v.tensor_scalar(out=G[:, 48:56], in0=G[:, 40:48],
                                              scalar1=G[:, 20:21], scalar2=None,
                                              op0=ALU.is_ge),
                    lambda v: v.tensor_scalar(out=G[:, 48:56], in0=G[:, 48:56],
                                              scalar1=-1.0, scalar2=1.0, op0=ALU.mult,
                                              op1=ALU.add),
                    lambda v: v.tensor_tensor(out=G[:, 56:64], in0=G[:, 48:56],
                                              in1=G[:, 40:48], op=ALU.mult),
                    lambda v: v.tensor_reduce(out=G[:, 21:22], in_=G[:, 56:64],
                                              axis=AX.X, op=ALU.max),
                    lambda v: v.scalar_tensor_tensor(
                        out=wse[:, tc * NE:(tc + 1) * NE], in0=G[:, 40:48],
                        scalar=G[:, 21:22], in1=G[:, 40:48], op0=ALU.is_ge,
                        op1=ALU.mult),
                ]

            chains_a = [phase_a(Gs[0], tc_a), phase_a(Gs[1], tc_b)]
            first = [True]

            def astep(op):
                if first[0]:
                    p.wait("v", glog_tok[tc_a])
                    p.wait("v", glog_tok[tc_b])
                    first[0] = False
                p.emit("v", op)
                p.sig("v")

            for i in range(len(chains_a[0])):
                for idx in range(2):
                    yield lambda op=chains_a[idx][i]: astep(op)

            se_box = [None]

            def sstep():
                p.wait("s", p.tok("v"))
                for G in Gs:
                    p.emit("s", lambda s, G=G: s.activation(
                        G[:, 32:40], G[:, 24:32], AF.Exp, bias=G[:, 17:18], scale=1.0))
                    p.emit("s", lambda s, G=G: s.activation(
                        G[:, 15:16], G[:, 14:15], AF.Exp, scale=-1.0))
                se_box[0] = p.sig("s")

            yield sstep

            chains_b = [phase_b(Gs[0], tc_a), phase_b(Gs[1], tc_b)]
            first_b = [True]

            def bstep(op):
                if first_b[0]:
                    p.wait("v", se_box[0])
                    first_b[0] = False
                p.emit("v", op)
                p.sig("v")

            for i in range(len(chains_b[0])):
                for idx in range(2):
                    yield lambda op=chains_b[idx][i]: bstep(op)

        def swiglu_load(srcR_h, jc):
            sa = (jc % 2) * 2
            p.wait("g", rel["wbuf"][jc % 2])
            la = p.dma("g", lambda g, sa=sa, srcR_h=srcR_h, jc=jc: g.dma_start(
                wb_b[:, sa * 1024:(sa + 2) * 1024], srcR_h[jc]))
            return la, la

        def swiglu_prefetch(srcT_h):
            return {jc: swiglu_load(srcT_h, jc) for jc in (0, 1)}

        def swiglu_stage(b, srcT_h, dst_sb, pre=None, cb=None, ti_toks=None):
            sc_tt_tok = dict(rel["sc_tt"])
            for jc in range(NJC):
                sa = (jc % 2) * 2
                a_v = wb_b[:, sa * 1024:(sa + 1) * 1024]
                b_v = wb_b[:, (sa + 1) * 1024:(sa + 2) * 1024]
                if pre is not None and jc in pre:
                    la, lb = pre[jc]
                else:
                    la, lb = swiglu_load(srcT_h, jc)

                for ti in range(n_tiles):
                    gidx = ti // 2
                    par = ti % 2
                    ab = (gidx % 2) * 4 + par
                    bb = (gidx % 2) * 4 + 2 + par
                    p.wait("t", la)
                    p.wait("t", lb)
                    p.wait("t", bank_free[ab])
                    for c in range(ND):
                        p.emit("t", lambda t, c=c, ti=ti, a_v=a_v, ab=ab: t.matmul(
                            PS[ab][:, :], a_v[:, c * 128:(c + 1) * 128],
                            xn[:, c * T + ti * 512: c * T + (ti + 1) * 512],
                            start=(c == 0), stop=(c == ND - 1)))
                    amm = p.sig("t")
                    p.wait("t", bank_free[bb])
                    for c in range(ND):
                        p.emit("t", lambda t, c=c, ti=ti, b_v=b_v, bb=bb: t.matmul(
                            PS[bb][:, :], b_v[:, c * 128:(c + 1) * 128],
                            xn[:, c * T + ti * 512: c * T + (ti + 1) * 512],
                            start=(c == 0), stop=(c == ND - 1)))
                    bmm = p.sig("t")
                    sc2 = sc_a if ti % 2 == 0 else sc_b
                    p.wait("s", amm)
                    p.wait("s", sc_tt_tok[ti % 2])
                    p.emit("s", lambda s, ab=ab, sc2=sc2: s.activation(
                        sc2[:, :], PS[ab][:, :], AF.Silu))
                    st_ = p.sig("s")
                    bank_free[ab] = st_
                    p.wait("v", st_)
                    p.wait("v", bmm)
                    p.emit("v", lambda v, jc=jc, ti=ti, bb=bb, sc2=sc2, dst_sb=dst_sb: v.tensor_tensor(
                        out=dst_sb[:, jc * T + ti * 512: jc * T + (ti + 1) * 512],
                        in0=sc2[:, :], in1=PS[bb][:, :], op=ALU.mult))
                    sc_tt_tok[ti % 2] = p.sig("v")
                    bank_free[bb] = sc_tt_tok[ti % 2]
                    if ti_toks is not None and jc == NJC - 1:
                        ti_toks[ti] = sc_tt_tok[ti % 2]
                    if cb is not None:
                        cb(jc, ti)
                rel["wbuf"][jc % 2] = p.tok("t")
            rel["sc_tt"] = dict(sc_tt_tok)
            return p.tok("v")

        def stage_wout(b, swg_tok, swg_ti=None, after_first_loads=None):
            load_tok = [None] * 8
            comp_tok = [None] * 8

            def emit_load(q8):
                sl = q8 % 2
                p.wait("g", comp_tok[q8 - 2] if q8 >= 2 else rel["woutbuf"][sl])
                load_tok[q8] = p.dma("g", lambda g, q8=q8, sl=sl: g.dma_start(
                    wob[sl][:, :], woutR_h[q8]))

            emit_load(0)
            emit_load(1)
            if after_first_loads is not None:
                after_first_loads()
            for q8 in range(8):
                sl = q8 % 2
                for tc in range(n_tc):
                    wb_ = tc % 2
                    p.wait("t", load_tok[q8])
                    p.wait("t", swg_ti[tc // 4] if swg_ti is not None else swg_tok)
                    p.wait("t", bank_free[wb_])
                    for jc in range(NJC):
                        p.emit("t", lambda t, jc=jc, tc=tc, sl=sl, wb_=wb_: t.matmul(
                            PS[wb_][:, 0:128],
                            swg[:, jc * T + tc * 128: jc * T + (tc + 1) * 128],
                            wob[sl][:, jc * 128:(jc + 1) * 128],
                            start=(jc == 0), stop=(jc == NJC - 1)))
                    wmm = p.sig("t")
                    p.wait("v", wmm)
                    p.emit("v", lambda v, tc=tc, q8=q8, wb_=wb_: v.tensor_tensor(
                        out=acc[:, tc * 1024 + q8 * 128: tc * 1024 + (q8 + 1) * 128],
                        in0=PS[wb_][:, 0:128],
                        in1=acc[:, tc * 1024 + q8 * 128: tc * 1024 + (q8 + 1) * 128],
                        op=ALU.add))
                    bank_free[wb_] = p.sig("v")
                comp_tok[q8] = p.tok("t")
                rel["woutbuf"][sl] = comp_tok[q8]
                if q8 + 2 < 8:
                    emit_load(q8 + 2)
            return p.tok("v")

        def stage_experts(b, h_tok, wse_tok, xn_rel_tok):
            order = [(e, dh) for e in range(NE) for dh in range(2)]
            load_tok = {}
            comp_tok = {}
            out_toks = []

            def emit_load(i):
                e, dh = order[i]
                buf = w2b[i % 2]
                if i == 0:
                    p.wait("g", rel["woutbuf"])
                    tA = p.dma("g", lambda g: g.dma_start(
                        wo8_b[:, 0:4096], w2R_h[0, 0][:, 0:4096]))
                    p.wait("g", xn_rel_tok)
                    tB = p.dma("g", lambda g, buf=buf: g.dma_start(
                        buf[:, 4096:8192], w2R_h[0, 0][:, 4096:8192]))
                    load_tok[0] = [tA, tB]
                    return
                p.wait("g", comp_tok[i - 2] if i >= 2 else xn_rel_tok)
                load_tok[i] = p.dma("g", lambda g, e=e, dh=dh, buf=buf: g.dma_start(
                    buf[:, :], w2R_h[e, dh]))

            emit_load(0)
            emit_load(1)
            for i, (e, dh) in enumerate(order):
                buf = w2b[i % 2]
                for tc in range(n_tc):
                    wb_ = tc % 2
                    p.wait("t", load_tok[i])
                    p.wait("t", h_tok)
                    p.wait("t", bank_free[wb_])
                    for jc in range(NJC):
                        rv = wo8_b if (i == 0 and jc < 8) else buf
                        p.emit("t", lambda t, jc=jc, tc=tc, rv=rv, wb_=wb_: t.matmul(
                            PS[wb_][:, :],
                            h_sb[:, jc * T + tc * 128: jc * T + (tc + 1) * 128],
                            rv[:, jc * 512:(jc + 1) * 512],
                            start=(jc == 0), stop=(jc == NJC - 1)))
                    emm = p.sig("t")
                    p.wait("v", emm)
                    p.wait("v", wse_tok)
                    p.emit("v", lambda v, tc=tc, dh=dh, e=e, wb_=wb_: v.scalar_tensor_tensor(
                        out=acc[:, tc * 1024 + dh * 512: tc * 1024 + dh * 512 + 512],
                        in0=PS[wb_][:, :], scalar=wse[:, tc * NE + e: tc * NE + e + 1],
                        in1=acc[:, tc * 1024 + dh * 512: tc * 1024 + dh * 512 + 512],
                        op0=ALU.mult, op1=ALU.add))
                    stok = p.sig("v")
                    bank_free[wb_] = stok
                    if e == NE - 1 and dh == 1:
                        p.wait("y", stok)
                        out_toks.append(p.dma("y", lambda y, b=b, tc=tc: y.dma_start(
                            out_h[b, tc * 128:(tc + 1) * 128, :],
                            acc[:, tc * 1024:(tc + 1) * 1024])))
                comp_tok[i] = p.tok("t")
                if i + 2 < len(order):
                    emit_load(i + 2)
            return out_toks

        def emit_tap(name):
            if name in tap_out:
                ncols = tap_out[name].shape[-1]
                src = {"xn": xn, "kv2": kv2, "k16": k16, "acc": acc, "wse": wse,
                       "swg": swg, "h": h_sb, "qfm": q_fm, "apre": apre,
                       "rn": rn_sb, "gsc": gsc}[name]
                p.barrier(("t", "v", "s"))
                tv = p.tok("v")
                p.wait("g", tv)
                gt = p.dma("g", lambda g, src=src, name=name, ncols=ncols: g.dma_start(
                    tap_out[name][:, :], src[:, 0:ncols]))
                p.wait("v", gt)
                p.emit("v", lambda v: v.nop())

        for b in range(NB):
            acc_ready = [None] * n_tc
            wk_tok = load_w_chunked(wk_sb, wk_h, rel["R0"])
            wv_tok = load_w_chunked(wv_sb, wv_h, rel["R1"])
            if b == 0:
                late.update(emit_late_consts())
            wq_tok = load_w_chunked(wq_sb, wq_h, rel["R3"])

            kv_tok = stage_lnkv(b, acc_ready, wk_tok, wv_tok)
            p.barrier()
            if b == 0:
                emit_tap("xn")
            rel["R0"] = p.tok("t")
            if b == 0:
                emit_tap("kv2")
                emit_tap("k16")

            wo_tok = load_w_chunked(wo_sb, wo_h, rel["R0"])
            pre_in = swiglu_prefetch(winR_h)
            stage_qattn(b, wq_tok, wo_tok, kv_tok, acc_ready)
            p.barrier()
            rel["R2"] = p.tok("t")
            rel["R3"] = p.tok("t")
            if b == 0:
                emit_tap("acc")
                emit_tap("qfm")
                emit_tap("apre")

            gates_iter = gates_steps()

            def gdrain(jc, ti, n=6):
                if jc < 1:
                    return
                for _ in range(n):
                    st = next(gates_iter, None)
                    if st is None:
                        return
                    st()

            swg_ti = {}
            swg_tok = swiglu_stage(b, winR_h, swg, pre_in, cb=gdrain, ti_toks=swg_ti)
            for st in gates_iter:
                st()
            gate_tok = p.tok("v")
            if b == 0:
                emit_tap("swg")
                emit_tap("wse")
                emit_tap("gsc")
            pre_h = {}
            stage_wout(b, swg_tok, swg_ti,
                       after_first_loads=lambda: pre_h.update(swiglu_prefetch(w1R_h)))
            p.barrier()

            h_tok = swiglu_stage(b, w1R_h, h_sb, pre_h)
            p.barrier()
            xn_rel = p.tok("t")
            if b == 0:
                emit_tap("h")

            out_toks = stage_experts(b, h_tok, gate_tok, xn_rel)
            p.barrier(("t", "v", "s", "y"))
            rel["acc"] = p.queue_done("y")
            rel["R0"] = rel["R1"] = rel["R2"] = rel["R3"] = p.tok("t")
            rel["xarena"] = p.tok("t")
            rel["wbuf"] = [p.tok("t"), p.tok("t")]
            rel["woutbuf"] = [p.tok("t"), p.tok("t")]

        p.wait_force("y", p.queue_done("y"))
        p.emit("y", lambda y: y.nop())
        p.wait_force("g", p.queue_done("g"))
        p.emit("g", lambda g: g.nop())

        p.check_deadlock()
        n_ins = {e: len(p.items[e]) for e in ENGINES}

        def runner(name):
            def run(eng):
                for waits, fn, incs in p.items[name]:
                    for s_, v_ in waits.items():
                        eng.wait_ge(sems[s_], v_)
                    ins = fn(eng)
                    for sem_n, amt in incs:
                        ins.then_inc(sems[sem_n], amt)
            return run

        blk.tensor(runner("t"))
        blk.vector(runner("v"))
        blk.scalar(runner("s"))
        blk.gpsimd(runner("g"))
        blk.sync(runner("y"))

    return nc, n_ins


def prep_inputs(inputs, n_tc=16, n_cores=8):
    T = n_tc * 128
    f32c = lambda a: np.ascontiguousarray(np.asarray(a), dtype=np.float32)
    x = f32c(inputs["L0_attn_inputs_q"])[:, :T, :]

    def projR(w):
        return f32c(np.asarray(w, dtype=np.float32)
                    .reshape(ND, 128, D).transpose(1, 0, 2).reshape(128, ND * D))

    def swigluR(w):
        wT = np.asarray(w, dtype=np.float32).T
        B = wT.reshape(ND, 128, 2 * HID).transpose(1, 0, 2)
        out = np.empty((NJC, 128, 2048), dtype=np.float32)
        for jc in range(NJC):
            a = B[:, :, jc * 128:(jc + 1) * 128].reshape(128, 1024)
            bb = B[:, :, HID + jc * 128: HID + (jc + 1) * 128].reshape(128, 1024)
            out[jc] = np.concatenate([a, bb], axis=1)
        return f32c(out)

    winR = swigluR(inputs["L0_ffn_w_in"])
    w1R = swigluR(inputs["L0_ffn_w1"])
    woutT = np.asarray(inputs["L0_ffn_w_out"], dtype=np.float32).T
    C = woutT.reshape(NJC, 128, D).transpose(1, 0, 2)
    woutR = f32c(np.stack([C[:, :, q8 * 128:(q8 + 1) * 128].reshape(128, 2048)
                           for q8 in range(8)]))
    w2T = np.asarray(inputs["L0_ffn_w2"], dtype=np.float32).transpose(0, 2, 1)
    A = w2T.reshape(NE, NJC, 128, D).transpose(0, 2, 1, 3)
    w2R = f32c(np.stack([A[:, :, :, 0:512].reshape(NE, 128, NJC * 512),
                         A[:, :, :, 512:1024].reshape(NE, 128, NJC * 512)], axis=1))
    gatesT = f32c(np.concatenate([np.asarray(inputs["L0_ffn_wg"]).T,
                                  np.asarray(inputs["L0_ffn_we"]).T], axis=1))
    bk = f32c(np.asarray(inputs["L0_attn_bk"]).reshape(1, D))
    bv = f32c(np.asarray(inputs["L0_attn_bv"]).reshape(1, D))
    bo = f32c(np.asarray(inputs["L0_attn_bo"]).reshape(1, D))
    bq = f32c(np.asarray(inputs["L0_attn_bq"]).reshape(1, D))
    geb = f32c(np.concatenate([np.asarray(inputs["L0_ffn_gb"]),
                               np.asarray(inputs["L0_ffn_eb"])]).reshape(1, 10))
    ident = np.eye(128, dtype=np.float32)
    e16 = np.zeros((NPAIR, 128, 128), dtype=np.float32)
    for pr in range(NPAIR):
        e16[pr, 2 * pr, 0:64] = 1.0
        e16[pr, 2 * pr + 1, 64:128] = 1.0
    maskb = np.zeros((128, 2), dtype=np.float32)
    maskb[0:64, 0] = 1.0
    maskb[64:128, 1] = 1.0

    shared = dict(wq=projR(inputs["L0_attn_Wq"]), wk=projR(inputs["L0_attn_Wk"]),
                  wv=projR(inputs["L0_attn_Wv"]), wo=projR(inputs["L0_attn_Wo"]),
                  winR=winR, w1R=w1R, woutR=woutR, w2R=w2R, gatesT=gatesT,
                  bk=bk, bv=bv, bo=bo, bq=bq, geb=geb,
                  ident=ident, e16=e16, maskb=maskb)
    in_maps = []
    for c in range(n_cores):
        m = dict(shared)
        m["x"] = np.ascontiguousarray(x[c * NB:(c + 1) * NB])
        in_maps.append(m)
    return in_maps


def kernel(**inputs):
    n_tc = 16
    in_maps = prep_inputs(inputs, n_tc=n_tc, n_cores=8)
    nc, _ = build(n_tc=n_tc)
    res = run_bass_kernel_spmd(nc, in_maps, core_ids=list(range(8)))
    outs = [np.asarray(r["out"], dtype=np.float32) for r in res.results]
    return np.concatenate(outs, axis=0)


# version 21 (final)
# speedup vs baseline: 1.0965x; 1.0015x over previous
import contextlib
import numpy as np

import concourse.bass as bass
import concourse.mybir as mybir
from concourse.bass_utils import run_bass_kernel_spmd

F32 = mybir.dt.float32
BF16 = mybir.dt.bfloat16
U8 = mybir.dt.uint8
AF = mybir.ActivationFunctionType
ALU = mybir.AluOpType
AX = mybir.AxisListType

D = 1024
ND = 8
HID = 2048
NJC = 16
NH = 16
NPAIR = 8
NE = 8
NB = 2
EPS_LN = 1e-5
EPS_ATTN = 1e-6
BIG = 1e9
LNKV_LA = 2

ENGINES = ("t", "v", "s", "g", "y")
SYNC_SELF_OK = ("t", "v", "s")
DMA_POOL = {"g": 8, "y": 4}


class Prog:
    def __init__(self):
        self.items = {e: [] for e in ENGINES}
        self.count = {}
        self.pending = {e: {} for e in ENGINES}
        self.max_waited = {e: {} for e in ENGINES}
        self.dma_slot = {q: 0 for q in DMA_POOL}
        self.sem_names = set(ENGINES) | {"gc"}
        for q, n in DMA_POOL.items():
            for i in range(n):
                self.sem_names.add(f"{q}{i}")
        for s in self.sem_names:
            self.count[s] = 0

    def wait(self, eng, tok):
        if tok is None:
            return
        if isinstance(tok, list):
            for t in tok:
                self.wait(eng, t)
            return
        sem, val = tok
        if sem == eng and eng in SYNC_SELF_OK:
            return
        if val <= self.max_waited[eng].get(sem, 0):
            return
        if val <= self.pending[eng].get(sem, 0):
            return
        self.pending[eng][sem] = val

    def wait_force(self, eng, tok):
        if tok is None:
            return
        if isinstance(tok, list):
            for t in tok:
                self.wait_force(eng, t)
            return
        sem, val = tok
        if val <= self.max_waited[eng].get(sem, 0):
            return
        if val <= self.pending[eng].get(sem, 0):
            return
        self.pending[eng][sem] = val

    AUTO_CHAIN = ("v", "s")

    def emit(self, eng, fn):
        if eng in self.AUTO_CHAIN and self.items[eng]:
            if not self.items[eng][-1][2]:
                self.sig(eng)
            self.wait_force(eng, (eng, self.count[eng]))
        w = self.pending[eng]
        self.pending[eng] = {}
        for s, v in w.items():
            self.max_waited[eng][s] = max(self.max_waited[eng].get(s, 0), v)
        self.items[eng].append([w, fn, []])

    def sig(self, eng, amt=1, sem=None):
        sem = sem or eng
        assert self.items[eng], f"sig on empty engine {eng}"
        self.items[eng][-1][2].append((sem, amt))
        self.count[sem] += amt
        return (sem, self.count[sem])

    def dma(self, q, fn):
        slot = self.dma_slot[q]
        self.dma_slot[q] = (slot + 1) % DMA_POOL[q]
        sem = f"{q}{slot}"
        self.wait_force(q, (sem, self.count[sem]))
        self.emit(q, fn)
        return self.sig(q, 16, sem=sem)

    def queue_done(self, q):
        return [(f"{q}{i}", self.count[f"{q}{i}"]) for i in range(DMA_POOL[q])]

    def tok(self, eng):
        return (eng, self.count[eng])

    def sig_or_reuse(self, eng):
        if not self.items[eng]:
            return None
        incs = self.items[eng][-1][2]
        if incs:
            s, _ = incs[-1]
            return (s, self.count[s])
        return self.sig(eng)

    def barrier(self, engines=("t", "v", "s")):
        toks = []
        for e in engines:
            if e in DMA_POOL:
                toks.extend(self.queue_done(e))
            else:
                t = self.sig_or_reuse(e)
                if t is not None:
                    toks.append(t)
        for e in engines:
            for t in toks:
                self.wait(e, t)

    def check_deadlock(self):
        ptr = {e: 0 for e in ENGINES}
        val = {s: 0 for s in self.sem_names}
        progress = True
        while progress:
            progress = False
            for e in ENGINES:
                while ptr[e] < len(self.items[e]):
                    waits, _, incs = self.items[e][ptr[e]]
                    if all(val[s] >= v for s, v in waits.items()):
                        for s, amt in incs:
                            val[s] += amt
                        ptr[e] += 1
                        progress = True
                    else:
                        break
        stuck = {}
        for e in ENGINES:
            if ptr[e] < len(self.items[e]):
                waits = self.items[e][ptr[e]][0]
                stuck[e] = {"at": ptr[e], "of": len(self.items[e]), "waits": waits,
                            "have": {s: val[s] for s in waits}}
        if stuck:
            raise RuntimeError(f"deadlock in program build: {stuck}")


def build(n_tc=16, taps=()):
    T = n_tc * 128
    n_tiles = (n_tc + 3) // 4

    nc = bass.Bass("TRN2", target_bir_lowering=False)

    def din(name, shape):
        return nc.dram_tensor(name, shape, F32, kind="ExternalInput").ap()

    x_h = din("x", [NB, T, D])
    out_h = nc.dram_tensor("out", [NB, T, D], F32, kind="ExternalOutput").ap()
    wq_h = din("wq", [128, ND * 1024])
    wk_h = din("wk", [128, ND * 1024])
    wv_h = din("wv", [128, ND * 1024])
    wo_h = din("wo", [128, ND * 1024])
    winR_h = din("winR", [NJC, 128, 2048])
    w1R_h = din("w1R", [NJC, 128, 2048])
    woutR_h = din("woutR", [8, 128, 2048])
    w2R_h = din("w2R", [NE, 2, 128, NJC * 512])
    gatesT_h = din("gatesT", [D, 10])
    bk_h = din("bk", [1, D])
    bv_h = din("bv", [1, D])
    bo_h = din("bo", [1, D])
    bq_h = din("bq", [1, D])
    geb_h = din("geb", [1, 10])

    ident_h = din("ident", [128, 128])
    e16_h = din("e16", [NPAIR, 128, 128])
    maskb_h = din("maskb", [128, 2])

    tap_out = {}
    for name, shape in taps:
        tap_out[name] = nc.dram_tensor("tap_" + name, list(shape), F32,
                                       kind="ExternalOutput").ap()

    es = contextlib.ExitStack()
    with es:
        acc = es.enter_context(nc.sbuf_tensor("acc", [128, n_tc * 1024], F32))
        harena = es.enter_context(nc.sbuf_tensor("harena", [128, max(65536, NJC * T * 2)], U8))
        xarena = es.enter_context(nc.sbuf_tensor("xarena", [128, max(n_tc * 2048, 32768)], U8))
        wbuf = es.enter_context(nc.sbuf_tensor("wbuf", [128, 8192], U8))
        woutbuf = es.enter_context(nc.sbuf_tensor("woutbuf", [128, 8192], U8))
        sc_a = es.enter_context(nc.sbuf_tensor("sc_a", [128, 512], F32))
        sc_b = es.enter_context(nc.sbuf_tensor("sc_b", [128, 512], F32))
        sc_c = es.enter_context(nc.sbuf_tensor("sc_c", [128, 512], F32))

        kv2 = es.enter_context(nc.sbuf_tensor("kv2", [128, NPAIR * 128], BF16))
        k16 = es.enter_context(nc.sbuf_tensor("k16", [128, ND * 128], BF16))
        rn_sb = es.enter_context(nc.sbuf_tensor("rn_sb", [128, 512], BF16))
        rb_sb = es.enter_context(nc.sbuf_tensor("rb_sb", [128, 1024], BF16))
        wse = es.enter_context(nc.sbuf_tensor("wse", [128, n_tc * NE], F32))
        gsc = es.enter_context(nc.sbuf_tensor("gsc", [128, 128], F32))
        glog = es.enter_context(nc.sbuf_tensor("glog", [128, n_tc * 10], F32))
        stat = es.enter_context(nc.sbuf_tensor("stat", [128, 32], F32))
        ident32 = es.enter_context(nc.sbuf_tensor("ident32_s", [128, 128], F32))
        e16b = es.enter_context(nc.sbuf_tensor("e16b", [128, NPAIR * 128], BF16))
        bbk = es.enter_context(nc.sbuf_tensor("bbk_s", [128, D], BF16))
        bbv = es.enter_context(nc.sbuf_tensor("bbv_s", [128, D], BF16))
        bbo = es.enter_context(nc.sbuf_tensor("bbo_s", [128, D], BF16))
        bqf = es.enter_context(nc.sbuf_tensor("bqf_s", [128, D], BF16))
        bgeb = es.enter_context(nc.sbuf_tensor("bgeb_s", [128, 10], BF16))
        oscl = es.enter_context(nc.sbuf_tensor("oscl_s", [128, 128], BF16))
        onesr = es.enter_context(nc.sbuf_tensor("onesr_s", [128, 512], BF16))
        maskb = es.enter_context(nc.sbuf_tensor("maskb_s", [128, 2], F32))
        onesb = es.enter_context(nc.sbuf_tensor("onesb", [128, 1], BF16))
        czero = es.enter_context(nc.sbuf_tensor("czero_s", [128, 4], F32))
        gatesb = es.enter_context(nc.sbuf_tensor("gatesb", [128, ND * 10], BF16))

        ha_b = harena.bitcast(BF16)
        ha_f = harena.bitcast(F32)
        xa_b = xarena.bitcast(BF16)
        wb_b = wbuf.bitcast(BF16)
        wo8_b = woutbuf.bitcast(BF16)

        def bv(base, byte_off, cols):
            return base[:, byte_off // 2: byte_off // 2 + cols]

        def fv(base, byte_off, cols):
            return base[:, byte_off // 4: byte_off // 4 + cols]

        R0, R1, R2, R3 = 0, 16384, 32768, 49152
        wk_sb = bv(ha_b, R0, ND * 1024)
        wv_sb = bv(ha_b, R1, ND * 1024)
        wq_sb = bv(ha_b, R3, ND * 1024)
        wo_sb = bv(ha_b, R0, ND * 1024)
        q_fm2 = [bv(ha_b, R1, ND * 512), bv(ha_b, R1 + 8192, ND * 512)]
        apre2 = [bv(ha_b, R2, ND * 512), bv(ha_b, R2 + 8192, ND * 512)]
        q_fm = q_fm2[0]
        apre = apre2[0]
        ktm = [bv(ha_b, R2, 1024), bv(ha_b, R2 + 2048, 1024)]
        vtm = [bv(ha_b, R2 + 4096, 1024), bv(ha_b, R2 + 6144, 1024)]
        xnst = [fv(ha_f, R2 + 8192, 1024), fv(ha_f, R2 + 12288, 1024)]
        swg = ha_b[:, 0:NJC * T]
        h_sb = ha_b[:, 0:NJC * T]
        xn = xa_b[:, 0:n_tc * 1024]
        w2b = [bv(xa_b, 0, 16 * 512), bv(xa_b, 16384, 16 * 512)]
        wob = [bv(wo8_b, 0, 16 * 128), bv(wo8_b, 4096, 16 * 128)]

        PS = [es.enter_context(nc.psum_tensor(f"pb{i}", [128, 512], F32)) for i in range(8)]

        p = Prog()
        sems = {s: es.enter_context(nc.semaphore(f"sem_{s}")) for s in sorted(p.sem_names)}
        blk = es.enter_context(nc.Block())
        bank_free = [None] * 8
        rel = {"R0": None, "R1": None, "R2": None, "R3": None, "acc": None,
               "wbuf": [None, None], "woutbuf": [None, None], "xarena": None}
        late = {}
        glog_tok = [None] * n_tc
        rel["sc_tt"] = {0: None, 1: None}

        def ld(fn):
            return p.dma("g", fn)

        tok_ident = ld(lambda g: g.dma_start(ident32[:, :], ident_h[:, :]))
        p.emit("g", lambda g: g.memset(czero[:, 0:1], 0.0))
        p.emit("g", lambda g: g.memset(czero[:, 1:2], EPS_LN))
        p.emit("g", lambda g: g.memset(czero[:, 2:3], 1.0))
        p.emit("g", lambda g: g.memset(oscl[:, :], 0.0))
        p.emit("g", lambda g: g.memset(onesb[:, :], 1.0))
        p.emit("g", lambda g: g.memset(bbk[:, :], 0.0))
        p.emit("g", lambda g: g.memset(bbv[:, :], 0.0))
        p.emit("g", lambda g: g.memset(rn_sb[:, :], 0.0))
        p.wait_force("g", p.sig("g", sem="gc"))
        p.emit("g", lambda g: g.memset(oscl[0:1, :], 1.0))
        init_tok = p.sig("g", sem="gc")
        p.wait_force("g", init_tok)
        tok_czero = tok_oscl = tok_onesb = init_tok
        tok_bbk = ld(lambda g: g.dma_start(bbk[0:1, :], bk_h[:, :]))
        tok_bbv = ld(lambda g: g.dma_start(bbv[0:1, :], bv_h[:, :]))
        nc.const_aps.register(F32, 0.0, czero[:, 0:1])
        nc.const_aps.register(F32, EPS_LN, czero[:, 1:2])
        nc.const_aps.register(F32, 1.0, czero[:, 2:3])
        tok_maskb = p.dma("y", lambda y: y.dma_start(maskb[:, :], maskb_h[:, :]))

        def emit_late_consts():
            lc = {}
            p.emit("g", lambda g: g.memset(onesr[:, :], 0.0))
            p.emit("g", lambda g: g.memset(bqf[:, :], 0.0))
            p.emit("g", lambda g: g.memset(bbo[:, :], 0.0))
            p.emit("g", lambda g: g.memset(bgeb[:, :], 0.0))
            p.wait_force("g", p.sig("g", sem="gc"))
            p.emit("g", lambda g: g.memset(onesr[0:1, :], 1.0))
            lc["z"] = p.sig("g", sem="gc")
            p.wait_force("g", lc["z"])
            lc["e16"] = ld(lambda g: g.dma_start(
                e16b[:, :].rearrange("p (q m) -> p q m", q=NPAIR),
                e16_h.rearrange("q k m -> k q m")))
            lc["bqf"] = [ld(lambda g: g.dma_start(bqf[0:1, :], bq_h[:, :])), lc["z"]]
            lc["bbo"] = [ld(lambda g: g.dma_start(bbo[0:1, :], bo_h[:, :])), lc["z"]]
            lc["onesr"] = lc["z"]
            lc["bgeb"] = [ld(lambda g: g.dma_start(bgeb[0:1, :], geb_h[:, :])), lc["z"]]
            lc["gatesb"] = ld(lambda g: g.dma_start(
                gatesb[:, :].rearrange("p (c n) -> p c n", c=ND),
                gatesT_h.rearrange("(c k) n -> k c n", k=128)))
            return lc

        def load_w_chunked(dst_view, src_ap, rel_tok):
            p.wait("g", rel_tok)
            return p.dma("g", lambda g: g.dma_start(dst_view, src_ap))

        def stage_lnkv(b, acc_ready, wk_tok, wv_tok):
            ln_done = [None] * n_tc
            tp_tok = [None] * n_tc
            xn_ready = [None] * n_tc
            kbanks = (0, 7)
            vbanks = (1,)
            exp_war = {0: None, 1: None}
            kv_started = [False, False, False]
            kvmm_tok = [None] * n_tc
            ktoks = {}
            vtoks = {}

            def emit_kv(tc):
                sl = tc % 2
                for half in range(2):
                    kvbank = 2 + half
                    p.wait("t", ktoks[tc * 2 + half])
                    p.wait("t", vtoks[tc * 2 + half])
                    if tc == 0:
                        p.wait("t", bank_free[kvbank])
                        p.wait("t", bank_free[4])
                        p.wait("t", tok_onesb)
                    for qq in range(4):
                        st = not kv_started[half]
                        kv_started[half] = True
                        p.emit("t", lambda t, sl=sl, half=half, qq=qq, st=st, kvbank=kvbank: t.matmul(
                            PS[kvbank][:, qq * 128:(qq + 1) * 128],
                            ktm[sl][:, half * 512 + qq * 128: half * 512 + (qq + 1) * 128],
                            vtm[sl][:, half * 512 + qq * 128: half * 512 + (qq + 1) * 128],
                            start=st, stop=False, skip_group_check=True))
                    for blkc in range(4):
                        st = not kv_started[2]
                        kv_started[2] = True
                        col = half * 4 + blkc
                        p.emit("t", lambda t, sl=sl, half=half, blkc=blkc, st=st, col=col: t.matmul(
                            PS[4][:, col:col + 1],
                            ktm[sl][:, half * 512 + blkc * 128: half * 512 + (blkc + 1) * 128],
                            onesb[:, :],
                            start=st, stop=False, skip_group_check=True))
                kvmm_tok[tc] = p.sig("t")

            p1_rstd = [None] * n_tc

            def emit_p1_stats(tc):
                if b > 0 and tc < 4:
                    p.wait("y", rel["acc"])
                acc_ready[tc] = p.dma("y", lambda y, b=b, tc=tc: y.dma_start(
                    acc[:, tc * 1024:(tc + 1) * 1024], x_h[b, tc * 128:(tc + 1) * 128, :]))
                so = (tc % 2) * 16
                xch = acc[:, tc * 1024:(tc + 1) * 1024]
                p.wait("v", acc_ready[tc])
                if tc >= 2:
                    p.wait("v", ln_done[tc - 2])
                p.emit("v", lambda v, xch=xch, so=so: v.bn_stats(
                    stat[:, so:so + 6], xch[:, 0:512]))
                p.emit("v", lambda v, xch=xch, so=so: v.bn_stats(
                    stat[:, so + 6:so + 12], xch[:, 512:1024]))
                p.emit("v", lambda v, so=so: v.bn_aggr(
                    stat[:, so + 12:so + 14], stat[:, so:so + 12]))
                tv = p.sig("v")
                p.wait("s", tv)
                p.wait("s", tok_czero)
                if tc >= 2:
                    p.wait("s", ln_done[tc - 2])
                p.emit("s", lambda s, so=so: s.activation(
                    stat[:, so + 14:so + 15], stat[:, so + 13:so + 14], AF.Ln,
                    bias=czero[:, 1:2], scale=1.0))
                p.emit("s", lambda s, so=so: s.activation(
                    stat[:, so + 15:so + 16], stat[:, so + 14:so + 15], AF.Exp,
                    scale=-0.5))
                p1_rstd[tc] = p.sig("s")

            def emit_p1_norm(tc):
                so = (tc % 2) * 16
                xch = acc[:, tc * 1024:(tc + 1) * 1024]
                p.wait("v", p1_rstd[tc])
                if tc >= 2:
                    p.wait("v", tp_tok[tc - 2])
                sl = tc % 2
                p.emit("v", lambda v, sl=sl, so=so, xch=xch: v.tensor_scalar(
                    out=xnst[sl][:, :], in0=xch,
                    scalar1=stat[:, so + 12:so + 13], scalar2=stat[:, so + 15:so + 16],
                    op0=ALU.subtract, op1=ALU.mult))
                ln_done[tc] = p.sig("v")

            def emit_p1_tp(tc):
                sl = tc % 2
                p.wait("t", ln_done[tc])
                p.wait("t", bank_free[5])
                p.wait("t", bank_free[6])
                p.wait("t", tok_ident)
                for c in range(ND):
                    p.emit("t", lambda t, sl=sl, c=c: t.transpose(
                        PS[5 + c // 4][:, (c % 4) * 128:(c % 4 + 1) * 128],
                        xnst[sl][:, c * 128:(c + 1) * 128], ident32[:, :]))
                tp_tok[tc] = p.sig("t")
                p.wait("s", tp_tok[tc])
                p.emit("s", lambda s, tc=tc: s.copy(
                    out=xn[:, :].rearrange("p (c t) -> p c t", c=ND)[:, 0:4, tc * 128:(tc + 1) * 128],
                    in_=PS[5][:, 0:512].rearrange("p (c t) -> p c t", c=4)))
                bank_free[5] = p.sig("s")
                p.wait("v", tp_tok[tc])
                p.emit("v", lambda v, tc=tc: v.tensor_copy(
                    out=xn[:, :].rearrange("p (c t) -> p c t", c=ND)[:, 4:8, tc * 128:(tc + 1) * 128],
                    in_=PS[6][:, 0:512].rearrange("p (c t) -> p c t", c=4)))
                bank_free[6] = p.sig("v")
                xn_ready[tc] = [bank_free[5], bank_free[6]]

            def emit_p2(tc):
                sl = tc % 2
                for half in range(2):
                    idx = tc * 2 + half
                    kb = kbanks[idx % 2]
                    vb = vbanks[0]
                    p.wait("t", wk_tok)
                    p.wait("t", wv_tok)
                    p.wait("t", [tok_oscl, tok_bbk, tok_bbv])
                    p.wait("t", xn_ready[tc])
                    p.wait("t", bank_free[kb])
                    for c in range(ND):
                        p.emit("t", lambda t, c=c, tc=tc, half=half, kb=kb: t.matmul(
                            PS[kb][:, :],
                            xn[:, c * T + tc * 128: c * T + (tc + 1) * 128],
                            wk_sb[:, c * 1024 + half * 512: c * 1024 + half * 512 + 512],
                            start=(c == 0), stop=False))
                    p.emit("t", lambda t, half=half, kb=kb: t.matmul(
                        PS[kb][:, :], oscl[:, :],
                        bbk[:, half * 512: half * 512 + 512], start=False, stop=True))
                    kmm = p.sig("t")
                    p.wait("t", bank_free[vb])
                    for c in range(ND):
                        p.emit("t", lambda t, c=c, tc=tc, half=half, vb=vb: t.matmul(
                            PS[vb][:, :],
                            xn[:, c * T + tc * 128: c * T + (tc + 1) * 128],
                            wv_sb[:, c * 1024 + half * 512: c * 1024 + half * 512 + 512],
                            start=(c == 0), stop=False))
                    p.emit("t", lambda t, half=half, vb=vb: t.matmul(
                        PS[vb][:, :], oscl[:, :],
                        bbv[:, half * 512: half * 512 + 512], start=False, stop=True))
                    vmm = p.sig("t")

                    eb = sc_a if (half == 0) else sc_c
                    p.wait("s", kmm)
                    p.wait("s", exp_war[half])
                    p.emit("s", lambda s, eb=eb, kb=kb: s.activation(
                        eb[:, :], PS[kb][:, :], AF.Exp))
                    te = p.sig("s")
                    p.wait("s", vmm)
                    if tc >= 2:
                        p.wait("s", kvmm_tok[tc - 2])
                    p.emit("s", lambda s, sl=sl, half=half, vb=vb: s.copy(
                        out=vtm[sl][:, half * 512: half * 512 + 512],
                        in_=PS[vb][:, :]))
                    vtoks[idx] = p.sig("s")
                    bank_free[vb] = vtoks[idx]
                    p.wait("v", te)
                    if tc >= 2:
                        p.wait("v", kvmm_tok[tc - 2])
                    p.emit("v", lambda v, eb=eb: v.tensor_scalar(
                        out=eb[:, :], in0=eb[:, :], scalar1=1.0, scalar2=None,
                        op0=ALU.min))
                    p.wait_force("v", p.sig("v"))
                    p.emit("v", lambda v, sl=sl, half=half, eb=eb, kb=kb: v.scalar_tensor_tensor(
                        out=ktm[sl][:, half * 512: half * 512 + 512],
                        in0=PS[kb][:, :], scalar=0.0, in1=eb[:, :],
                        op0=ALU.max, op1=ALU.add))
                    ktoks[idx] = p.sig("v")
                    exp_war[half] = ktoks[idx]
                    bank_free[kb] = ktoks[idx]

            LA = LNKV_LA if LNKV_LA > 0 else n_tc
            emit_p1_stats(0)
            for i in range(n_tc + LA + 1):
                if i + 1 < n_tc:
                    emit_p1_stats(i + 1)
                if i < n_tc:
                    emit_p1_norm(i)
                if LA <= i < n_tc + LA:
                    emit_p2(i - LA)
                if i >= LA + 1:
                    emit_kv(i - LA - 1)
                if i < n_tc:
                    emit_p1_tp(i)

            last_t = p.tok("t")
            p.emit("g", lambda g: g.memset(kv2[:, :], 0.0))
            p.emit("g", lambda g: g.memset(k16[:, :], 0.0))
            gz = p.sig("g", sem="gc")
            p.wait("v", last_t)
            p.wait("v", gz)
            p.wait("v", tok_maskb)
            for pair in range(NPAIR):
                bank = 2 + pair // 4
                qq = pair % 4
                p.emit("v", lambda v, pair=pair, bank=bank, qq=qq: v.tensor_copy(
                    out=kv2[0:64, pair * 128: pair * 128 + 64],
                    in_=PS[bank][0:64, qq * 128: qq * 128 + 64]))
                p.emit("v", lambda v, pair=pair, bank=bank, qq=qq: v.tensor_copy(
                    out=kv2[64:128, pair * 128 + 64: pair * 128 + 128],
                    in_=PS[bank][64:128, qq * 128 + 64: qq * 128 + 128]))
            for c in range(ND):
                p.emit("v", lambda v, c=c: v.tensor_scalar(
                    out=k16[:, c * 128 + 2 * c: c * 128 + 2 * c + 1],
                    in0=PS[4][:, c:c + 1], scalar1=maskb[:, 0:1], scalar2=None,
                    op0=ALU.mult))
                p.emit("v", lambda v, c=c: v.tensor_scalar(
                    out=k16[:, c * 128 + 2 * c + 1: c * 128 + 2 * c + 2],
                    in0=PS[4][:, c:c + 1], scalar1=maskb[:, 1:2], scalar2=None,
                    op0=ALU.mult))
            ev = p.sig("v")
            for i in (2, 3, 4):
                bank_free[i] = ev
            return ev

        def stage_qattn(b, wq_tok, wo_tok, kv_tok, acc_ready):
            exp_war = {0: None, 1: None}
            qelu = {}
            num_done = [None] * n_tiles
            wo_done = [None] * n_tiles
            rb_done = [None] * n_tiles
            rntoks = [None] * n_tiles
            apre_sl_war = {0: None, 1: None}

            def emit_qproj(ti, dcs):
                qf = q_fm2[ti % 2]
                for dc in dcs:
                    qb = dc % 2
                    p.wait("t", wq_tok)
                    p.wait("t", [late["bqf"], late["onesr"]])
                    p.wait("t", bank_free[qb])
                    for c in range(ND):
                        p.emit("t", lambda t, c=c, dc=dc, ti=ti, qb=qb: t.matmul(
                            PS[qb][:, :],
                            wq_sb[:, c * 1024 + dc * 128: c * 1024 + (dc + 1) * 128],
                            xn[:, c * T + ti * 512: c * T + (ti + 1) * 512],
                            start=(c == 0), stop=False))
                    p.emit("t", lambda t, dc=dc, qb=qb: t.matmul(
                        PS[qb][:, :], bqf[:, dc * 128:(dc + 1) * 128],
                        onesr[:, :], start=False, stop=True))
                    qmm = p.sig("t")
                    eb = sc_a if (dc % 2 == 0) else sc_c
                    p.wait("s", qmm)
                    p.wait("s", exp_war[dc % 2])
                    p.emit("s", lambda s, qb=qb, eb=eb: s.activation(
                        eb[:, :], PS[qb][:, :], AF.Exp))
                    te = p.sig("s")
                    p.wait("v", te)
                    if ti >= 2:
                        p.wait("v", num_done[ti - 2])
                    p.emit("v", lambda v, eb=eb: v.tensor_scalar(
                        out=eb[:, :], in0=eb[:, :], scalar1=1.0, scalar2=None,
                        op0=ALU.min))
                    p.wait_force("v", p.sig("v"))
                    p.emit("v", lambda v, dc=dc, qb=qb, eb=eb, qf=qf: v.scalar_tensor_tensor(
                        out=qf[:, dc * 512:(dc + 1) * 512],
                        in0=PS[qb][:, :], scalar=0.0, in1=eb[:, :],
                        op0=ALU.max, op1=ALU.add))
                    qelu[(ti, dc)] = p.sig("v")
                    exp_war[dc % 2] = qelu[(ti, dc)]
                    bank_free[qb] = qelu[(ti, dc)]

            def emit_nmm(ti):
                qf = q_fm2[ti % 2]
                p.wait("t", kv_tok)
                p.wait("t", bank_free[4])
                for pair in range(NPAIR):
                    p.wait("t", qelu[(ti, pair)])
                    p.emit("t", lambda t, pair=pair, qf=qf: t.matmul(
                        PS[4][:, :], k16[:, pair * 128:(pair + 1) * 128],
                        qf[:, pair * 512:(pair + 1) * 512],
                        start=(pair == 0), stop=(pair == NPAIR - 1)))
                nmm = p.sig("t")
                p.wait("v", nmm)
                p.wait("v", init_tok)
                if ti >= 1:
                    p.wait("v", rb_done[ti - 1])
                p.emit("v", lambda v: v.tensor_scalar(
                    out=sc_b[0:16, :], in0=PS[4][0:16, :], scalar1=EPS_ATTN,
                    scalar2=None, op0=ALU.max))
                p.wait_force("v", p.sig("v"))
                p.emit("v", lambda v: v.reciprocal(out=sc_b[0:16, :], in_=sc_b[0:16, :]))
                p.wait_force("v", p.sig("v"))
                p.emit("v", lambda v: v.tensor_copy(out=rn_sb[0:16, :], in_=sc_b[0:16, :]))
                rntoks[ti] = p.sig("v")
                bank_free[4] = rntoks[ti]

            apre_toks = {}

            def emit_tail_a(ti):
                qf = q_fm2[ti % 2]
                ap = apre2[ti % 2]
                apre_tok = apre_toks[ti] = [None] * NPAIR
                for pair in range(NPAIR):
                    nb_ = (2, 3, 4)[pair % 3]
                    rbb = 5 + pair % 2
                    p.wait("t", bank_free[nb_])
                    p.emit("t", lambda t, pair=pair, nb_=nb_, qf=qf: t.matmul(
                        PS[nb_][:, :], kv2[:, pair * 128:(pair + 1) * 128],
                        qf[:, pair * 512:(pair + 1) * 512], start=True, stop=True))
                    num_t = p.sig("t")
                    if pair == NPAIR - 1:
                        num_done[ti] = num_t
                    p.wait("t", rntoks[ti])
                    p.wait("t", late["e16"])
                    p.wait("t", bank_free[rbb])
                    p.emit("t", lambda t, pair=pair, rbb=rbb: t.matmul(
                        PS[rbb][:, :], e16b[:, pair * 128:(pair + 1) * 128],
                        rn_sb[:, :], start=True, stop=True))
                    rb_t = p.sig("t")
                    rb_done[ti] = rb_t
                    sl = pair % 2
                    p.wait("s", rb_t)
                    p.wait("s", apre_sl_war[sl])
                    p.emit("s", lambda s, sl=sl, rbb=rbb: s.copy(
                        out=rb_sb[:, sl * 512:(sl + 1) * 512], in_=PS[rbb][:, :]))
                    bank_free[rbb] = p.sig("s")
                    p.wait("v", bank_free[rbb])
                    p.wait("v", num_t)
                    if ti >= 2:
                        p.wait("v", wo_done[ti - 2])
                    p.emit("v", lambda v, pair=pair, sl=sl, nb_=nb_, ap=ap: v.tensor_tensor(
                        out=ap[:, pair * 512:(pair + 1) * 512], in0=PS[nb_][:, :],
                        in1=rb_sb[:, sl * 512:(sl + 1) * 512], op=ALU.mult))
                    apre_tok[pair] = p.sig("v")
                    apre_sl_war[sl] = apre_tok[pair]
                    bank_free[nb_] = apre_tok[pair]

            def emit_tail_b(ti):
                ap = apre2[ti % 2]
                apre_tok = apre_toks[ti]
                for sl4 in range(4):
                    for dh in range(2):
                        wb_ = 7 if (sl4 * 2 + dh) % 2 == 0 else 4
                        p.wait("t", wo_tok)
                        p.wait("t", [late["bbo"], tok_oscl])
                        p.wait("t", bank_free[wb_])
                        for c in range(ND):
                            p.wait("t", apre_tok[c])
                            p.emit("t", lambda t, c=c, sl4=sl4, dh=dh, wb_=wb_, ap=ap: t.matmul(
                                PS[wb_][:, :],
                                ap[:, c * 512 + sl4 * 128: c * 512 + (sl4 + 1) * 128],
                                wo_sb[:, c * 1024 + dh * 512: c * 1024 + dh * 512 + 512],
                                start=(c == 0), stop=False))
                        p.emit("t", lambda t, dh=dh, wb_=wb_: t.matmul(
                            PS[wb_][:, :], oscl[:, :],
                            bbo[:, dh * 512: dh * 512 + 512], start=False, stop=True))
                        womm = p.sig("t")
                        tcg = ti * 4 + sl4
                        p.wait("v", womm)
                        p.wait("v", acc_ready[tcg])
                        p.emit("v", lambda v, tcg=tcg, dh=dh, wb_=wb_: v.tensor_tensor(
                            out=acc[:, tcg * 1024 + dh * 512: tcg * 1024 + dh * 512 + 512],
                            in0=PS[wb_][:, :],
                            in1=acc[:, tcg * 1024 + dh * 512: tcg * 1024 + dh * 512 + 512],
                            op=ALU.add))
                        bank_free[wb_] = p.sig("v")
                wo_done[ti] = p.tok("t")
                for k4 in range(4):
                    gates_mm(ti * 4 + k4)

            emit_qproj(0, range(ND))
            for ti in range(n_tiles):
                emit_nmm(ti)
                if ti + 1 < n_tiles:
                    emit_qproj(ti + 1, range(0, 5))
                emit_tail_a(ti)
                if ti + 1 < n_tiles:
                    emit_qproj(ti + 1, range(5, 8))
                emit_tail_b(ti)
            return p.tok("v")

        def gates_mm(tc):
            gb_ = 5 + tc % 2
            p.wait("t", [late["gatesb"], late["bgeb"], tok_oscl])
            p.wait("t", bank_free[gb_])
            for c in range(ND):
                p.emit("t", lambda t, c=c, tc=tc, gb_=gb_: t.matmul(
                    PS[gb_][:, 0:10],
                    xn[:, c * T + tc * 128: c * T + (tc + 1) * 128],
                    gatesb[:, c * 10:(c + 1) * 10],
                    start=(c == 0), stop=False))
            p.emit("t", lambda t, gb_=gb_: t.matmul(
                PS[gb_][:, 0:10], oscl[:, :], bgeb[:, :], start=False, stop=True))
            gmm = p.sig("t")
            p.wait("s", gmm)
            p.emit("s", lambda s, tc=tc, gb_=gb_: s.copy(
                out=glog[:, tc * 10:(tc + 1) * 10], in_=PS[gb_][:, 0:10]))
            glog_tok[tc] = bank_free[gb_] = p.sig("s")

        def gates_steps():
            for pr in range(n_tc // 2):
                yield from gates_pair_steps(2 * pr, 2 * pr + 1)

        def gates_pair_steps(tc_a, tc_b):
            Gs = [gsc[:, 0:64], gsc[:, 64:128]]

            def phase_a(G, tc):
                return [
                    lambda v: v.tensor_copy(out=G[:, 0:10], in_=glog[:, tc * 10:(tc + 1) * 10]),
                    lambda v: v.tensor_tensor(out=G[:, 10:11], in0=G[:, 0:1],
                                              in1=G[:, 1:2], op=ALU.subtract),
                    lambda v: v.tensor_scalar(out=G[:, 11:12], in0=G[:, 10:11],
                                              scalar1=0.0, scalar2=None, op0=ALU.is_lt),
                    lambda v: v.tensor_scalar(out=G[:, 12:13], in0=G[:, 11:12],
                                              scalar1=BIG, scalar2=None, op0=ALU.mult),
                    lambda v: v.tensor_scalar(out=G[:, 13:14], in0=G[:, 11:12],
                                              scalar1=-BIG, scalar2=BIG, op0=ALU.mult,
                                              op1=ALU.add),
                    lambda v: v.tensor_scalar(out=G[:, 24:28], in0=G[:, 2:6],
                                              scalar1=G[:, 12:13], scalar2=None,
                                              op0=ALU.subtract),
                    lambda v: v.tensor_scalar(out=G[:, 28:32], in0=G[:, 6:10],
                                              scalar1=G[:, 13:14], scalar2=None,
                                              op0=ALU.subtract),
                    lambda v: v.tensor_reduce(out=G[:, 17:18], in_=G[:, 24:32],
                                              axis=AX.X, op=ALU.max, negate=True),
                    lambda v: v.tensor_scalar(out=G[:, 14:15], in0=G[:, 10:11],
                                              scalar1=-1.0, scalar2=G[:, 10:11],
                                              op0=ALU.mult, op1=ALU.max),
                ]

            def phase_b(G, tc):
                return [
                    lambda v: v.tensor_reduce(out=G[:, 18:19], in_=G[:, 32:40],
                                              axis=AX.X, op=ALU.add),
                    lambda v: v.tensor_scalar(out=G[:, 16:17], in0=G[:, 15:16],
                                              scalar1=1.0, scalar2=None, op0=ALU.add),
                    lambda v: v.reciprocal(out=G[:, 19:20], in_=G[:, 18:19]),
                    lambda v: v.reciprocal(out=G[:, 16:17], in_=G[:, 16:17]),
                    lambda v: v.tensor_scalar(out=G[:, 40:48], in0=G[:, 32:40],
                                              scalar1=G[:, 19:20], scalar2=G[:, 16:17],
                                              op0=ALU.mult, op1=ALU.mult),
                    lambda v: v.tensor_reduce(out=G[:, 20:21], in_=G[:, 40:48],
                                              axis=AX.X, op=ALU.max),
                    lambda v: v.tensor_scalar(out=G[:, 48:56], in0=G[:, 40:48],
                                              scalar1=G[:, 20:21], scalar2=None,
                                              op0=ALU.is_ge),
                    lambda v: v.tensor_scalar(out=G[:, 48:56], in0=G[:, 48:56],
                                              scalar1=-1.0, scalar2=1.0, op0=ALU.mult,
                                              op1=ALU.add),
                    lambda v: v.tensor_tensor(out=G[:, 56:64], in0=G[:, 48:56],
                                              in1=G[:, 40:48], op=ALU.mult),
                    lambda v: v.tensor_reduce(out=G[:, 21:22], in_=G[:, 56:64],
                                              axis=AX.X, op=ALU.max),
                    lambda v: v.scalar_tensor_tensor(
                        out=wse[:, tc * NE:(tc + 1) * NE], in0=G[:, 40:48],
                        scalar=G[:, 21:22], in1=G[:, 40:48], op0=ALU.is_ge,
                        op1=ALU.mult),
                ]

            chains_a = [phase_a(Gs[0], tc_a), phase_a(Gs[1], tc_b)]
            first = [True]

            def astep(op):
                if first[0]:
                    p.wait("v", glog_tok[tc_a])
                    p.wait("v", glog_tok[tc_b])
                    first[0] = False
                p.emit("v", op)
                p.sig("v")

            for i in range(len(chains_a[0])):
                for idx in range(2):
                    yield lambda op=chains_a[idx][i]: astep(op)

            se_box = [None]

            def sstep():
                p.wait("s", p.tok("v"))
                for G in Gs:
                    p.emit("s", lambda s, G=G: s.activation(
                        G[:, 32:40], G[:, 24:32], AF.Exp, bias=G[:, 17:18], scale=1.0))
                    p.emit("s", lambda s, G=G: s.activation(
                        G[:, 15:16], G[:, 14:15], AF.Exp, scale=-1.0))
                se_box[0] = p.sig("s")

            yield sstep

            chains_b = [phase_b(Gs[0], tc_a), phase_b(Gs[1], tc_b)]
            first_b = [True]

            def bstep(op):
                if first_b[0]:
                    p.wait("v", se_box[0])
                    first_b[0] = False
                p.emit("v", op)
                p.sig("v")

            for i in range(len(chains_b[0])):
                for idx in range(2):
                    yield lambda op=chains_b[idx][i]: bstep(op)

        def swiglu_load(srcR_h, jc):
            sa = (jc % 2) * 2
            p.wait("g", rel["wbuf"][jc % 2])
            la = p.dma("g", lambda g, sa=sa, srcR_h=srcR_h, jc=jc: g.dma_start(
                wb_b[:, sa * 1024:(sa + 2) * 1024], srcR_h[jc]))
            return la, la

        def swiglu_prefetch(srcT_h):
            return {jc: swiglu_load(srcT_h, jc) for jc in (0, 1)}

        def swiglu_stage(b, srcT_h, dst_sb, pre=None, cb=None, ti_toks=None):
            sc_tt_tok = dict(rel["sc_tt"])
            for jc in range(NJC):
                sa = (jc % 2) * 2
                a_v = wb_b[:, sa * 1024:(sa + 1) * 1024]
                b_v = wb_b[:, (sa + 1) * 1024:(sa + 2) * 1024]
                if pre is not None and jc in pre:
                    la, lb = pre[jc]
                else:
                    la, lb = swiglu_load(srcT_h, jc)

                for ti in range(n_tiles):
                    gidx = ti // 2
                    par = ti % 2
                    ab = (gidx % 2) * 4 + par
                    bb = (gidx % 2) * 4 + 2 + par
                    p.wait("t", la)
                    p.wait("t", lb)
                    p.wait("t", bank_free[ab])
                    for c in range(ND):
                        p.emit("t", lambda t, c=c, ti=ti, a_v=a_v, ab=ab: t.matmul(
                            PS[ab][:, :], a_v[:, c * 128:(c + 1) * 128],
                            xn[:, c * T + ti * 512: c * T + (ti + 1) * 512],
                            start=(c == 0), stop=(c == ND - 1)))
                    amm = p.sig("t")
                    p.wait("t", bank_free[bb])
                    for c in range(ND):
                        p.emit("t", lambda t, c=c, ti=ti, b_v=b_v, bb=bb: t.matmul(
                            PS[bb][:, :], b_v[:, c * 128:(c + 1) * 128],
                            xn[:, c * T + ti * 512: c * T + (ti + 1) * 512],
                            start=(c == 0), stop=(c == ND - 1)))
                    bmm = p.sig("t")
                    sc2 = sc_a if ti % 2 == 0 else sc_b
                    p.wait("s", amm)
                    p.wait("s", sc_tt_tok[ti % 2])
                    p.emit("s", lambda s, ab=ab, sc2=sc2: s.activation(
                        sc2[:, :], PS[ab][:, :], AF.Silu))
                    st_ = p.sig("s")
                    bank_free[ab] = st_
                    p.wait("v", st_)
                    p.wait("v", bmm)
                    p.emit("v", lambda v, jc=jc, ti=ti, bb=bb, sc2=sc2, dst_sb=dst_sb: v.tensor_tensor(
                        out=dst_sb[:, jc * T + ti * 512: jc * T + (ti + 1) * 512],
                        in0=sc2[:, :], in1=PS[bb][:, :], op=ALU.mult))
                    sc_tt_tok[ti % 2] = p.sig("v")
                    bank_free[bb] = sc_tt_tok[ti % 2]
                    if ti_toks is not None and jc == NJC - 1:
                        ti_toks[ti] = sc_tt_tok[ti % 2]
                    if cb is not None:
                        cb(jc, ti)
                rel["wbuf"][jc % 2] = p.tok("t")
            rel["sc_tt"] = dict(sc_tt_tok)
            return p.tok("v")

        def stage_wout(b, swg_tok, swg_ti=None, after_first_loads=None):
            load_tok = [None] * 8
            comp_tok = [None] * 8

            def emit_load(q8):
                sl = q8 % 2
                p.wait("g", comp_tok[q8 - 2] if q8 >= 2 else rel["woutbuf"][sl])
                load_tok[q8] = p.dma("g", lambda g, q8=q8, sl=sl: g.dma_start(
                    wob[sl][:, :], woutR_h[q8]))

            emit_load(0)
            emit_load(1)
            if after_first_loads is not None:
                after_first_loads()
            for q8 in range(8):
                sl = q8 % 2
                for tc in range(n_tc):
                    wb_ = tc % 2
                    p.wait("t", load_tok[q8])
                    p.wait("t", swg_ti[tc // 4] if swg_ti is not None else swg_tok)
                    p.wait("t", bank_free[wb_])
                    for jc in range(NJC):
                        p.emit("t", lambda t, jc=jc, tc=tc, sl=sl, wb_=wb_: t.matmul(
                            PS[wb_][:, 0:128],
                            swg[:, jc * T + tc * 128: jc * T + (tc + 1) * 128],
                            wob[sl][:, jc * 128:(jc + 1) * 128],
                            start=(jc == 0), stop=(jc == NJC - 1)))
                    wmm = p.sig("t")
                    p.wait("v", wmm)
                    p.emit("v", lambda v, tc=tc, q8=q8, wb_=wb_: v.tensor_tensor(
                        out=acc[:, tc * 1024 + q8 * 128: tc * 1024 + (q8 + 1) * 128],
                        in0=PS[wb_][:, 0:128],
                        in1=acc[:, tc * 1024 + q8 * 128: tc * 1024 + (q8 + 1) * 128],
                        op=ALU.add))
                    bank_free[wb_] = p.sig("v")
                comp_tok[q8] = p.tok("t")
                rel["woutbuf"][sl] = comp_tok[q8]
                if q8 + 2 < 8:
                    emit_load(q8 + 2)
            return p.tok("v")

        def stage_experts(b, h_tok, wse_tok, xn_rel_tok):
            order = [(e, dh) for e in range(NE) for dh in range(2)]
            load_tok = {}
            comp_tok = {}
            out_toks = []

            def emit_load(i):
                e, dh = order[i]
                buf = w2b[i % 2]
                if i == 0:
                    p.wait("g", rel["woutbuf"])
                    tA = p.dma("g", lambda g: g.dma_start(
                        wo8_b[:, 0:4096], w2R_h[0, 0][:, 0:4096]))
                    p.wait("g", xn_rel_tok)
                    tB = p.dma("g", lambda g, buf=buf: g.dma_start(
                        buf[:, 4096:8192], w2R_h[0, 0][:, 4096:8192]))
                    load_tok[0] = [tA, tB]
                    return
                p.wait("g", comp_tok[i - 2] if i >= 2 else xn_rel_tok)
                load_tok[i] = p.dma("g", lambda g, e=e, dh=dh, buf=buf: g.dma_start(
                    buf[:, :], w2R_h[e, dh]))

            emit_load(0)
            emit_load(1)
            for i, (e, dh) in enumerate(order):
                buf = w2b[i % 2]
                for tc in range(n_tc):
                    wb_ = tc % 2
                    p.wait("t", load_tok[i])
                    p.wait("t", h_tok)
                    p.wait("t", bank_free[wb_])
                    for jc in range(NJC):
                        rv = wo8_b if (i == 0 and jc < 8) else buf
                        p.emit("t", lambda t, jc=jc, tc=tc, rv=rv, wb_=wb_: t.matmul(
                            PS[wb_][:, :],
                            h_sb[:, jc * T + tc * 128: jc * T + (tc + 1) * 128],
                            rv[:, jc * 512:(jc + 1) * 512],
                            start=(jc == 0), stop=(jc == NJC - 1)))
                    emm = p.sig("t")
                    p.wait("v", emm)
                    p.wait("v", wse_tok)
                    p.emit("v", lambda v, tc=tc, dh=dh, e=e, wb_=wb_: v.scalar_tensor_tensor(
                        out=acc[:, tc * 1024 + dh * 512: tc * 1024 + dh * 512 + 512],
                        in0=PS[wb_][:, :], scalar=wse[:, tc * NE + e: tc * NE + e + 1],
                        in1=acc[:, tc * 1024 + dh * 512: tc * 1024 + dh * 512 + 512],
                        op0=ALU.mult, op1=ALU.add))
                    stok = p.sig("v")
                    bank_free[wb_] = stok
                    if e == NE - 1 and dh == 1:
                        p.wait("y", stok)
                        out_toks.append(p.dma("y", lambda y, b=b, tc=tc: y.dma_start(
                            out_h[b, tc * 128:(tc + 1) * 128, :],
                            acc[:, tc * 1024:(tc + 1) * 1024])))
                comp_tok[i] = p.tok("t")
                if i + 2 < len(order):
                    emit_load(i + 2)
            return out_toks

        def emit_tap(name):
            if name in tap_out:
                ncols = tap_out[name].shape[-1]
                src = {"xn": xn, "kv2": kv2, "k16": k16, "acc": acc, "wse": wse,
                       "swg": swg, "h": h_sb, "qfm": q_fm, "apre": apre,
                       "rn": rn_sb, "gsc": gsc}[name]
                p.barrier(("t", "v", "s"))
                tv = p.tok("v")
                p.wait("g", tv)
                gt = p.dma("g", lambda g, src=src, name=name, ncols=ncols: g.dma_start(
                    tap_out[name][:, :], src[:, 0:ncols]))
                p.wait("v", gt)
                p.emit("v", lambda v: v.nop())

        for b in range(NB):
            acc_ready = [None] * n_tc
            wk_tok = load_w_chunked(wk_sb, wk_h, rel["R0"])
            wv_tok = load_w_chunked(wv_sb, wv_h, rel["R1"])
            if b == 0:
                late.update(emit_late_consts())
            wq_tok = load_w_chunked(wq_sb, wq_h, rel["R3"])

            kv_tok = stage_lnkv(b, acc_ready, wk_tok, wv_tok)
            p.barrier()
            if b == 0:
                emit_tap("xn")
            rel["R0"] = p.tok("t")
            if b == 0:
                emit_tap("kv2")
                emit_tap("k16")

            wo_tok = load_w_chunked(wo_sb, wo_h, rel["R0"])
            pre_in = swiglu_prefetch(winR_h)
            stage_qattn(b, wq_tok, wo_tok, kv_tok, acc_ready)
            p.barrier()
            rel["R2"] = p.tok("t")
            rel["R3"] = p.tok("t")
            if b == 0:
                emit_tap("acc")
                emit_tap("qfm")
                emit_tap("apre")

            gates_iter = gates_steps()

            def gdrain(jc, ti, n=6):
                if jc < 1:
                    return
                for _ in range(n):
                    st = next(gates_iter, None)
                    if st is None:
                        return
                    st()

            swg_ti = {}
            swg_tok = swiglu_stage(b, winR_h, swg, pre_in, cb=gdrain, ti_toks=swg_ti)
            for st in gates_iter:
                st()
            gate_tok = p.tok("v")
            if b == 0:
                emit_tap("swg")
                emit_tap("wse")
                emit_tap("gsc")
            pre_h = {}
            stage_wout(b, swg_tok, swg_ti,
                       after_first_loads=lambda: pre_h.update(swiglu_prefetch(w1R_h)))
            p.barrier()

            h_tok = swiglu_stage(b, w1R_h, h_sb, pre_h)
            p.barrier()
            xn_rel = p.tok("t")
            if b == 0:
                emit_tap("h")

            out_toks = stage_experts(b, h_tok, gate_tok, xn_rel)
            p.barrier(("t", "v", "s", "y"))
            rel["acc"] = p.queue_done("y")
            rel["R0"] = rel["R1"] = rel["R2"] = rel["R3"] = p.tok("t")
            rel["xarena"] = p.tok("t")
            rel["wbuf"] = [p.tok("t"), p.tok("t")]
            rel["woutbuf"] = [p.tok("t"), p.tok("t")]

        p.wait_force("y", p.queue_done("y"))
        p.emit("y", lambda y: y.nop())
        p.wait_force("g", p.queue_done("g"))
        p.emit("g", lambda g: g.nop())

        p.check_deadlock()
        n_ins = {e: len(p.items[e]) for e in ENGINES}

        def runner(name):
            def run(eng):
                for waits, fn, incs in p.items[name]:
                    for s_, v_ in waits.items():
                        eng.wait_ge(sems[s_], v_)
                    ins = fn(eng)
                    for sem_n, amt in incs:
                        ins.then_inc(sems[sem_n], amt)
            return run

        blk.tensor(runner("t"))
        blk.vector(runner("v"))
        blk.scalar(runner("s"))
        blk.gpsimd(runner("g"))
        blk.sync(runner("y"))

    return nc, n_ins


def prep_inputs(inputs, n_tc=16, n_cores=8):
    T = n_tc * 128
    f32c = lambda a: np.ascontiguousarray(np.asarray(a), dtype=np.float32)
    x = f32c(inputs["L0_attn_inputs_q"])[:, :T, :]

    def projR(w):
        return f32c(np.asarray(w, dtype=np.float32)
                    .reshape(ND, 128, D).transpose(1, 0, 2).reshape(128, ND * D))

    def swigluR(w):
        wT = np.asarray(w, dtype=np.float32).T
        B = wT.reshape(ND, 128, 2 * HID).transpose(1, 0, 2)
        out = np.empty((NJC, 128, 2048), dtype=np.float32)
        for jc in range(NJC):
            a = B[:, :, jc * 128:(jc + 1) * 128].reshape(128, 1024)
            bb = B[:, :, HID + jc * 128: HID + (jc + 1) * 128].reshape(128, 1024)
            out[jc] = np.concatenate([a, bb], axis=1)
        return f32c(out)

    winR = swigluR(inputs["L0_ffn_w_in"])
    w1R = swigluR(inputs["L0_ffn_w1"])
    woutT = np.asarray(inputs["L0_ffn_w_out"], dtype=np.float32).T
    C = woutT.reshape(NJC, 128, D).transpose(1, 0, 2)
    woutR = f32c(np.stack([C[:, :, q8 * 128:(q8 + 1) * 128].reshape(128, 2048)
                           for q8 in range(8)]))
    w2T = np.asarray(inputs["L0_ffn_w2"], dtype=np.float32).transpose(0, 2, 1)
    A = w2T.reshape(NE, NJC, 128, D).transpose(0, 2, 1, 3)
    w2R = f32c(np.stack([A[:, :, :, 0:512].reshape(NE, 128, NJC * 512),
                         A[:, :, :, 512:1024].reshape(NE, 128, NJC * 512)], axis=1))
    gatesT = f32c(np.concatenate([np.asarray(inputs["L0_ffn_wg"]).T,
                                  np.asarray(inputs["L0_ffn_we"]).T], axis=1))
    bk = f32c(np.asarray(inputs["L0_attn_bk"]).reshape(1, D))
    bv = f32c(np.asarray(inputs["L0_attn_bv"]).reshape(1, D))
    bo = f32c(np.asarray(inputs["L0_attn_bo"]).reshape(1, D))
    bq = f32c(np.asarray(inputs["L0_attn_bq"]).reshape(1, D))
    geb = f32c(np.concatenate([np.asarray(inputs["L0_ffn_gb"]),
                               np.asarray(inputs["L0_ffn_eb"])]).reshape(1, 10))
    ident = np.eye(128, dtype=np.float32)
    e16 = np.zeros((NPAIR, 128, 128), dtype=np.float32)
    for pr in range(NPAIR):
        e16[pr, 2 * pr, 0:64] = 1.0
        e16[pr, 2 * pr + 1, 64:128] = 1.0
    maskb = np.zeros((128, 2), dtype=np.float32)
    maskb[0:64, 0] = 1.0
    maskb[64:128, 1] = 1.0

    shared = dict(wq=projR(inputs["L0_attn_Wq"]), wk=projR(inputs["L0_attn_Wk"]),
                  wv=projR(inputs["L0_attn_Wv"]), wo=projR(inputs["L0_attn_Wo"]),
                  winR=winR, w1R=w1R, woutR=woutR, w2R=w2R, gatesT=gatesT,
                  bk=bk, bv=bv, bo=bo, bq=bq, geb=geb,
                  ident=ident, e16=e16, maskb=maskb)
    in_maps = []
    for c in range(n_cores):
        m = dict(shared)
        m["x"] = np.ascontiguousarray(x[c * NB:(c + 1) * NB])
        in_maps.append(m)
    return in_maps


def kernel(**inputs):
    n_tc = 16
    in_maps = prep_inputs(inputs, n_tc=n_tc, n_cores=8)
    nc, _ = build(n_tc=n_tc)
    res = run_bass_kernel_spmd(nc, in_maps, core_ids=list(range(8)))
    outs = [np.asarray(r["out"], dtype=np.float32) for r in res.results]
    return np.concatenate(outs, axis=0)
